# version 01
# baseline (speedup 1.0000x reference)
import numpy as np
import ml_dtypes
from contextlib import ExitStack
import concourse.bass as bass
import concourse.mybir as mybir
from concourse.bass import IndirectOffsetOnAxis
from concourse.bass_utils import run_bass_kernel_spmd

F32 = mybir.dt.float32
BF16 = mybir.dt.bfloat16
I32 = mybir.dt.int32
AF = mybir.ActivationFunctionType
ALU = mybir.AluOpType
AX = mybir.AxisListType
bfnp = ml_dtypes.bfloat16

S, D, NT, NST = 8192, 1024, 64, 16
ALVL = 99
H, HD, NE, DH = 16, 64, 8, 4096
EPS = 1e-5


class P_:
    pass


def build_A():
    nc = bass.Bass("TRN2", target_bir_lowering=False)
    P = P_()
    P.nc = nc
    P.x = nc.dram_tensor("x", [S, D], F32, kind="ExternalInput").ap()
    P.wqt = nc.dram_tensor("wqt", [D, D], F32, kind="ExternalInput").ap()
    P.wkt = nc.dram_tensor("wkt", [D, D], F32, kind="ExternalInput").ap()
    P.wvt = nc.dram_tensor("wvt", [D, D], F32, kind="ExternalInput").ap()
    P.wot = nc.dram_tensor("wot", [D, D], F32, kind="ExternalInput").ap()
    P.bk_d = nc.dram_tensor("bk", [1, D], F32, kind="ExternalInput").ap()
    P.bv_d = nc.dram_tensor("bv", [1, D], F32, kind="ExternalInput").ap()
    P.bo_d = nc.dram_tensor("bo", [1, D], F32, kind="ExternalInput").ap()
    P.bqc_d = nc.dram_tensor("bqc", [128, 8], F32, kind="ExternalInput").ap()
    P.wg_d = nc.dram_tensor("wg", [D, NE], F32, kind="ExternalInput").ap()
    P.o_x2 = nc.dram_tensor("o_x2", [S, D], F32, kind="ExternalOutput").ap()
    P.o_ln2 = nc.dram_tensor("o_ln2", [S, D], BF16, kind="ExternalOutput").ap()
    P.o_cnt = nc.dram_tensor("o_cnt", [1, NE], F32, kind="ExternalOutput").ap()
    P.o_rank = nc.dram_tensor("o_rank", [128, NT], F32, kind="ExternalOutput").ap()
    P.o_eidx = nc.dram_tensor("o_eidx", [128, NT], F32, kind="ExternalOutput").ap()
    P.o_gv = nc.dram_tensor("o_gv", [128, NT], F32, kind="ExternalOutput").ap()
    P.ln1st = nc.dram_tensor("ln1stash", [S, D], BF16).ap()
    P.z_dram = nc.dram_tensor("z_dram", [H, 512], F32).ap()

    es = ExitStack()
    P.es = es
    sb = lambda name, shape, dt: es.enter_context(nc.sbuf_tensor(name, shape, dt))
    ps = lambda name, shape, dt: es.enter_context(nc.psum_tensor(name, shape, dt))
    P.wk_sb = sb("wk_sb", [128, 8 * D], BF16)
    P.wv_sb = sb("wv_sb", [128, 8 * D], BF16)
    P.wq_sb = sb("wq_sb", [128, 8 * D], BF16)
    P.wo_sb = sb("wo_sb", [128, 8 * D], BF16)
    P.xa = [sb(f"xa{b}", [128, D], F32) for b in range(2)]
    P.ln1tm = [sb(f"ln1tm{b}", [128, D], BF16) for b in range(2)]
    P.ln1f = [sb(f"ln1f{b}", [128, D], BF16) for b in range(2)]
    P.kt = [sb(f"kt{b}", [128, D], BF16) for b in range(2)]
    P.vt = [sb(f"vt{b}", [128, D], BF16) for b in range(2)]
    P.kvt = sb("kvt", [128, 8 * HD], BF16)
    P.kvacc = [sb(f"kvacc{b}", [128, 512], F32) for b in range(2)]
    P.ksacc = [sb(f"ksacc{b}", [33, 512], F32) for b in range(2)]
    P.ksc = sb("ksc", [128, 8], F32)
    P.KS = sb("KS", [128, 8 * H], BF16)
    P.st6 = [sb(f"st6_{b}", [128, 12], F32) for b in range(2)]
    P.st2 = [sb(f"st2_{b}", [128, 2], F32) for b in range(2)]
    P.vep = [sb(f"vep{b}", [128, 1], F32) for b in range(2)]
    P.lnv = [sb(f"lnv{b}", [128, 1], F32) for b in range(2)]
    P.rstd = [sb(f"rstd{b}", [128, 1], F32) for b in range(2)]
    P.ucl = [sb(f"ucl{b}", [128, 512], F32) for b in range(2)]
    P.wcl = [sb(f"wcl{b}", [128, 512], BF16) for b in range(2)]
    P.tcl = [sb(f"tcl{b}", [128, 512], BF16) for b in range(2)]
    P.bkr = sb("bkr", [1, D], BF16)
    P.bvr = sb("bvr", [1, D], BF16)
    P.bor = sb("bor", [1, D], BF16)
    P.bqcs = sb("bqcs", [128, 8], F32)
    P.nbqc = sb("nbqc", [128, 8], F32)
    P.wg_sb = sb("wg_sb", [128, 8 * NE], BF16)
    P.ones1 = sb("ones1", [1, 128], BF16)
    P.ones1f = sb("ones1f", [1, 128], F32)
    P.onesc = sb("onesc", [128, 1], BF16)
    P.iden = sb("iden", [128, 128], BF16)
    P.idenf = sb("idenf", [128, 128], F32)
    P.SU = sb("SU", [128, 128], BF16)
    P.su_i32 = sb("su_i32", [128, 128], I32)
    P.iotaP = sb("iotaP", [128, 1], I32)
    P.iota8 = sb("iota8", [128, 8], I32)
    P.iota16 = sb("iota16", [128, 16], I32)
    P.iota8f = sb("iota8f", [128, 8], F32)
    P.iota8B = sb("iota8B", [128, 8], F32)
    P.iota16f = sb("iota16f", [128, 16], F32)
    P.hpc = sb("hpc", [128, 1], F32)
    P.hb8 = sb("hb8", [128, 8], F32)
    P.msk16 = sb("msk16", [128, 128], F32)
    P.spacer = sb("spacer", [128, 8], F32)
    P.ln1fB = [sb(f"ln1fB{b}", [128, 8 * 512], BF16) for b in range(2)]
    P.q0 = sb("q0", [128, 8 * 512], BF16)
    P.atf = sb("atf", [128, 8 * 512], BF16)
    P.zex = sb("zex", [128, 8 * 512], BF16)
    P.zsb = sb("zsb", [H, 512], F32)
    P.zl = sb("zl", [H, 512], F32)
    P.zr = sb("zr", [H, 512], F32)
    P.xb = [sb(f"xb{b}", [128, D], F32) for b in range(2)]
    P.x2sb = [sb(f"x2sb{b}", [128, D], F32) for b in range(2)]
    P.ln2tm = [sb(f"ln2tm{b}", [128, D], BF16) for b in range(2)]
    P.ln2f = [sb(f"ln2f{b}", [128, 8 * 512], BF16) for b in range(2)]
    P.lg = sb("lg", [128, 32], F32)
    P.mx8 = sb("mx8", [128, 4], F32)
    P.sme = sb("sme", [128, 32], F32)
    P.smx = sb("smx", [128, 32], F32)
    P.ssum = sb("ssum", [128, 4], F32)
    P.eqm = sb("eqm", [128, 32], F32)
    P.cnd = sb("cnd", [128, 32], F32)
    P.ohb = sb("ohb", [128, 64], BF16)
    P.rl = sb("rl", [128, 32], F32)
    P.runrow_f = sb("runrow_f", [1, NE], F32)
    P.rnkT = sb("rnkT", [128, NT], F32)
    P.eixT = sb("eixT", [128, NT], F32)
    P.gvT = sb("gvT", [128, NT], F32)
    P.tot_sb = sb("tot_sb", [1, NE], F32)
    P.pk = [ps(f"pk{b}", [128, 512], F32) for b in range(2)]
    P.pv = [ps(f"pv{b}", [128, 512], F32) for b in range(2)]
    P.pkv = ps("pkv", [128, 512], F32)
    P.pmisc = ps("pmisc", [128, 512], F32)
    P.pkv2 = ps("pkv2", [128, 512], F32)
    P.ptr = ps("ptr", [128, 1024], BF16)
    P.sem = {}
    for n in ["sxa0", "sxa1", "svst", "srst", "sapp", "sstw0", "sstw1", "strn", "stev", "skp",
              "suw", "selu", "skv", "swt", "sa2",
              "slnf0", "slnf1", "slnc0", "slnc1", "sqp", "sqe", "szp", "szs", "szr", "szw", "szx",
              "sat", "sae", "sop", "sxb0", "sxb1", "sx2h", "sx2w0", "sx2w1", "sln2a",
              "sln2w0", "sln2w1",
              "sct", "sgd", "sgx", "soh", "srk", "srun", "ssel", "slgc", "sfin", "swtc", "svo", "sso", "skva", "smn"]:
        P.sem[n] = es.enter_context(nc.semaphore(n))
    P.vn = 0
    P.sn = 0
    return P


def vm(P, ins):
    P.vn += 1
    ins.then_inc(P.sem["svo"], 1)


def vw(P, e):
    e.wait_ge(P.sem["svo"], P.vn)


def sm(P, ins):
    P.sn += 1
    ins.then_inc(P.sem["sso"], 1)


def sw(P, e):
    e.wait_ge(P.sem["sso"], P.sn)


def A_sync(P, e):
    s = P.sem
    for i in range(NT):
        if i >= 2:
            e.wait_ge(s["sapp" if ALVL >= 3 else "svst"], i - 1)
        e.dma_start(out=P.xa[i % 2][:, :], in_=P.x[i * 128:(i + 1) * 128, :]).then_inc(s[f"sxa{i % 2}"], 16)


def A_gpsimd_consts(P, e):
    s = P.sem["swt"]
    n = 0
    def dma(dst, src):
        nonlocal n
        e.dma_start(out=dst, in_=src).then_inc(s, 16)
        n += 16
    dma(P.wk_sb[:, :], P.wkt.rearrange("(jc p) i -> p jc i", p=128))
    dma(P.wv_sb[:, :], P.wvt.rearrange("(jc p) i -> p jc i", p=128))
    dma(P.bkr[:, :], P.bk_d[:, :])
    dma(P.bvr[:, :], P.bv_d[:, :])
    dma(P.bor[:, :], P.bo_d[:, :])
    dma(P.bqcs[:, :], P.bqc_d[:, :])
    dma(P.wg_sb[:, :], P.wg_d.rearrange("(jc p) e -> p jc e", p=128))
    sc = P.sem["swtc"]
    e.iota(P.iotaP[:, :], [[1, 1]], base=0, channel_multiplier=1).then_inc(sc, 1)
    e.iota(P.iota8[:, :], [[1, 8]], base=0, channel_multiplier=0).then_inc(sc, 1)
    e.iota(P.iota16[:, :], [[1, 16]], base=0, channel_multiplier=0).then_inc(sc, 1)
    e.iota(P.su_i32[:, :], [[1, 128]], base=0, channel_multiplier=-1).then_inc(sc, 1)
    P.swt_pre = n
    dma(P.wq_sb[:, :], P.wqt.rearrange("(jc p) i -> p jc i", p=128))
    dma(P.wo_sb[:, :], P.wot.rearrange("(jc p) i -> p jc i", p=128))
    P.swt_c = n


def A_vec_consts(P, e):
    s = P.sem
    e.wait_ge(s["swt"], P.swt_c)
    e.wait_ge(s["swtc"], 4)
    e.memset(P.ones1[:, :], 1.0)
    e.memset(P.ones1f[:, :], 1.0)
    e.memset(P.onesc[:, :], 1.0)
    e.memset(P.runrow_f[:, :], 0.0)
    e.tensor_scalar(P.SU[:, :], P.su_i32[:, :], 0.0, None, ALU.is_gt)
    e.tensor_scalar(P.iden[:, :], P.su_i32[:, :], 0.0, None, ALU.is_equal).then_inc(s["swtc"], 1)
    e.tensor_scalar(P.iota8f[:, :], P.iota8[:, :], 0.0, None, ALU.add)
    e.tensor_scalar(P.iota16f[:, :], P.iota16[:, :], 0.0, None, ALU.add)
    e.tensor_scalar(P.iota8B[:, :], P.iota8[:, :], 16384.0, None, ALU.add)
    e.tensor_scalar(P.hpc[:, :], P.iotaP[:, :], 63.5, None, ALU.is_gt)
    e.wait_ge(s["swtc"], 5)
    e.tensor_scalar(P.idenf[:, :], P.iden[:, :], 0.0, None, ALU.add)
    e.memset(P.pmisc[0:33, :], 0.0)
    e.tensor_scalar(P.nbqc[:, :], P.bqcs[:, :], -1.0, None, ALU.mult).then_inc(s["swtc"], 1)


def A_vector(P, e):
    s = P.sem
    e.wait_ge(s["swtc"], 6)
    for i in range(NT + 4):
        j = i - 2
        a = i - 1
        if i < NT:
            b = i % 2
            e.wait_ge(s[f"sxa{i % 2}"], 16 * (i // 2 + 1))
            e.bn_stats(P.st6[b][:, 0:6], P.xa[b][:, 0:512])
            vm(P, e.bn_stats(P.st6[b][:, 6:12], P.xa[b][:, 512:1024]))
        if 0 <= j < NT and ALVL >= 4:
            bj = j % 2
            e.wait_ge(s["skp"], j + 1)
            if j >= 1:
                e.wait_ge(s["skv"] if ALVL >= 50 else s["skp"], j if ALVL >= 50 else j + 1)
                e.wait_ge(s["selu"], j)
            e.tensor_scalar(P.ucl[0][:, :], P.pk[0][:, :], 0.0, None, ALU.min)
            e.tensor_scalar(P.ucl[1][:, :], P.pk[1][:, :], 0.0, None, ALU.min).then_inc(s["smn"], 1)
            e.tensor_scalar(P.tcl[0][:, :], P.pk[0][:, :], 0.0, None, ALU.max)
            vm(P, e.tensor_scalar(P.tcl[1][:, :], P.pk[1][:, :], 0.0, None, ALU.max))
            e.tensor_copy(P.vt[bj][:, 0:512], P.pv[0][:, :])
            e.tensor_copy(P.vt[bj][:, 512:1024], P.pv[1][:, :])
        if i < NT:
            b = i % 2
            vw(P, e)
            vm(P, e.bn_aggr(P.st2[b][:, :], P.st6[b][:, :]))
        if 0 <= j < NT and ALVL >= 4:
            bj = j % 2
            if ALVL >= 40:
                e.wait_ge(s["suw"], j + 1)
            vw(P, e)
            if ALVL >= 40:
                e.tensor_tensor(P.kt[bj][:, 0:512], P.tcl[0][:, :], P.wcl[0][:, :], ALU.add)
                e.tensor_tensor(P.kt[bj][:, 512:1024], P.tcl[1][:, :], P.wcl[1][:, :],
                                ALU.add).then_inc(s["selu"], 1)
            elif ALVL >= 38:
                e.tensor_scalar(P.kt[bj][:, 0:512], P.tcl[0][:, :], 0.0, None, ALU.add)
                e.tensor_scalar(P.kt[bj][:, 512:1024], P.tcl[1][:, :], 0.0, None, ALU.add).then_inc(s["selu"], 1)
            elif ALVL >= 37:
                e.tensor_tensor(P.kt[bj][:, 0:512], P.tcl[0][:, :], P.tcl[1][:, :], ALU.add)
                e.tensor_tensor(P.kt[bj][:, 512:1024], P.tcl[1][:, :], P.tcl[0][:, :],
                                ALU.add).then_inc(s["selu"], 1)
            elif ALVL >= 36:
                e.tensor_scalar(P.kt[bj][:, 0:512], P.tcl[0][:, :], 0.0, None, ALU.add)
                e.tensor_scalar(P.kt[bj][:, 512:1024], P.tcl[1][:, :], 0.0, None, ALU.add).then_inc(s["selu"], 1)
            elif ALVL >= 35:
                e.tensor_copy(P.kt[bj][:, 0:512], P.tcl[0][:, :])
                e.tensor_copy(P.kt[bj][:, 512:1024], P.tcl[1][:, :]).then_inc(s["selu"], 1)
            else:
                e.engine_nop().then_inc(s["selu"], 1)
        jk = i - 3
        if 0 <= jk < NT and ALVL >= 50:
            e.wait_ge(s["skv"], jk + 1)
            if jk >= 1:
                e.wait_ge(s["skva"], jk)
            if jk == 0:
                e.tensor_copy(P.kvacc[0][0:64, :], P.pkv[0:64, :])
                e.tensor_copy(P.kvacc[0][64:128, :], P.pkv2[0:64, :])
                ins = e.tensor_copy(P.ksacc[0][:, :], P.pmisc[0:33, :])
            else:
                e.tensor_tensor(P.kvacc[jk % 2][0:64, :], P.pkv[0:64, :],
                                P.kvacc[(jk - 1) % 2][0:64, :], ALU.add)
                e.tensor_tensor(P.kvacc[jk % 2][64:128, :], P.pkv2[0:64, :],
                                P.kvacc[(jk - 1) % 2][64:128, :], ALU.add)
                ins = e.tensor_tensor(P.ksacc[jk % 2][:, :], P.pmisc[0:33, :],
                                      P.ksacc[(jk - 1) % 2][:, :], ALU.add)
            ins.then_inc(s["skva"], 1)
        if i < NT:
            b = i % 2
            vw(P, e)
            e.tensor_scalar(P.vep[b][:, :], P.st2[b][:, 1:2], EPS, None, ALU.add).then_inc(s["svst"], 1)
        if 0 <= a < NT and ALVL >= 3:
            ba = a % 2
            e.wait_ge(s["srst"], a + 1)
            if a >= 2:
                e.wait_ge(s["strn"], a - 1)
                e.wait_ge(s[f"sstw{a % 2}"], 16 * (a // 2))
            e.tensor_scalar(P.ln1tm[ba][:, :], P.xa[ba][:, :], P.st2[ba][:, 0:1],
                            P.rstd[ba][:, 0:1], ALU.subtract, ALU.mult).then_inc(s["sapp"], 1)
        if 0 <= a < NT and ALVL >= 3:
            e.wait_ge(s["strn"], a + 1)
            if a >= 2 and ALVL >= 4:
                e.wait_ge(s["skp"], a - 1)
            e.tensor_copy(P.ln1f[a % 2][:, :], P.ptr[:, :]).then_inc(s["stev"], 1)


def A_scalar(P, e):
    s = P.sem
    for i in range(NT + 1):
        j = i - 1
        if i < NT:
            b = i % 2
            e.wait_ge(s["svst"], i + 1)
            sm(P, e.activation(P.lnv[b][:, :], P.vep[b][:, :], AF.Ln))
        if 0 <= j < NT and ALVL >= 38:
            e.wait_ge(s["smn"], j + 1)
            e.activation(P.wcl[0][:, :], P.ucl[0][:, :], AF.Exp)
            e.activation(P.wcl[1][:, :], P.ucl[1][:, :], AF.Exp).then_inc(s["suw"], 1)
        if i < NT:
            b = i % 2
            sw(P, e)
            e.activation(P.rstd[b][:, :], P.lnv[b][:, :], AF.Exp, scale=-0.5).then_inc(s["srst"], 1)
        if 0 <= j < NT and ALVL >= 3:
            e.wait_ge(s["sapp"], j + 1)
            e.dma_start(out=P.ln1st[j * 128:(j + 1) * 128, :],
                        in_=P.ln1tm[j % 2][:, :]).then_inc(s[f"sstw{j % 2}"], 16)


def A_tensor(P, e):
    s = P.sem
    e.wait_ge(s["swt"], P.swt_c)
    e.wait_ge(s["swtc"], 6)
    for i in range(NT + 3):
        a = i - 1
        j = i - 2
        j2 = i - 3
        if 0 <= j < NT and ALVL >= 4:
            bj = j % 2
            e.wait_ge(s["stev"], j + 1)
            if j >= 1:
                e.wait_ge(s["selu"], j)
            for hlf in range(2):
                for jc in range(8):
                    e.matmul(P.pk[hlf][:, :], P.ln1f[bj][:, jc * 128:(jc + 1) * 128],
                             P.wk_sb[:, jc * D + hlf * 512: jc * D + hlf * 512 + 512],
                             start=(jc == 0), stop=False)
                e.matmul(P.pk[hlf][:, :], P.ones1[:, :],
                         P.bkr[0:1, hlf * 512:(hlf + 1) * 512], start=False, stop=True)
            for hlf in range(2):
                for jc in range(8):
                    e.matmul(P.pv[hlf][:, :], P.ln1f[bj][:, jc * 128:(jc + 1) * 128],
                             P.wv_sb[:, jc * D + hlf * 512: jc * D + hlf * 512 + 512],
                             start=(jc == 0), stop=False)
                ins = e.matmul(P.pv[hlf][:, :], P.ones1[:, :],
                               P.bvr[0:1, hlf * 512:(hlf + 1) * 512], start=False, stop=True)
            ins.then_inc(s["skp"], 1)
        if 0 <= j2 < NT and ALVL >= 50:
            bj = j2 % 2
            e.wait_ge(s["selu"], j2 + 1)
            if j2 >= 1:
                e.wait_ge(s["skva"], j2)
            for h in range(H):
                pr = h // 2
                bank = P.pkv if h % 2 == 0 else P.pkv2
                e.matmul(bank[0:64, pr * 64:(pr + 1) * 64],
                         P.kt[bj][:, h * 64:(h + 1) * 64], P.vt[bj][:, h * 64:(h + 1) * 64],
                         start=(h < 2), stop=(h >= H - 2))
            for hlf in range(2):
                ins = e.matmul(P.pmisc[hlf * 32:hlf * 32 + 1, :], P.onesc[:, :],
                               P.kt[bj][:, hlf * 512:(hlf + 1) * 512],
                               start=True, stop=True)
            ins.then_inc(s["skv"], 1)
        if 0 <= a < NT and ALVL >= 3:
            e.wait_ge(s["sapp"], a + 1)
            if a >= 1:
                e.wait_ge(s["stev"], a)
            for jc in range(8):
                ins = e.transpose(P.ptr[:, jc * 128:(jc + 1) * 128],
                                  P.ln1tm[a % 2][:, jc * 128:(jc + 1) * 128], P.iden[:, :])
            ins.then_inc(s["strn"], 1)


def A2_vector(P, e):
    if ALVL < 50:
        return
    s = P.sem
    e.wait_ge(s["skva"], NT)
    vw(P, e)
    e.tensor_scalar(P.kvt[:, :], P.kvacc[(NT - 1) % 2][:, :], 0.0, None, ALU.add).then_inc(s["sa2"], 1)
    e.wait_ge(s["sa2"], 3)
    e.tensor_copy(P.ksc[:, :], P.pmisc[:, 64:72])
    vm(P, e.tensor_scalar(P.hb8[:, :], P.iota8f[:, :], 2.0, P.hpc[:, 0:1], ALU.mult, ALU.add))
    vw(P, e)
    for c in range(8):
        ins = e.tensor_scalar(P.msk16[:, c * 16:(c + 1) * 16], P.iota16f[:, :],
                              P.hb8[:, c:c + 1], None, ALU.is_equal)
    vm(P, ins)
    vw(P, e)
    for c in range(8):
        ins = e.tensor_scalar(P.KS[:, c * 16:(c + 1) * 16], P.msk16[:, c * 16:(c + 1) * 16],
                              P.ksc[:, c:c + 1], None, ALU.mult)
    ins.then_inc(s["sa2"], 1)


def A2_tensor(P, e):
    if ALVL < 50:
        return
    s = P.sem
    e.wait_ge(s["sa2"], 1)
    e.wait_ge(s["skva"], NT)
    ka = P.ksacc[(NT - 1) % 2]
    for jc in range(8):
        if jc < 4:
            srcrow, idn = ka[0:1, jc * 128:(jc + 1) * 128], P.idenf[0:1, 0:1]
        else:
            srcrow, idn = ka[32:33, (jc - 4) * 128:(jc - 3) * 128], P.idenf[32:33, 32:33]
        ins = e.transpose(P.pmisc[:, 64 + jc:65 + jc], srcrow, idn)
    ins.then_inc(s["sa2"], 2)


def B_sync(P, e):
    s = P.sem
    for st in range(NST):
        e.wait_ge(s["sstw0"], 16 * (2 * st + 2))
        e.wait_ge(s["sstw1"], 16 * (2 * st + 2))
        if st >= 2:
            e.wait_ge(s["sqp"], 8 * (st - 1))
        for jc in range(8):
            e.dma_start_transpose(P.ln1fB[st % 2][:, jc * 512:(jc + 1) * 512],
                                  P.ln1st[st * 512:(st + 1) * 512, jc * 128:(jc + 1) * 128]
                                  ).then_inc(s[f"slnf{st % 2}"], 16)
        e.wait_ge(s["szr"], st + 1)
        if st >= 1:
            e.wait_ge(s["szx"], 32 * st)
        e.dma_start(out=P.z_dram[:, :], in_=P.zr[:, :]).then_inc(s["szw"], 16)
        for t in range(4):
            i = st * 4 + t
            if i >= 2:
                e.wait_ge(s["sx2h"], 2 * (i - 1))
            e.dma_start(out=P.xb[i % 2][:, :], in_=P.x[i * 128:(i + 1) * 128, :]).then_inc(s[f"sxb{i % 2}"], 16)


def B_gpsimd(P, e):
    s = P.sem
    for st in range(NST):
        e.wait_ge(s["szw"], 16 * (st + 1))
        if st >= 1:
            e.wait_ge(s["sae"], 8 * st)
        dstv = P.zex[:, :].rearrange("p (c t) -> p c t", t=512)
        src_ev = P.z_dram[0::2, :].unsqueeze(0).to_broadcast([64, 8, 512])
        src_od = P.z_dram[1::2, :].unsqueeze(0).to_broadcast([64, 8, 512])
        e.dma_start(out=dstv[0:64, :, :], in_=src_ev).then_inc(s["szx"], 16)
        e.dma_start(out=dstv[64:128, :, :], in_=src_od).then_inc(s["szx"], 16)


def B_tensor(P, e):
    s = P.sem
    e.wait_ge(s["swt"], P.swt_c)
    e.wait_ge(s["swtc"], 6)
    e.wait_ge(s["sa2"], 4)
    for st in range(NST):
        b = st % 2
        e.wait_ge(s[f"slnf{st % 2}"], 128 * (st // 2 + 1))
        for ic in range(8):
            g = 8 * st + ic
            pq = P.pk[ic % 2]
            if g >= 2:
                e.wait_ge(s["sqe"], g - 1)
            for jc in range(8):
                ins = e.matmul(pq[:, :], P.wq_sb[:, jc * D + ic * 128: jc * D + ic * 128 + 128],
                               P.ln1fB[b][:, jc * 512:(jc + 1) * 512],
                               start=(jc == 0), stop=(jc == 7))
            ins.then_inc(s["sqp"], 1)
        e.wait_ge(s["sqe"], 8 * (st + 1))
        if st >= 1:
            e.wait_ge(s["szs"], st)
        for jc in range(8):
            ins = e.matmul(P.pmisc[0:16, :], P.KS[:, jc * 16:(jc + 1) * 16],
                           P.q0[:, jc * 512:(jc + 1) * 512], start=(jc == 0), stop=(jc == 7))
        ins.then_inc(s["szp"], 1)
        for c in range(8):
            g = 8 * st + c
            pa = P.pv[c % 2]
            if g >= 2:
                e.wait_ge(s["sae"], g - 1)
            for hh in range(2):
                h = 2 * c + hh
                ins = e.matmul(pa[hh * 64:hh * 64 + 64, :],
                               P.kvt[hh * 64:hh * 64 + 64, c * 64:(c + 1) * 64],
                               P.q0[hh * 64:hh * 64 + 64, c * 512:(c + 1) * 512],
                               start=True, stop=True)
            ins.then_inc(s["sat"], 1)
        e.wait_ge(s["sae"], 8 * (st + 1))
        for t in range(4):
            i = st * 4 + t
            for hlf in range(2):
                gg = 2 * i + hlf
                if gg >= 1:
                    e.wait_ge(s["sx2h"], gg)
                for mc in range(8):
                    e.matmul(P.pkv[:, :], P.atf[:, mc * 512 + t * 128: mc * 512 + t * 128 + 128],
                             P.wo_sb[:, mc * D + hlf * 512: mc * D + hlf * 512 + 512],
                             start=(mc == 0), stop=False)
                e.matmul(P.pkv[:, :], P.ones1[:, :], P.bor[0:1, hlf * 512:(hlf + 1) * 512],
                         start=False, stop=True).then_inc(s["sop"], 1)


def B_vector(P, e):
    s = P.sem
    for st in range(NST):
        for ic in range(8):
            g = 8 * st + ic
            pq = P.pk[ic % 2]
            e.wait_ge(s["sqp"], g + 1)
            if g >= 1:
                e.wait_ge(s["sqe"], g)
            e.tensor_scalar(P.ucl[ic % 2][:, :], pq[:, :], P.bqcs[:, ic:ic + 1], 0.0,
                            ALU.add, ALU.min).then_inc(s["smn"], 1)
            vm(P, e.tensor_scalar(P.tcl[0][:, :], pq[:, :], P.bqcs[:, ic:ic + 1], 0.0,
                                  ALU.add, ALU.max))
            e.wait_ge(s["suw"], NT + g + 1)
            vw(P, e)
            e.tensor_tensor(P.q0[:, ic * 512:(ic + 1) * 512], P.tcl[0][:, :],
                            P.wcl[ic % 2][:, :], ALU.add).then_inc(s["sqe"], 1)
        e.wait_ge(s["szp"], st + 1)
        e.tensor_scalar(P.zsb[:, :], P.pmisc[0:16, :], 1e-6, None, ALU.add).then_inc(s["szs"], 1)
        for c in range(8):
            g = 8 * st + c
            pa = P.pv[c % 2]
            e.wait_ge(s["sat"], g + 1)
            e.wait_ge(s["szx"], 32 * (st + 1))
            e.tensor_tensor(P.atf[:, c * 512:(c + 1) * 512], pa[:, :],
                            P.zex[:, c * 512:(c + 1) * 512], ALU.mult).then_inc(s["sae"], 1)
        for t in range(4):
            i = st * 4 + t
            bi = i % 2
            e.wait_ge(s[f"sxb{i % 2}"], 16 * (i // 2 + 1))
            if i >= 2:
                e.wait_ge(s[f"sx2w{i % 2}"], 16 * (i // 2))
                e.wait_ge(s["sln2a"], i - 1)
            for hlf in range(2):
                e.wait_ge(s["sop"], 2 * i + hlf + 1)
                e.tensor_tensor(P.x2sb[bi][:, hlf * 512:(hlf + 1) * 512], P.pkv[:, :],
                                P.xb[bi][:, hlf * 512:(hlf + 1) * 512], ALU.add
                                ).then_inc(s["sx2h"], 1)
            e.wait_ge(s["sx2h"], 2 * (i + 1))
            e.bn_stats(P.st6[bi][:, 0:6], P.x2sb[bi][:, 0:512])
            vm(P, e.bn_stats(P.st6[bi][:, 6:12], P.x2sb[bi][:, 512:1024]))
            vw(P, e)
            vm(P, e.bn_aggr(P.st2[bi][:, :], P.st6[bi][:, :]))
            vw(P, e)
            e.tensor_scalar(P.vep[bi][:, :], P.st2[bi][:, 1:2], EPS, None, ALU.add).then_inc(s["svst"], 1)
            e.wait_ge(s["srst"], NT + i + 1)
            if i >= 2:
                e.wait_ge(s[f"sln2w{i % 2}"], 16 * (i // 2))
            e.tensor_scalar(P.ln2tm[bi][:, :], P.x2sb[bi][:, :], P.st2[bi][:, 0:1],
                            P.rstd[bi][:, 0:1], ALU.subtract, ALU.mult).then_inc(s["sln2a"], 1)


def B_scalar(P, e):
    s = P.sem
    for st in range(NST):
        for ic in range(8):
            g = 8 * st + ic
            e.wait_ge(s["smn"], NT + g + 1)
            if g >= 2:
                e.wait_ge(s["sqe"], g - 1)
            e.activation(P.wcl[ic % 2][:, :], P.ucl[ic % 2][:, :], AF.Exp).then_inc(s["suw"], 1)
        e.wait_ge(s["szs"], st + 1)
        sm(P, e.activation(P.zl[:, :], P.zsb[:, :], AF.Ln))
        sw(P, e)
        e.activation(P.zr[:, :], P.zl[:, :], AF.Exp, scale=-1.0).then_inc(s["szr"], 1)
        for t in range(4):
            i = st * 4 + t
            bi = i % 2
            e.wait_ge(s["svst"], NT + i + 1)
            sm(P, e.activation(P.lnv[bi][:, :], P.vep[bi][:, :], AF.Ln))
            e.wait_ge(s["sx2h"], 2 * (i + 1))
            e.dma_start(out=P.o_x2[i * 128:(i + 1) * 128, :], in_=P.x2sb[bi][:, :]).then_inc(s[f"sx2w{i % 2}"], 16)
            sw(P, e)
            e.activation(P.rstd[bi][:, :], P.lnv[bi][:, :], AF.Exp, scale=-0.5).then_inc(s["srst"], 1)
            e.wait_ge(s["sln2a"], i + 1)
            e.dma_start(out=P.o_ln2[i * 128:(i + 1) * 128, :], in_=P.ln2tm[bi][:, :]).then_inc(s[f"sln2w{i % 2}"], 16)


def C0_sync(P, e):
    s = P.sem
    for st in range(NST):
        e.wait_ge(s["sln2w0"], 16 * 2 * (st + 1))
        e.wait_ge(s["sln2w1"], 16 * 2 * (st + 1))
        if st >= 2:
            e.wait_ge(s["sct"], st - 1)
        for jc in range(8):
            e.dma_start_transpose(P.ln2f[st % 2][:, jc * 512:(jc + 1) * 512],
                                  P.o_ln2[st * 512:(st + 1) * 512, jc * 128:(jc + 1) * 128]
                                  ).then_inc(s[f"slnc{st % 2}"], 16)


def C0_tensor(P, e):
    s = P.sem
    for st in range(NST):
        b = st % 2
        e.wait_ge(s[f"slnc{st % 2}"], 128 * (st // 2 + 1))
        if st >= 1:
            e.wait_ge(s["slgc"], st)
        for t in range(4):
            for jc in range(8):
                ins = e.matmul(P.pk[0][:, t * 8:(t + 1) * 8],
                               P.ln2f[b][:, jc * 512 + t * 128: jc * 512 + t * 128 + 128],
                               P.wg_sb[:, jc * 8:(jc + 1) * 8],
                               start=(jc == 0), stop=(jc == 7))
        ins.then_inc(s["sct"], 1)
        for t in range(4):
            i = st * 4 + t
            e.wait_ge(s["soh"], 4 * (st + 1))
            e.wait_ge(s["srun"], i)
            if i >= 4:
                e.wait_ge(s["ssel"], 4 * st)
            sl = P.pv[0][:, (i % 4) * 8:(i % 4) * 8 + 8]
            e.matmul(sl, P.ones1f[:, :], P.runrow_f[0:1, :], start=True, stop=False)
            e.matmul(sl, P.SU[:, :], P.ohb[:, (st % 2) * 32 + t * 8:(st % 2) * 32 + (t + 1) * 8], start=False, stop=True)
            e.matmul(P.pkv2[32:33, 0:8], P.onesc[:, :], P.ohb[:, (st % 2) * 32 + t * 8:(st % 2) * 32 + (t + 1) * 8],
                     start=(i == 0), stop=(i == NT - 1)).then_inc(s["srk"], 1)


def C0_vector(P, e):
    s = P.sem
    for st in range(NST):
        e.wait_ge(s["sct"], st + 1)
        if st >= 1:
            e.wait_ge(s["sgx"], st)
        vm(P, e.tensor_copy(P.lg[:, :], P.pk[0][:, 0:32]))
        vw(P, e)
        for t in range(4):
            ins = e.reduce_max(P.mx8[:, t:t + 1], P.lg[:, t * 8:(t + 1) * 8], AX.X)
        vm(P, ins)
        vw(P, e)
        for t in range(4):
            ins = e.tensor_scalar(P.smx[:, t * 8:(t + 1) * 8], P.lg[:, t * 8:(t + 1) * 8],
                                  P.mx8[:, t:t + 1], None, ALU.subtract)
            if t == 3:
                ins.then_inc(s["sgd"], 1)
        e.wait_ge(s["sgd"], st + 1)
        for t in range(4):
            ins = e.tensor_scalar(P.eqm[:, t * 8:(t + 1) * 8], P.smx[:, t * 8:(t + 1) * 8],
                                  -1e-30, None, ALU.is_ge)
            if t == 0:
                ins.then_inc(s["slgc"], 1)
        vm(P, ins)
        vw(P, e)
        for t in range(4):
            ins = e.scalar_tensor_tensor(P.cnd[:, t * 8:(t + 1) * 8], P.eqm[:, t * 8:(t + 1) * 8],
                                         -16384.0, P.iota8B[:, :], ALU.mult, ALU.add)
        vm(P, ins)
        vw(P, e)
        for t in range(4):
            ins = e.tensor_reduce(P.eixT[:, st * 4 + t: st * 4 + t + 1], P.cnd[:, t * 8:(t + 1) * 8],
                                  AX.X, ALU.min)
        vm(P, ins)
        vw(P, e)
        if st >= 2:
            e.wait_ge(s["srk"], 4 * (st - 1))
        ohof = (st % 2) * 32
        for t in range(4):
            ins = e.tensor_scalar(P.ohb[:, ohof + t * 8:ohof + (t + 1) * 8], P.iota8f[:, :],
                                  P.eixT[:, st * 4 + t: st * 4 + t + 1], None, ALU.is_equal)
        ins.then_inc(s["soh"], 4)
        e.wait_ge(s["sgx"], st + 1)
        for t in range(4):
            ins = e.reduce_sum(P.ssum[:, t:t + 1], P.sme[:, t * 8:(t + 1) * 8], AX.X)
        vm(P, ins)
        vw(P, e)
        e.reciprocal(P.gvT[:, st * 4:(st + 1) * 4], P.ssum[:, 0:4])
        for t in range(4):
            i = st * 4 + t
            e.wait_ge(s["srk"], i + 1)
            ins = e.tensor_tensor(P.rl[:, t * 8:(t + 1) * 8], P.pv[0][:, (i % 4) * 8:(i % 4) * 8 + 8],
                                  P.ohb[:, (st % 2) * 32 + t * 8:(st % 2) * 32 + (t + 1) * 8], ALU.mult)
            if t == 3:
                vm(P, ins)
            e.tensor_copy(P.runrow_f[0:1, :], P.pkv2[32:33, 0:8]).then_inc(s["srun"], 1)
        vw(P, e)
        for t in range(4):
            ins = e.reduce_sum(P.rnkT[:, st * 4 + t: st * 4 + t + 1], P.rl[:, t * 8:(t + 1) * 8], AX.X)
        ins.then_inc(s["ssel"], 4)
    e.wait_ge(s["srun"], NT)
    e.tensor_copy(P.tot_sb[0:1, :], P.pkv2[32:33, 0:8]).then_inc(s["sfin"], 1)


def C0_scalar(P, e):
    s = P.sem
    for st in range(NST):
        e.wait_ge(s["sgd"], st + 1)
        e.activation(P.sme[:, :], P.smx[:, :], AF.Exp).then_inc(s["sgx"], 1)


def final_gpsimd(P, e):
    s = P.sem
    e.wait_ge(s["sfin"], 1)
    e.wait_ge(s["ssel"], NT)
    n = P.swt_c
    e.dma_start(out=P.o_cnt[:, :], in_=P.tot_sb[0:1, :]).then_inc(s["swt"], 16)
    e.dma_start(out=P.o_rank[:, :], in_=P.rnkT[:, :]).then_inc(s["swt"], 16)
    e.dma_start(out=P.o_eidx[:, :], in_=P.eixT[:, :]).then_inc(s["swt"], 16)
    e.dma_start(out=P.o_gv[:, :], in_=P.gvT[:, :]).then_inc(s["swt"], 16)
    e.wait_ge(s["swt"], n + 64)
    e.wait_ge(s["sx2w0"], 16 * NT // 2)
    e.wait_ge(s["sx2w1"], 16 * NT // 2)
    e.wait_ge(s["sln2w0"], 16 * NT // 2)
    e.wait_ge(s["sln2w1"], 16 * NT // 2)


def emit_A(P, phases=("A", "B", "C0")):
    nc = P.nc
    pA = "A" in phases; pB = "B" in phases; pC = "C0" in phases
    with P.es:
        block = P.es.enter_context(nc.Block())

        @block.sync
        def _(e):
            if pA: A_sync(P, e)
            if pB: B_sync(P, e)
            if pC: C0_sync(P, e)

        @block.gpsimd
        def _(e):
            A_gpsimd_consts(P, e)
            if pB: B_gpsimd(P, e)
            if pC:
                final_gpsimd(P, e)
            elif pB:
                e.wait_ge(P.sem["sx2w0"], 16 * NT // 2)
                e.wait_ge(P.sem["sx2w1"], 16 * NT // 2)
                e.wait_ge(P.sem["sln2w0"], 16 * NT // 2)
                e.wait_ge(P.sem["sln2w1"], 16 * NT // 2)
            else:
                if ALVL >= 3:
                    e.wait_ge(P.sem["sstw0"], 16 * NT // 2)
                    e.wait_ge(P.sem["sstw1"], 16 * NT // 2)
                if ALVL >= 4:
                    e.wait_ge(P.sem["selu"], NT)
                if ALVL >= 50:
                    e.wait_ge(P.sem["skva"], NT)
                if ALVL < 3:
                    e.wait_ge(P.sem["srst"], NT)

        @block.vector
        def _(e):
            A_vec_consts(P, e)
            if pA: A_vector(P, e)
            if pA: A2_vector(P, e)
            if pB: B_vector(P, e)
            if pC: C0_vector(P, e)

        @block.scalar
        def _(e):
            if pA: A_scalar(P, e)
            if pB: B_scalar(P, e)
            if pC: C0_scalar(P, e)

        @block.tensor
        def _(e):
            if pA: A_tensor(P, e)
            if pA: A2_tensor(P, e)
            if pB: B_tensor(P, e)
            if pC: C0_tensor(P, e)
    return nc


def host_prep_A(inp):
    Wq = np.asarray(inp["L0_attn_Wq"]); Wk = np.asarray(inp["L0_attn_Wk"])
    Wv = np.asarray(inp["L0_attn_Wv"]); Wo = np.asarray(inp["L0_attn_Wo"])
    base = {
        "wqt": np.ascontiguousarray(Wq.T), "wkt": np.ascontiguousarray(Wk.T),
        "wvt": np.ascontiguousarray(Wv.T), "wot": np.ascontiguousarray(Wo.T),
        "bk": np.asarray(inp["L0_attn_bk"]).reshape(1, D),
        "bv": np.asarray(inp["L0_attn_bv"]).reshape(1, D),
        "bo": np.asarray(inp["L0_attn_bo"]).reshape(1, D),
        "bqc": np.ascontiguousarray(np.asarray(inp["L0_attn_bq"]).reshape(8, 128).T),
        "wg": np.ascontiguousarray(np.asarray(inp["L0_ffn_wg"])),
    }
    x = np.asarray(inp["L0_attn_hidden_states"])
    return [dict(base, x=np.ascontiguousarray(x[c])) for c in range(8)]


GRP = 256
NGRP = 40
NSLOT = NGRP * GRP
W1SL = 128 * 8 * 512
W2SL = 128 * 4 * 1024
SENT = 1 << 30


def build_B():
    nc = bass.Bass("TRN2", target_bir_lowering=False)
    P = P_()
    P.nc = nc
    P.cnt8 = nc.dram_tensor("cnt8", [8, 8], F32, kind="ExternalInput").ap()
    P.rmsk = nc.dram_tensor("rmsk", [8, 1], F32, kind="ExternalInput").ap()
    P.mhot = nc.dram_tensor("mhot", [8, 1], F32, kind="ExternalInput").ap()
    P.x2d = nc.dram_tensor("x2d", [S, D], F32, kind="ExternalInput").ap()
    P.ln2d = nc.dram_tensor("ln2d", [S, D], BF16, kind="ExternalInput").ap()
    P.rnkd = nc.dram_tensor("rnkd", [128, NT], F32, kind="ExternalInput").ap()
    P.eixd = nc.dram_tensor("eixd", [128, NT], F32, kind="ExternalInput").ap()
    P.gvd = nc.dram_tensor("gvd", [128, NT], F32, kind="ExternalInput").ap()
    P.w1r = nc.dram_tensor("w1r", [1, 8 * 128 * 8 * DH], F32, kind="ExternalInput").ap()
    P.w2r = nc.dram_tensor("w2r", [1, 8 * 128 * 32 * 1024], F32, kind="ExternalInput").ap()
    P.b1r = nc.dram_tensor("b1r", [8 * 128, 32], F32, kind="ExternalInput").ap()
    P.b2d = nc.dram_tensor("b2d", [8, 1024], F32, kind="ExternalInput").ap()
    P.y = nc.dram_tensor("y", [S, D], F32, kind="ExternalOutput").ap()
    P.dispd = nc.dram_tensor("dispd", [NSLOT, D], BF16).ap()
    P.eod = nc.dram_tensor("eod", [NSLOT, D], F32).ap()

    es = ExitStack()
    P.es = es
    sb = lambda name, shape, dt: es.enter_context(nc.sbuf_tensor(name, shape, dt))
    ps = lambda name, shape, dt: es.enter_context(nc.psum_tensor(name, shape, dt))
    P.w1sb = sb("w1sb", [128, 8 * DH], BF16)
    P.w2sb = sb("w2sb", [128, 32 * 1024], BF16)
    P.hfm = sb("hfm", [128, 32 * GRP], BF16)
    P.dspf = [sb(f"dspf{b}", [128, 8 * GRP], BF16) for b in range(2)]
    P.b1g = [sb(f"b1g{b}", [128, 32], F32) for b in range(2)]
    P.b2sb = sb("b2sb", [8, 1024], BF16)
    P.eosb = [sb(f"eosb{b}", [128, D], F32) for b in range(2)]
    P.cnts = sb("cnts", [8, 8], F32)
    P.rmsks = sb("rmsks", [8, 1], F32)
    P.mhots = sb("mhots", [8, 1], F32)
    P.rnks = sb("rnks", [128, NT], F32)
    P.eixs = sb("eixs", [128, NT], F32)
    P.gvs = sb("gvs", [128, NT], F32)
    P.gvmT = sb("gvmT", [128, NT], F32)
    P.slotT = sb("slotT", [128, NT], I32)
    P.slotF = sb("slotF", [128, NT], F32)
    P.coffr = sb("coffr", [1, 8], F32)
    P.mycr = sb("mycr", [1, 8], F32)
    P.remr = sb("remr", [1, 8], F32)
    P.kptr = sb("kptr", [1, 8], F32)
    P.ntgi = sb("ntgi", [1, 8], I32)
    P.t8i = sb("t8i", [1, 8], I32)
    P.t8i2 = sb("t8i2", [1, 8], I32)
    P.ntgr = sb("ntgr", [1, 8], F32)
    P.ntgc = sb("ntgc", [8, 1], F32)
    P.gsr = sb("gsr", [1, 8], F32)
    P.gs256r = sb("gs256r", [1, 8], F32)
    P.t8a = sb("t8a", [1, 8], F32)
    P.t8b = sb("t8b", [1, 8], F32)
    P.Erow = sb("Erow", [1, NGRP], F32)
    P.swf = sb("swf", [1, NGRP], F32)
    P.swsent = sb("swsent", [1, NGRP], F32)

    P.wof_f = sb("wof_f", [1, NGRP], F32)
    P.wof = sb("wof", [1, NGRP], I32)
    P.t40a = sb("t40a", [1, NGRP], F32)
    P.t40b = sb("t40b", [1, NGRP], F32)
    P.iota40 = sb("iota40", [1, NGRP], I32)
    P.iota40f = sb("iota40f", [1, NGRP], F32)
    P.iexb = sb("iexb", [128, NGRP], I32)
    P.iexf = sb("iexf", [128, NGRP], F32)
    P.ohgall = sb("ohgall", [8, NGRP], BF16)
    P.ohgf = sb("ohgf", [8, NGRP], F32)
    P.iotaP = sb("iotaP", [128, 1], I32)
    P.iotaPf = sb("iotaPf", [128, 1], F32)
    P.iota8 = sb("iota8", [128, 8], I32)
    P.iota8f = sb("iota8f", [128, 8], F32)
    P.iotaP8 = sb("iotaP8", [8, NGRP], I32)
    P.iotaP8f = sb("iotaP8f", [8, NGRP], F32)
    P.su8 = sb("su8", [8, 8], F32)
    P.su_i32 = sb("su_i32", [128, 128], I32)
    P.ones1f = sb("ones1f", [1, 128], F32)
    P.idenf = sb("idenf", [128, 128], F32)
    P.bigc = sb("bigc", [128, 1], F32)
    P.ohD = sb("ohD", [128, 8], F32)
    P.cselc = sb("cselc", [128, 8], F32)
    P.gselc = sb("gselc", [128, 8], F32)
    P.csel = sb("csel", [128, 1], F32)
    P.gsel = sb("gsel", [128, 1], F32)
    P.globc = sb("globc", [128, 1], F32)
    P.keepc = sb("keepc", [128, 1], F32)
    P.keepi = sb("keepi", [128, 1], I32)
    P.sl0 = sb("sl0", [128, 1], F32)
    P.coffb = sb("coffb", [128, 8], F32)
    P.gs256b = sb("gs256b", [128, 8], F32)
    P.zt = sb("zt", [128, D], BF16)
    P.ln2t = [sb(f"ln2t{b}", [128, D], BF16) for b in range(2)]
    P.x2t = [sb(f"x2t{b}", [128, D], F32) for b in range(2)]
    P.gath = [sb(f"gath{b}", [128, D], F32) for b in range(2)]
    P.osb = [sb(f"osb{b}", [128, D], F32) for b in range(2)]
    P.ph = [ps(f"ph{b}", [128, GRP], F32) for b in range(2)]
    P.peo = [ps(f"peo{b}", [128, 512], F32) for b in range(2)]
    P.pmb = ps("pmb", [128, 512], F32)
    P.sem = {}
    for n in ["sin", "sioc", "sc1", "sc1b", "sslot", "sdl0", "sdl1", "ssc0", "ssc1",
              "swl", "smm", "sdf0", "sdf1", "sb1_0", "sb1_1", "sh", "sho",
              "seo", "sev", "seow0", "seow1", "sga0", "sga1", "szg", "sx2l0", "sx2l1", "sout",
              "syw0", "syw1", "svo", "szt", "sztd"]:
        P.sem[n] = es.enter_context(nc.semaphore(n))
    P.vn = 0
    return P


def Bl_gpsimd(P, e):
    s = P.sem
    n = 0
    def dma(dst, src):
        nonlocal n
        e.dma_start(out=dst, in_=src).then_inc(s["sin"], 16)
        n += 16
    dma(P.cnts[:, :], P.cnt8[:, :])
    dma(P.rmsks[:, :], P.rmsk[:, :])
    dma(P.mhots[:, :], P.mhot[:, :])
    dma(P.rnks[:, :], P.rnkd[:, :])
    dma(P.eixs[:, :], P.eixd[:, :])
    dma(P.gvs[:, :], P.gvd[:, :])
    dma(P.b2sb[:, :], P.b2d[:, :])
    P.sin_c = n
    e.iota(P.iota40[:, :], [[1, NGRP]], base=0, channel_multiplier=0).then_inc(s["sioc"], 1)
    e.iota(P.iotaP[:, :], [[1, 1]], base=0, channel_multiplier=1).then_inc(s["sioc"], 1)
    e.iota(P.iota8[:, :], [[1, 8]], base=0, channel_multiplier=0).then_inc(s["sioc"], 1)
    e.iota(P.iotaP8[:, :], [[0, NGRP]], base=0, channel_multiplier=1).then_inc(s["sioc"], 1)
    e.iota(P.su_i32[:, :], [[1, 128]], base=0, channel_multiplier=-1).then_inc(s["sioc"], 1)
    e.memset(P.gath[0][:, :], 0.0)
    e.memset(P.gath[1][:, :], 0.0).then_inc(s["szg"], 1)
    e.wait_ge(s["szt"], 1)
    e.dma_start(out=P.dispd[:, :].rearrange("(r p) d -> p r d", p=128),
                in_=P.zt[:, :].unsqueeze(1).to_broadcast([128, NSLOT // 128, D])
                ).then_inc(s["sztd"], 16)
    e.wait_ge(s["sztd"], 16)
    with e.register() as breg:
        e.reg_mov(breg, NSLOT - 1)
        bval = e.snap(breg)
        for i in range(NT):
            e.wait_ge(s["sslot"], i + 1)
            e.wait_ge(s[f"sdl{i % 2}"], 16 * (i // 2 + 1))
            if i >= 8:
                e.wait_ge(s[f"ssc{i % 2}"], 16 * ((i - 8) // 2 + 1))
            e.indirect_dma_start(
                P.dispd[:, :], IndirectOffsetOnAxis(P.slotT[:, i:i + 1], 0),
                P.ln2t[i % 2][:, :], None,
                bounds_check=bval, oob_is_err=False).then_inc(s[f"ssc{i % 2}"], 16)
    EXPSZ = 128 * 8 * DH
    with e.register() as breg1:
        e.reg_mov(breg1, 8 * 128 - 1)
        bval1 = e.snap(breg1)
        for g in range(NGRP):
            if g >= 1:
                e.wait_ge(s["sev"], 4 * g)
            if g >= 2:
                e.wait_ge(s["swl"], 32 * (g - 1))
            with e.register() as rg:
                e.reg_load(rg, P.wof[0:1, g:g + 1])
                val = e.snap(rg, min_val=0, max_val=7 * EXPSZ, donate=True)
                e.dma_start(P.w1sb[:, :], P.w1r[0:1, bass.ds(val, EXPSZ)],
                            bounds_check="skip_entire_dma").then_inc(s["swl"], 16)
                e.dma_start(P.w2sb[:, :], P.w2r[0:1, bass.ds(val, EXPSZ)],
                            bounds_check="skip_entire_dma").then_inc(s["swl"], 16)
            if g >= 2:
                e.wait_ge(s["sh"], 32 * (g - 1))
            e.indirect_dma_start(
                P.b1g[g % 2][:, :], None,
                P.b1r[:, :], IndirectOffsetOnAxis(P.iexb[:, g:g + 1], 0),
                bounds_check=bval1, oob_is_err=False).then_inc(s[f"sb1_{g % 2}"], 16)
    with e.register() as breg2:
      e.reg_mov(breg2, NSLOT - 1)
      bval2 = e.snap(breg2)
      for i in range(NT):
          e.wait_ge(s["seow0"], 16 * NGRP)
          e.wait_ge(s["seow1"], 16 * NGRP)
          if i >= 2:
                e.wait_ge(s["sout"], i - 1)
          e.indirect_dma_start(
                P.gath[i % 2][:, :], None,
                P.eod[:, :], IndirectOffsetOnAxis(P.slotT[:, i:i + 1], 0),
                bounds_check=bval2, oob_is_err=False).then_inc(s[f"sga{i % 2}"], 16)
    e.wait_ge(s["syw0"], 16 * NT // 2)
    e.wait_ge(s["syw1"], 16 * NT // 2)


def Bl_sync(P, e):
    s = P.sem
    for i in range(NT):
        if i >= 2:
            e.wait_ge(s[f"ssc{i % 2}"], 16 * ((i - 2) // 2 + 1))
        e.dma_start(out=P.ln2t[i % 2][:, :], in_=P.ln2d[i * 128:(i + 1) * 128, :]
                    ).then_inc(s[f"sdl{i % 2}"], 16)
    for g in range(NGRP):
        e.wait_ge(s["ssc0"], 16 * NT // 2)
        e.wait_ge(s["ssc1"], 16 * NT // 2)
        if g >= 2:
            e.wait_ge(s["sh"], 32 * (g - 1))
        for mc in range(8):
            e.dma_start_transpose(
                P.dspf[g % 2][:, mc * GRP:(mc + 1) * GRP],
                P.dispd[g * GRP:(g + 1) * GRP, mc * 128:(mc + 1) * 128]
                ).then_inc(s[f"sdf{g % 2}"], 16)
    for i in range(NT + 2):
        if i < NT:
            if i >= 2:
                e.wait_ge(s["sout"], i - 1)
            e.dma_start(out=P.x2t[i % 2][:, :], in_=P.x2d[i * 128:(i + 1) * 128, :]
                        ).then_inc(s[f"sx2l{i % 2}"], 16)
        j = i - 2
        if j >= 0:
            e.wait_ge(s["sout"], j + 1)
            e.dma_start(out=P.y[j * 128:(j + 1) * 128, :], in_=P.osb[j % 2][:, :]
                        ).then_inc(s[f"syw{j % 2}"], 16)


def Bl_tensor(P, e):
    s = P.sem
    e.wait_ge(s["sin"], P.sin_c)
    e.wait_ge(s["sc1"], 1)
    e.matmul(P.pmb[0:1, 0:8], P.rmsks[:, :], P.cnts[:, :], start=True, stop=True)
    e.matmul(P.pmb[32:33, 0:8], P.mhots[:, :], P.cnts[:, :], start=True, stop=True
             ).then_inc(s["sc1b"], 1)
    e.wait_ge(s["sc1"], 2)
    e.transpose(P.pmb[0:8, 8:9], P.ntgr[:, :], P.idenf[0:1, 0:1]).then_inc(s["sc1b"], 1)
    e.wait_ge(s["sc1"], 3)
    e.matmul(P.pmb[0:1, 16:24], P.ntgc[:, :], P.su8[:, :], start=True, stop=True
             ).then_inc(s["sc1b"], 1)
    e.wait_ge(s["sc1"], 4)
    e.matmul(P.pmb[:, 64:104], P.ones1f[:, :], P.Erow[:, :], start=True, stop=True)
    e.matmul(P.pmb[:, 104:112], P.ones1f[:, :], P.coffr[:, :], start=True, stop=True)
    e.matmul(P.pmb[:, 112:120], P.ones1f[:, :], P.gs256r[:, :], start=True, stop=True
             ).then_inc(s["sc1b"], 1)
    for g in range(NGRP):
        e.wait_ge(s["swl"], 32 * (g + 1))
        e.wait_ge(s[f"sdf{g % 2}"], 128 * (g // 2 + 1))
        for hc in range(32):
            ghc = 32 * g + hc
            if ghc >= 2:
                e.wait_ge(s["sho"], ghc - 1)
            for mcb in range(8):
                ins = e.matmul(P.ph[hc % 2][:, :],
                               P.w1sb[:, mcb * DH + hc * 128: mcb * DH + hc * 128 + 128],
                               P.dspf[g % 2][:, mcb * GRP:(mcb + 1) * GRP],
                               start=(mcb == 0), stop=(mcb == 7))
            ins.then_inc(s["sh"], 1)
        e.wait_ge(s["sho"], 32 * (g + 1))
        for cs in range(2):
            for hf in range(2):
                u = 4 * g + 2 * cs + hf
                if u >= 2:
                    e.wait_ge(s["sev"], u - 1)
                for hc in range(32):
                    e.matmul(P.peo[hf][:, :],
                             P.hfm[:, hc * GRP + cs * 128: hc * GRP + cs * 128 + 128],
                             P.w2sb[:, hc * 1024 + hf * 512: hc * 1024 + hf * 512 + 512],
                             start=(hc == 0), stop=False)
                ins = e.matmul(P.peo[hf][:, :],
                               P.ohgall[:, g:g + 1].to_broadcast([8, 128]),
                               P.b2sb[0:8, hf * 512:(hf + 1) * 512],
                               start=False, stop=True)
                ins.then_inc(s["seo"], 1)


def Bl_vector(P, e):
    s = P.sem
    e.wait_ge(s["sin"], P.sin_c)
    e.wait_ge(s["sioc"], 5)
    e.memset(P.zt[:, :], 0.0).then_inc(s["szt"], 1)
    e.memset(P.ones1f[:, :], 1.0)
    e.memset(P.bigc[:, :], 1048576.0)
    e.memset(P.Erow[:, :], -1.0)
    e.memset(P.swf[:, :], 1.0)
    e.tensor_scalar(P.iota40f[:, :], P.iota40[:, :], 0.0, None, ALU.add)
    e.tensor_scalar(P.iotaPf[:, :], P.iotaP[:, :], 0.0, None, ALU.add)
    e.tensor_scalar(P.iota8f[:, :], P.iota8[:, :], 0.0, None, ALU.add)
    e.tensor_scalar(P.iotaP8f[:, :], P.iotaP8[:, :], 0.0, None, ALU.add)
    e.tensor_scalar(P.idenf[:, :], P.su_i32[:, :], 0.0, None, ALU.is_equal)
    vm(P, e.tensor_scalar(P.su8[:, :], P.su_i32[0:8, 0:8], 0.0, None, ALU.is_gt))
    vw(P, e)
    e.engine_nop().then_inc(s["sc1"], 1)
    e.wait_ge(s["sc1b"], 1)
    e.tensor_scalar(P.coffr[:, :], P.pmb[0:1, 0:8], 0.0, None, ALU.add)
    vm(P, e.tensor_scalar(P.mycr[:, :], P.pmb[32:33, 0:8], 0.0, None, ALU.add))
    vw(P, e)
    vm(P, e.tensor_scalar(P.remr[:, :], P.coffr[:, :], -1.0, 8192.0, ALU.mult, ALU.add))
    vw(P, e)
    vm(P, e.tensor_scalar(P.kptr[:, :], P.remr[:, :], 0.0, None, ALU.max))
    vw(P, e)
    vm(P, e.tensor_tensor(P.t8a[:, :], P.kptr[:, :], P.mycr[:, :], ALU.min))
    vw(P, e)
    vm(P, e.tensor_scalar(P.t8i[:, :], P.t8a[:, :], 0, None, ALU.add))
    vw(P, e)
    vm(P, e.tensor_scalar(P.t8i2[:, :], P.t8i[:, :], 255, None, ALU.add))
    vw(P, e)
    vm(P, e.tensor_scalar(P.ntgi[:, :], P.t8i2[:, :], 8, None, ALU.arith_shift_right))
    vw(P, e)
    vm(P, e.tensor_scalar(P.ntgr[:, :], P.ntgi[:, :], 0.0, None, ALU.add))
    vw(P, e)
    e.engine_nop().then_inc(s["sc1"], 1)
    e.wait_ge(s["sc1b"], 2)
    vm(P, e.tensor_scalar(P.ntgc[:, :], P.pmb[0:8, 8:9], 0.0, None, ALU.add))
    vw(P, e)
    e.engine_nop().then_inc(s["sc1"], 1)
    e.wait_ge(s["sc1b"], 3)
    vm(P, e.tensor_scalar(P.gsr[:, :], P.pmb[0:1, 16:24], 0.0, None, ALU.add))
    vw(P, e)
    vm(P, e.tensor_scalar(P.gs256r[:, :], P.gsr[:, :], 256.0, None, ALU.mult))
    vw(P, e)
    for ee in range(8):
        vm(P, e.tensor_scalar(P.t40a[:, :], P.iota40f[:, :], P.gsr[0:1, ee:ee + 1], None, ALU.is_ge))
        vw(P, e)
        vm(P, e.tensor_tensor(P.t40b[:, :], P.Erow[:, :], P.t40a[:, :], ALU.add))
        vw(P, e)
        vm(P, e.tensor_scalar(P.Erow[:, :], P.t40b[:, :], 0.0, None, ALU.add))
        vw(P, e)
    vm(P, e.tensor_tensor(P.swf[0:1, 1:NGRP], P.Erow[0:1, 1:NGRP], P.Erow[0:1, 0:NGRP - 1],
                          ALU.not_equal))
    vw(P, e)
    vm(P, e.tensor_scalar(P.swsent[:, :], P.swf[:, :], float(-SENT), float(SENT),
                          ALU.mult, ALU.add))
    vw(P, e)
    vm(P, e.tensor_scalar(P.t40a[:, :], P.Erow[:, :], float(128 * 8 * DH), None, ALU.mult))
    vw(P, e)
    vm(P, e.tensor_tensor(P.wof_f[:, :], P.t40a[:, :], P.swsent[:, :], ALU.add))
    vw(P, e)
    vm(P, e.tensor_scalar(P.wof[:, :], P.wof_f[:, :], 0.0, None, ALU.add))
    vw(P, e)

    e.engine_nop().then_inc(s["sc1"], 1)
    e.wait_ge(s["sc1b"], 4)
    vm(P, e.tensor_scalar(P.iexf[:, :], P.pmb[:, 64:104], 128.0, P.iotaPf[:, 0:1],
                          ALU.mult, ALU.add))
    vw(P, e)
    e.tensor_scalar(P.iexb[:, :], P.iexf[:, :], 0.0, None, ALU.add)

    vm(P, e.tensor_scalar(P.ohgf[:, :], P.pmb[0:8, 64:104], 0.0, None, ALU.add))
    vw(P, e)
    e.tensor_tensor(P.ohgall[:, :], P.iotaP8f[:, :], P.ohgf[:, :], ALU.is_equal)
    e.tensor_scalar(P.coffb[:, :], P.pmb[:, 104:112], 0.0, None, ALU.add)
    vm(P, e.tensor_scalar(P.gs256b[:, :], P.pmb[:, 112:120], 0.0, None, ALU.add))
    vw(P, e)
    for i in range(NT):
        vm(P, e.tensor_scalar(P.ohD[:, :], P.iota8f[:, :], P.eixs[:, i:i + 1], None, ALU.is_equal))
        vw(P, e)
        e.tensor_tensor(P.cselc[:, :], P.ohD[:, :], P.coffb[:, :], ALU.mult)
        vm(P, e.tensor_tensor(P.gselc[:, :], P.ohD[:, :], P.gs256b[:, :], ALU.mult))
        vw(P, e)
        e.reduce_sum(P.csel[:, :], P.cselc[:, :], AX.X)
        vm(P, e.reduce_sum(P.gsel[:, :], P.gselc[:, :], AX.X))
        vw(P, e)
        e.tensor_tensor(P.globc[:, :], P.rnks[:, i:i + 1], P.csel[:, :], ALU.add)
        vm(P, e.tensor_tensor(P.sl0[:, :], P.rnks[:, i:i + 1], P.gsel[:, :], ALU.add))
        vw(P, e)
        vm(P, e.tensor_scalar(P.keepc[:, :], P.globc[:, :], 8192.0, None, ALU.is_lt))
        vw(P, e)
        e.tensor_scalar(P.keepi[:, :], P.globc[:, :], 8192.0, None, ALU.is_lt)
        vm(P, e.tensor_tensor(P.gvmT[:, i:i + 1], P.gvs[:, i:i + 1], P.keepc[:, :], ALU.mult))
        vw(P, e)
        vm(P, e.select(P.slotF[:, i:i + 1], P.keepi[:, :], P.sl0[:, :], P.bigc[:, :], add_drain=True))
        vw(P, e)
        e.tensor_scalar(P.slotT[:, i:i + 1], P.slotF[:, i:i + 1], 0.0, None,
                        ALU.add).then_inc(s["sslot"], 1)
    for g in range(NGRP):
        e.wait_ge(s[f"sb1_{g % 2}"], 16 * (g // 2 + 1))
        for hc in range(32):
            e.wait_ge(s["sh"], 32 * g + hc + 1)
            e.tensor_scalar(P.hfm[:, hc * GRP:(hc + 1) * GRP], P.ph[hc % 2][:, :],
                            P.b1g[g % 2][:, hc:hc + 1], 0.0, ALU.add, ALU.max
                            ).then_inc(s["sho"], 1)
        for cs in range(2):
            if g >= 1:
                e.wait_ge(s[f"seow{cs}"], 16 * g)
            for hf in range(2):
                u = 4 * g + 2 * cs + hf
                e.wait_ge(s["seo"], u + 1)
                e.tensor_scalar(P.eosb[cs][:, hf * 512:(hf + 1) * 512], P.peo[hf][:, :],
                                0.0, None, ALU.add).then_inc(s["sev"], 1)
    for i in range(NT):
        if i < 2:
            e.wait_ge(s["szg"], 1)
        e.wait_ge(s[f"sga{i % 2}"], 16 * (i // 2 + 1))
        e.wait_ge(s[f"sx2l{i % 2}"], 16 * (i // 2 + 1))
        if i >= 2:
            e.wait_ge(s[f"syw{i % 2}"], 16 * (i // 2))
        e.scalar_tensor_tensor(P.osb[i % 2][:, :], P.gath[i % 2][:, :], P.gvmT[:, i:i + 1],
                               P.x2t[i % 2][:, :], ALU.mult, ALU.add).then_inc(s["sout"], 1)


def Bl_scalar(P, e):
    s = P.sem
    for g in range(NGRP):
        for cs in range(2):
            e.wait_ge(s["sev"], 4 * g + 2 * (cs + 1))
            e.dma_start(out=P.eod[g * GRP + cs * 128: g * GRP + (cs + 1) * 128, :],
                        in_=P.eosb[cs][:, :]).then_inc(s[f"seow{cs}"], 16)


def emit_B(P):
    nc = P.nc
    with P.es:
        block = P.es.enter_context(nc.Block())

        @block.sync
        def _(e):
            Bl_sync(P, e)

        @block.gpsimd
        def _(e):
            Bl_gpsimd(P, e)

        @block.vector
        def _(e):
            Bl_vector(P, e)

        @block.scalar
        def _(e):
            Bl_scalar(P, e)

        @block.tensor
        def _(e):
            Bl_tensor(P, e)
    return nc


def host_prep_B(inp, a_results):
    w1 = np.asarray(inp["L0_ffn_w1"])
    w2 = np.asarray(inp["L0_ffn_w2"])
    b1 = np.asarray(inp["L0_ffn_b1"])
    b2 = np.asarray(inp["L0_ffn_b2"])
    w1r = np.ascontiguousarray(w1.reshape(8, 8, 128, DH).transpose(0, 2, 1, 3)).reshape(1, -1)
    w2r = np.ascontiguousarray(w2.reshape(8, 32, 128, 1024).transpose(0, 2, 1, 3)).reshape(1, -1)
    b1r = np.ascontiguousarray(b1.reshape(8, 32, 128).transpose(0, 2, 1)).reshape(8 * 128, 32)
    cnt8 = np.concatenate([np.asarray(r["o_cnt"]) for r in a_results], 0)
    base = {"w1r": w1r, "w2r": w2r, "b1r": b1r, "b2d": b2, "cnt8": cnt8}
    maps = []
    for c in range(8):
        m = dict(base)
        m["rmsk"] = (np.arange(8) < c).astype(np.float32).reshape(8, 1)
        m["mhot"] = (np.arange(8) == c).astype(np.float32).reshape(8, 1)
        m["x2d"] = np.asarray(a_results[c]["o_x2"])
        m["ln2d"] = np.asarray(a_results[c]["o_ln2"])
        m["rnkd"] = np.asarray(a_results[c]["o_rank"])
        m["eixd"] = np.asarray(a_results[c]["o_eidx"])
        m["gvd"] = np.asarray(a_results[c]["o_gv"])
        maps.append(m)
    return maps


def kernel(**inputs):
    PA = build_A()
    ncA = emit_A(PA)
    mapsA = host_prep_A(inputs)
    resA = run_bass_kernel_spmd(ncA, mapsA, core_ids=list(range(8)))
    a_results = [{k: np.asarray(v) for k, v in resA.results[c].items()} for c in range(8)]

    PB = build_B()
    ncB = emit_B(PB)
    mapsB = host_prep_B(inputs, a_results)
    resB = run_bass_kernel_spmd(ncB, mapsB, core_ids=list(range(8)))
    y = np.stack([np.asarray(resB.results[c]["y"], dtype=np.float32) for c in range(8)])
    return y


# version 02
# speedup vs baseline: 1.3461x; 1.3461x over previous
import numpy as np
import ml_dtypes
from contextlib import ExitStack
import concourse.bass as bass
import concourse.mybir as mybir
from concourse.bass import IndirectOffsetOnAxis
from concourse.bass_utils import run_bass_kernel_spmd

F32 = mybir.dt.float32
BF16 = mybir.dt.bfloat16
I32 = mybir.dt.int32
AF = mybir.ActivationFunctionType
ALU = mybir.AluOpType
AX = mybir.AxisListType
bfnp = ml_dtypes.bfloat16

S, D, NT, NST = 8192, 1024, 64, 16
ALVL = 99
H, HD, NE, DH = 16, 64, 8, 4096
EPS = 1e-5


class P_:
    pass


def build_A():
    nc = bass.Bass("TRN2", target_bir_lowering=False)
    P = P_()
    P.nc = nc
    P.x = nc.dram_tensor("x", [S, D], F32, kind="ExternalInput").ap()
    P.wqt = nc.dram_tensor("wqt", [D, D], F32, kind="ExternalInput").ap()
    P.wkt = nc.dram_tensor("wkt", [D, D], F32, kind="ExternalInput").ap()
    P.wvt = nc.dram_tensor("wvt", [D, D], F32, kind="ExternalInput").ap()
    P.wot = nc.dram_tensor("wot", [D, D], F32, kind="ExternalInput").ap()
    P.bk_d = nc.dram_tensor("bk", [1, D], F32, kind="ExternalInput").ap()
    P.bv_d = nc.dram_tensor("bv", [1, D], F32, kind="ExternalInput").ap()
    P.bo_d = nc.dram_tensor("bo", [1, D], F32, kind="ExternalInput").ap()
    P.bqc_d = nc.dram_tensor("bqc", [128, 8], F32, kind="ExternalInput").ap()
    P.wg_d = nc.dram_tensor("wg", [D, NE], F32, kind="ExternalInput").ap()
    P.o_x2 = nc.dram_tensor("o_x2", [S, D], F32, kind="ExternalOutput").ap()
    P.o_ln2 = nc.dram_tensor("o_ln2", [S, D], BF16, kind="ExternalOutput").ap()
    P.o_cnt = nc.dram_tensor("o_cnt", [1, NE], F32, kind="ExternalOutput").ap()
    P.o_rank = nc.dram_tensor("o_rank", [128, NT], F32, kind="ExternalOutput").ap()
    P.o_eidx = nc.dram_tensor("o_eidx", [128, NT], F32, kind="ExternalOutput").ap()
    P.o_gv = nc.dram_tensor("o_gv", [128, NT], F32, kind="ExternalOutput").ap()
    P.w1r = nc.dram_tensor("w1r", [1, 8 * 128 * 8 * DH], F32, kind="ExternalInput").ap()
    P.w2r = nc.dram_tensor("w2r", [1, 8 * 128 * 32 * 1024], F32, kind="ExternalInput").ap()
    P.w1c = nc.dram_tensor("w1c", [1, 8 * 128 * 8 * DH], BF16, kind="ExternalOutput").ap()
    P.w2c = nc.dram_tensor("w2c", [1, 8 * 128 * 32 * 1024], BF16, kind="ExternalOutput").ap()
    P.ln1st = nc.dram_tensor("ln1stash", [S, D], BF16).ap()
    P.z_dram = nc.dram_tensor("z_dram", [H, 512], F32).ap()

    es = ExitStack()
    P.es = es
    sb = lambda name, shape, dt: es.enter_context(nc.sbuf_tensor(name, shape, dt))
    ps = lambda name, shape, dt: es.enter_context(nc.psum_tensor(name, shape, dt))
    P.wk_sb = sb("wk_sb", [128, 8 * D], BF16)
    P.wv_sb = sb("wv_sb", [128, 8 * D], BF16)
    P.wq_sb = sb("wq_sb", [128, 8 * D], BF16)
    P.wo_sb = sb("wo_sb", [128, 8 * D], BF16)
    P.xa = [sb(f"xa{b}", [128, D], F32) for b in range(2)]
    P.ln1tm = [sb(f"ln1tm{b}", [128, D], BF16) for b in range(2)]
    P.ln1f = [sb(f"ln1f{b}", [128, D], BF16) for b in range(2)]
    P.kt = [sb(f"kt{b}", [128, D], BF16) for b in range(2)]
    P.vt = [sb(f"vt{b}", [128, D], BF16) for b in range(2)]
    P.kvt = sb("kvt", [128, 8 * HD], BF16)
    P.kvacc = [sb(f"kvacc{b}", [128, 512], F32) for b in range(2)]
    P.ksacc = [sb(f"ksacc{b}", [33, 512], F32) for b in range(2)]
    P.ksc = sb("ksc", [128, 8], F32)
    P.KS = sb("KS", [128, 8 * H], BF16)
    P.st6 = [sb(f"st6_{b}", [128, 12], F32) for b in range(2)]
    P.st2 = [sb(f"st2_{b}", [128, 2], F32) for b in range(2)]
    P.vep = [sb(f"vep{b}", [128, 1], F32) for b in range(2)]
    P.lnv = [sb(f"lnv{b}", [128, 1], F32) for b in range(2)]
    P.rstd = [sb(f"rstd{b}", [128, 1], F32) for b in range(2)]
    P.ucl = [sb(f"ucl{b}", [128, 512], F32) for b in range(2)]
    P.wcl = [sb(f"wcl{b}", [128, 512], BF16) for b in range(2)]
    P.tcl = [sb(f"tcl{b}", [128, 512], BF16) for b in range(2)]
    P.bkr = sb("bkr", [1, D], BF16)
    P.bvr = sb("bvr", [1, D], BF16)
    P.bor = sb("bor", [1, D], BF16)
    P.bqcs = sb("bqcs", [128, 8], F32)
    P.nbqc = sb("nbqc", [128, 8], F32)
    P.wg_sb = sb("wg_sb", [128, 8 * NE], BF16)
    P.ones1 = sb("ones1", [1, 128], BF16)
    P.ones1f = sb("ones1f", [1, 128], F32)
    P.onesc = sb("onesc", [128, 1], BF16)
    P.iden = sb("iden", [128, 128], BF16)
    P.idenf = sb("idenf", [128, 128], F32)
    P.SU = sb("SU", [128, 128], BF16)
    P.su_i32 = sb("su_i32", [128, 128], I32)
    P.iotaP = sb("iotaP", [128, 1], I32)
    P.iota8 = sb("iota8", [128, 8], I32)
    P.iota16 = sb("iota16", [128, 16], I32)
    P.iota8f = sb("iota8f", [128, 8], F32)
    P.iota8B = sb("iota8B", [128, 8], F32)
    P.iota16f = sb("iota16f", [128, 16], F32)
    P.hpc = sb("hpc", [128, 1], F32)
    P.hb8 = sb("hb8", [128, 8], F32)
    P.msk16 = sb("msk16", [128, 128], F32)
    P.spacer = sb("spacer", [128, 8], F32)
    P.ln1fB = [sb(f"ln1fB{b}", [128, 8 * 512], BF16) for b in range(2)]
    P.q0 = sb("q0", [128, 8 * 512], BF16)
    P.atf = sb("atf", [128, 8 * 512], BF16)
    P.zex = sb("zex", [128, 8 * 512], BF16)
    P.zsb = sb("zsb", [H, 512], F32)
    P.zl = sb("zl", [H, 512], F32)
    P.zr = sb("zr", [H, 512], F32)
    P.xb = [sb(f"xb{b}", [128, D], F32) for b in range(2)]
    P.x2sb = [sb(f"x2sb{b}", [128, D], F32) for b in range(2)]
    P.ln2tm = [sb(f"ln2tm{b}", [128, D], BF16) for b in range(2)]
    P.ln2f = [sb(f"ln2f{b}", [128, 8 * 512], BF16) for b in range(2)]
    P.lg = sb("lg", [128, 32], F32)
    P.mx8 = sb("mx8", [128, 4], F32)
    P.sme = sb("sme", [128, 32], F32)
    P.smx = sb("smx", [128, 32], F32)
    P.ssum = sb("ssum", [128, 4], F32)
    P.eqm = sb("eqm", [128, 32], F32)
    P.cnd = sb("cnd", [128, 32], F32)
    P.ohb = sb("ohb", [128, 64], BF16)
    P.rl = sb("rl", [128, 32], F32)
    P.runrow_f = sb("runrow_f", [1, NE], F32)
    P.rnkT = sb("rnkT", [128, NT], F32)
    P.eixT = sb("eixT", [128, NT], F32)
    P.gvT = sb("gvT", [128, NT], F32)
    P.tot_sb = sb("tot_sb", [1, NE], F32)
    P.pk = [ps(f"pk{b}", [128, 512], F32) for b in range(2)]
    P.pv = [ps(f"pv{b}", [128, 512], F32) for b in range(2)]
    P.pkv = ps("pkv", [128, 512], F32)
    P.pmisc = ps("pmisc", [128, 512], F32)
    P.pkv2 = ps("pkv2", [128, 512], F32)
    P.ptr = ps("ptr", [128, 1024], BF16)
    P.sem = {}
    for n in ["sxa0", "sxa1", "svst", "srst", "sapp", "sstw0", "sstw1", "strn", "stev", "skp",
              "suw", "selu", "skv", "swt", "sa2",
              "slnf0", "slnf1", "slnc0", "slnc1", "sqp", "sqe", "szp", "szs", "szr", "szw", "szx",
              "sat", "sae", "sop", "sxb0", "sxb1", "sx2h", "sx2w0", "sx2w1", "sln2a",
              "sln2w0", "sln2w1",
              "sct", "sgd", "sgx", "soh", "srk", "srun", "ssel", "slgc", "sfin", "swtc", "svo", "sso", "skva", "smn", "swc"]:
        P.sem[n] = es.enter_context(nc.semaphore(n))
    P.vn = 0
    P.sn = 0
    return P


def vm(P, ins):
    P.vn += 1
    ins.then_inc(P.sem["svo"], 1)


def vw(P, e):
    e.wait_ge(P.sem["svo"], P.vn)


def sm(P, ins):
    P.sn += 1
    ins.then_inc(P.sem["sso"], 1)


def sw(P, e):
    e.wait_ge(P.sem["sso"], P.sn)


def A_sync(P, e):
    s = P.sem
    for i in range(NT):
        if i >= 2:
            e.wait_ge(s["sapp" if ALVL >= 3 else "svst"], i - 1)
        e.dma_start(out=P.xa[i % 2][:, :], in_=P.x[i * 128:(i + 1) * 128, :]).then_inc(s[f"sxa{i % 2}"], 16)


def A_gpsimd_consts(P, e):
    s = P.sem["swt"]
    n = 0
    def dma(dst, src):
        nonlocal n
        e.dma_start(out=dst, in_=src).then_inc(s, 16)
        n += 16
    dma(P.wk_sb[:, :], P.wkt.rearrange("(jc p) i -> p jc i", p=128))
    dma(P.wv_sb[:, :], P.wvt.rearrange("(jc p) i -> p jc i", p=128))
    dma(P.bkr[:, :], P.bk_d[:, :])
    dma(P.bvr[:, :], P.bv_d[:, :])
    dma(P.bor[:, :], P.bo_d[:, :])
    dma(P.bqcs[:, :], P.bqc_d[:, :])
    dma(P.wg_sb[:, :], P.wg_d.rearrange("(jc p) e -> p jc e", p=128))
    sc = P.sem["swtc"]
    e.iota(P.iotaP[:, :], [[1, 1]], base=0, channel_multiplier=1).then_inc(sc, 1)
    e.iota(P.iota8[:, :], [[1, 8]], base=0, channel_multiplier=0).then_inc(sc, 1)
    e.iota(P.iota16[:, :], [[1, 16]], base=0, channel_multiplier=0).then_inc(sc, 1)
    e.iota(P.su_i32[:, :], [[1, 128]], base=0, channel_multiplier=-1).then_inc(sc, 1)
    CH = 1 << 22
    for c8 in range(8):
        for t, (dst, srcw) in enumerate(((P.w1c, P.w1r), (P.w2c, P.w2r))):
            e.dma_start(
                out=dst[0:1, c8 * CH:(c8 + 1) * CH].rearrange("o (r f) -> (o r) f", f=8192),
                in_=srcw[0:1, c8 * CH:(c8 + 1) * CH].rearrange("o (r f) -> (o r) f", f=8192),
            ).then_inc(P.sem["swc"], 16)
    P.swt_pre = n
    dma(P.wq_sb[:, :], P.wqt.rearrange("(jc p) i -> p jc i", p=128))
    dma(P.wo_sb[:, :], P.wot.rearrange("(jc p) i -> p jc i", p=128))
    P.swt_c = n


def A_vec_consts(P, e):
    s = P.sem
    e.wait_ge(s["swt"], P.swt_c)
    e.wait_ge(s["swtc"], 4)
    e.memset(P.ones1[:, :], 1.0)
    e.memset(P.ones1f[:, :], 1.0)
    e.memset(P.onesc[:, :], 1.0)
    e.memset(P.runrow_f[:, :], 0.0)
    e.tensor_scalar(P.SU[:, :], P.su_i32[:, :], 0.0, None, ALU.is_gt)
    e.tensor_scalar(P.iden[:, :], P.su_i32[:, :], 0.0, None, ALU.is_equal).then_inc(s["swtc"], 1)
    e.tensor_scalar(P.iota8f[:, :], P.iota8[:, :], 0.0, None, ALU.add)
    e.tensor_scalar(P.iota16f[:, :], P.iota16[:, :], 0.0, None, ALU.add)
    e.tensor_scalar(P.iota8B[:, :], P.iota8[:, :], 16384.0, None, ALU.add)
    e.tensor_scalar(P.hpc[:, :], P.iotaP[:, :], 63.5, None, ALU.is_gt)
    e.wait_ge(s["swtc"], 5)
    e.tensor_scalar(P.idenf[:, :], P.iden[:, :], 0.0, None, ALU.add)
    e.memset(P.pmisc[0:33, :], 0.0)
    e.tensor_scalar(P.nbqc[:, :], P.bqcs[:, :], -1.0, None, ALU.mult).then_inc(s["swtc"], 1)


def A_vector(P, e):
    s = P.sem
    e.wait_ge(s["swtc"], 6)
    for i in range(NT + 4):
        j = i - 2
        a = i - 1
        if i < NT:
            b = i % 2
            e.wait_ge(s[f"sxa{i % 2}"], 16 * (i // 2 + 1))
            e.bn_stats(P.st6[b][:, 0:6], P.xa[b][:, 0:512])
            vm(P, e.bn_stats(P.st6[b][:, 6:12], P.xa[b][:, 512:1024]))
        if 0 <= j < NT and ALVL >= 4:
            bj = j % 2
            e.wait_ge(s["skp"], j + 1)
            if j >= 1:
                e.wait_ge(s["skv"] if ALVL >= 50 else s["skp"], j if ALVL >= 50 else j + 1)
                e.wait_ge(s["selu"], j)
            e.tensor_scalar(P.ucl[0][:, :], P.pk[0][:, :], 0.0, None, ALU.min)
            e.tensor_scalar(P.ucl[1][:, :], P.pk[1][:, :], 0.0, None, ALU.min).then_inc(s["smn"], 1)
            e.tensor_scalar(P.tcl[0][:, :], P.pk[0][:, :], 0.0, None, ALU.max)
            vm(P, e.tensor_scalar(P.tcl[1][:, :], P.pk[1][:, :], 0.0, None, ALU.max))
            e.tensor_copy(P.vt[bj][:, 0:512], P.pv[0][:, :])
            e.tensor_copy(P.vt[bj][:, 512:1024], P.pv[1][:, :])
        if i < NT:
            b = i % 2
            vw(P, e)
            vm(P, e.bn_aggr(P.st2[b][:, :], P.st6[b][:, :]))
        if 0 <= j < NT and ALVL >= 4:
            bj = j % 2
            if ALVL >= 40:
                e.wait_ge(s["suw"], j + 1)
            vw(P, e)
            if ALVL >= 40:
                e.tensor_tensor(P.kt[bj][:, 0:512], P.tcl[0][:, :], P.wcl[0][:, :], ALU.add)
                e.tensor_tensor(P.kt[bj][:, 512:1024], P.tcl[1][:, :], P.wcl[1][:, :],
                                ALU.add).then_inc(s["selu"], 1)
            elif ALVL >= 38:
                e.tensor_scalar(P.kt[bj][:, 0:512], P.tcl[0][:, :], 0.0, None, ALU.add)
                e.tensor_scalar(P.kt[bj][:, 512:1024], P.tcl[1][:, :], 0.0, None, ALU.add).then_inc(s["selu"], 1)
            elif ALVL >= 37:
                e.tensor_tensor(P.kt[bj][:, 0:512], P.tcl[0][:, :], P.tcl[1][:, :], ALU.add)
                e.tensor_tensor(P.kt[bj][:, 512:1024], P.tcl[1][:, :], P.tcl[0][:, :],
                                ALU.add).then_inc(s["selu"], 1)
            elif ALVL >= 36:
                e.tensor_scalar(P.kt[bj][:, 0:512], P.tcl[0][:, :], 0.0, None, ALU.add)
                e.tensor_scalar(P.kt[bj][:, 512:1024], P.tcl[1][:, :], 0.0, None, ALU.add).then_inc(s["selu"], 1)
            elif ALVL >= 35:
                e.tensor_copy(P.kt[bj][:, 0:512], P.tcl[0][:, :])
                e.tensor_copy(P.kt[bj][:, 512:1024], P.tcl[1][:, :]).then_inc(s["selu"], 1)
            else:
                e.engine_nop().then_inc(s["selu"], 1)
        jk = i - 3
        if 0 <= jk < NT and ALVL >= 50:
            e.wait_ge(s["skv"], jk + 1)
            if jk >= 1:
                e.wait_ge(s["skva"], jk)
            if jk == 0:
                e.tensor_copy(P.kvacc[0][0:64, :], P.pkv[0:64, :])
                e.tensor_copy(P.kvacc[0][64:128, :], P.pkv2[0:64, :])
                ins = e.tensor_copy(P.ksacc[0][:, :], P.pmisc[0:33, :])
            else:
                e.tensor_tensor(P.kvacc[jk % 2][0:64, :], P.pkv[0:64, :],
                                P.kvacc[(jk - 1) % 2][0:64, :], ALU.add)
                e.tensor_tensor(P.kvacc[jk % 2][64:128, :], P.pkv2[0:64, :],
                                P.kvacc[(jk - 1) % 2][64:128, :], ALU.add)
                ins = e.tensor_tensor(P.ksacc[jk % 2][:, :], P.pmisc[0:33, :],
                                      P.ksacc[(jk - 1) % 2][:, :], ALU.add)
            ins.then_inc(s["skva"], 1)
        if i < NT:
            b = i % 2
            vw(P, e)
            e.tensor_scalar(P.vep[b][:, :], P.st2[b][:, 1:2], EPS, None, ALU.add).then_inc(s["svst"], 1)
        if 0 <= a < NT and ALVL >= 3:
            ba = a % 2
            e.wait_ge(s["srst"], a + 1)
            if a >= 2:
                e.wait_ge(s["strn"], a - 1)
                e.wait_ge(s[f"sstw{a % 2}"], 16 * (a // 2))
            e.tensor_scalar(P.ln1tm[ba][:, :], P.xa[ba][:, :], P.st2[ba][:, 0:1],
                            P.rstd[ba][:, 0:1], ALU.subtract, ALU.mult).then_inc(s["sapp"], 1)
        if 0 <= a < NT and ALVL >= 3:
            e.wait_ge(s["strn"], a + 1)
            if a >= 2 and ALVL >= 4:
                e.wait_ge(s["skp"], a - 1)
            e.tensor_copy(P.ln1f[a % 2][:, :], P.ptr[:, :]).then_inc(s["stev"], 1)


def A_scalar(P, e):
    s = P.sem
    for i in range(NT + 1):
        j = i - 1
        if i < NT:
            b = i % 2
            e.wait_ge(s["svst"], i + 1)
            sm(P, e.activation(P.lnv[b][:, :], P.vep[b][:, :], AF.Ln))
        if 0 <= j < NT and ALVL >= 38:
            e.wait_ge(s["smn"], j + 1)
            e.activation(P.wcl[0][:, :], P.ucl[0][:, :], AF.Exp)
            e.activation(P.wcl[1][:, :], P.ucl[1][:, :], AF.Exp).then_inc(s["suw"], 1)
        if i < NT:
            b = i % 2
            sw(P, e)
            e.activation(P.rstd[b][:, :], P.lnv[b][:, :], AF.Exp, scale=-0.5).then_inc(s["srst"], 1)
        if 0 <= j < NT and ALVL >= 3:
            e.wait_ge(s["sapp"], j + 1)
            e.dma_start(out=P.ln1st[j * 128:(j + 1) * 128, :],
                        in_=P.ln1tm[j % 2][:, :]).then_inc(s[f"sstw{j % 2}"], 16)


def A_tensor(P, e):
    s = P.sem
    e.wait_ge(s["swt"], P.swt_c)
    e.wait_ge(s["swtc"], 6)
    for i in range(NT + 3):
        a = i - 1
        j = i - 2
        j2 = i - 3
        if 0 <= j < NT and ALVL >= 4:
            bj = j % 2
            e.wait_ge(s["stev"], j + 1)
            if j >= 1:
                e.wait_ge(s["selu"], j)
            for hlf in range(2):
                for jc in range(8):
                    e.matmul(P.pk[hlf][:, :], P.ln1f[bj][:, jc * 128:(jc + 1) * 128],
                             P.wk_sb[:, jc * D + hlf * 512: jc * D + hlf * 512 + 512],
                             start=(jc == 0), stop=False)
                e.matmul(P.pk[hlf][:, :], P.ones1[:, :],
                         P.bkr[0:1, hlf * 512:(hlf + 1) * 512], start=False, stop=True)
            for hlf in range(2):
                for jc in range(8):
                    e.matmul(P.pv[hlf][:, :], P.ln1f[bj][:, jc * 128:(jc + 1) * 128],
                             P.wv_sb[:, jc * D + hlf * 512: jc * D + hlf * 512 + 512],
                             start=(jc == 0), stop=False)
                ins = e.matmul(P.pv[hlf][:, :], P.ones1[:, :],
                               P.bvr[0:1, hlf * 512:(hlf + 1) * 512], start=False, stop=True)
            ins.then_inc(s["skp"], 1)
        if 0 <= j2 < NT and ALVL >= 50:
            bj = j2 % 2
            e.wait_ge(s["selu"], j2 + 1)
            if j2 >= 1:
                e.wait_ge(s["skva"], j2)
            for h in range(H):
                pr = h // 2
                bank = P.pkv if h % 2 == 0 else P.pkv2
                e.matmul(bank[0:64, pr * 64:(pr + 1) * 64],
                         P.kt[bj][:, h * 64:(h + 1) * 64], P.vt[bj][:, h * 64:(h + 1) * 64],
                         start=(h < 2), stop=(h >= H - 2))
            for hlf in range(2):
                ins = e.matmul(P.pmisc[hlf * 32:hlf * 32 + 1, :], P.onesc[:, :],
                               P.kt[bj][:, hlf * 512:(hlf + 1) * 512],
                               start=True, stop=True)
            ins.then_inc(s["skv"], 1)
        if 0 <= a < NT and ALVL >= 3:
            e.wait_ge(s["sapp"], a + 1)
            if a >= 1:
                e.wait_ge(s["stev"], a)
            for jc in range(8):
                ins = e.transpose(P.ptr[:, jc * 128:(jc + 1) * 128],
                                  P.ln1tm[a % 2][:, jc * 128:(jc + 1) * 128], P.iden[:, :])
            ins.then_inc(s["strn"], 1)


def A2_vector(P, e):
    if ALVL < 50:
        return
    s = P.sem
    e.wait_ge(s["skva"], NT)
    vw(P, e)
    e.tensor_scalar(P.kvt[:, :], P.kvacc[(NT - 1) % 2][:, :], 0.0, None, ALU.add).then_inc(s["sa2"], 1)
    e.wait_ge(s["sa2"], 3)
    e.tensor_copy(P.ksc[:, :], P.pmisc[:, 64:72])
    vm(P, e.tensor_scalar(P.hb8[:, :], P.iota8f[:, :], 2.0, P.hpc[:, 0:1], ALU.mult, ALU.add))
    vw(P, e)
    for c in range(8):
        ins = e.tensor_scalar(P.msk16[:, c * 16:(c + 1) * 16], P.iota16f[:, :],
                              P.hb8[:, c:c + 1], None, ALU.is_equal)
    vm(P, ins)
    vw(P, e)
    for c in range(8):
        ins = e.tensor_scalar(P.KS[:, c * 16:(c + 1) * 16], P.msk16[:, c * 16:(c + 1) * 16],
                              P.ksc[:, c:c + 1], None, ALU.mult)
    ins.then_inc(s["sa2"], 1)


def A2_tensor(P, e):
    if ALVL < 50:
        return
    s = P.sem
    e.wait_ge(s["sa2"], 1)
    e.wait_ge(s["skva"], NT)
    ka = P.ksacc[(NT - 1) % 2]
    for jc in range(8):
        if jc < 4:
            srcrow, idn = ka[0:1, jc * 128:(jc + 1) * 128], P.idenf[0:1, 0:1]
        else:
            srcrow, idn = ka[32:33, (jc - 4) * 128:(jc - 3) * 128], P.idenf[32:33, 32:33]
        ins = e.transpose(P.pmisc[:, 64 + jc:65 + jc], srcrow, idn)
    ins.then_inc(s["sa2"], 2)


def B_sync(P, e):
    s = P.sem
    for st in range(NST):
        e.wait_ge(s["sstw0"], 16 * (2 * st + 2))
        e.wait_ge(s["sstw1"], 16 * (2 * st + 2))
        if st >= 2:
            e.wait_ge(s["sqp"], 8 * (st - 1))
        for jc in range(8):
            e.dma_start_transpose(P.ln1fB[st % 2][:, jc * 512:(jc + 1) * 512],
                                  P.ln1st[st * 512:(st + 1) * 512, jc * 128:(jc + 1) * 128]
                                  ).then_inc(s[f"slnf{st % 2}"], 16)
        e.wait_ge(s["szr"], st + 1)
        if st >= 1:
            e.wait_ge(s["szx"], 32 * st)
        e.dma_start(out=P.z_dram[:, :], in_=P.zr[:, :]).then_inc(s["szw"], 16)
        for t in range(4):
            i = st * 4 + t
            if i >= 2:
                e.wait_ge(s["sx2h"], 2 * (i - 1))
            e.dma_start(out=P.xb[i % 2][:, :], in_=P.x[i * 128:(i + 1) * 128, :]).then_inc(s[f"sxb{i % 2}"], 16)


def B_gpsimd(P, e):
    s = P.sem
    for st in range(NST):
        e.wait_ge(s["szw"], 16 * (st + 1))
        if st >= 1:
            e.wait_ge(s["sae"], 8 * st)
        dstv = P.zex[:, :].rearrange("p (c t) -> p c t", t=512)
        src_ev = P.z_dram[0::2, :].unsqueeze(0).to_broadcast([64, 8, 512])
        src_od = P.z_dram[1::2, :].unsqueeze(0).to_broadcast([64, 8, 512])
        e.dma_start(out=dstv[0:64, :, :], in_=src_ev).then_inc(s["szx"], 16)
        e.dma_start(out=dstv[64:128, :, :], in_=src_od).then_inc(s["szx"], 16)


def B_tensor(P, e):
    s = P.sem
    e.wait_ge(s["swt"], P.swt_c)
    e.wait_ge(s["swtc"], 6)
    e.wait_ge(s["sa2"], 4)
    for st in range(NST):
        b = st % 2
        e.wait_ge(s[f"slnf{st % 2}"], 128 * (st // 2 + 1))
        for ic in range(8):
            g = 8 * st + ic
            pq = P.pk[ic % 2]
            if g >= 2:
                e.wait_ge(s["sqe"], g - 1)
            for jc in range(8):
                ins = e.matmul(pq[:, :], P.wq_sb[:, jc * D + ic * 128: jc * D + ic * 128 + 128],
                               P.ln1fB[b][:, jc * 512:(jc + 1) * 512],
                               start=(jc == 0), stop=(jc == 7))
            ins.then_inc(s["sqp"], 1)
        e.wait_ge(s["sqe"], 8 * (st + 1))
        if st >= 1:
            e.wait_ge(s["szs"], st)
        for jc in range(8):
            ins = e.matmul(P.pmisc[0:16, :], P.KS[:, jc * 16:(jc + 1) * 16],
                           P.q0[:, jc * 512:(jc + 1) * 512], start=(jc == 0), stop=(jc == 7))
        ins.then_inc(s["szp"], 1)
        for c in range(8):
            g = 8 * st + c
            pa = P.pv[c % 2]
            if g >= 2:
                e.wait_ge(s["sae"], g - 1)
            for hh in range(2):
                h = 2 * c + hh
                ins = e.matmul(pa[hh * 64:hh * 64 + 64, :],
                               P.kvt[hh * 64:hh * 64 + 64, c * 64:(c + 1) * 64],
                               P.q0[hh * 64:hh * 64 + 64, c * 512:(c + 1) * 512],
                               start=True, stop=True)
            ins.then_inc(s["sat"], 1)
        e.wait_ge(s["sae"], 8 * (st + 1))
        for t in range(4):
            i = st * 4 + t
            for hlf in range(2):
                gg = 2 * i + hlf
                if gg >= 1:
                    e.wait_ge(s["sx2h"], gg)
                for mc in range(8):
                    e.matmul(P.pkv[:, :], P.atf[:, mc * 512 + t * 128: mc * 512 + t * 128 + 128],
                             P.wo_sb[:, mc * D + hlf * 512: mc * D + hlf * 512 + 512],
                             start=(mc == 0), stop=False)
                e.matmul(P.pkv[:, :], P.ones1[:, :], P.bor[0:1, hlf * 512:(hlf + 1) * 512],
                         start=False, stop=True).then_inc(s["sop"], 1)


def B_vector(P, e):
    s = P.sem
    for st in range(NST):
        for ic in range(8):
            g = 8 * st + ic
            pq = P.pk[ic % 2]
            e.wait_ge(s["sqp"], g + 1)
            if g >= 1:
                e.wait_ge(s["sqe"], g)
            e.tensor_scalar(P.ucl[ic % 2][:, :], pq[:, :], P.bqcs[:, ic:ic + 1], 0.0,
                            ALU.add, ALU.min).then_inc(s["smn"], 1)
            vm(P, e.tensor_scalar(P.tcl[0][:, :], pq[:, :], P.bqcs[:, ic:ic + 1], 0.0,
                                  ALU.add, ALU.max))
            e.wait_ge(s["suw"], NT + g + 1)
            vw(P, e)
            e.tensor_tensor(P.q0[:, ic * 512:(ic + 1) * 512], P.tcl[0][:, :],
                            P.wcl[ic % 2][:, :], ALU.add).then_inc(s["sqe"], 1)
        e.wait_ge(s["szp"], st + 1)
        e.tensor_scalar(P.zsb[:, :], P.pmisc[0:16, :], 1e-6, None, ALU.add).then_inc(s["szs"], 1)
        for c in range(8):
            g = 8 * st + c
            pa = P.pv[c % 2]
            e.wait_ge(s["sat"], g + 1)
            e.wait_ge(s["szx"], 32 * (st + 1))
            e.tensor_tensor(P.atf[:, c * 512:(c + 1) * 512], pa[:, :],
                            P.zex[:, c * 512:(c + 1) * 512], ALU.mult).then_inc(s["sae"], 1)
        for t in range(4):
            i = st * 4 + t
            bi = i % 2
            e.wait_ge(s[f"sxb{i % 2}"], 16 * (i // 2 + 1))
            if i >= 2:
                e.wait_ge(s[f"sx2w{i % 2}"], 16 * (i // 2))
                e.wait_ge(s["sln2a"], i - 1)
            for hlf in range(2):
                e.wait_ge(s["sop"], 2 * i + hlf + 1)
                e.tensor_tensor(P.x2sb[bi][:, hlf * 512:(hlf + 1) * 512], P.pkv[:, :],
                                P.xb[bi][:, hlf * 512:(hlf + 1) * 512], ALU.add
                                ).then_inc(s["sx2h"], 1)
            e.wait_ge(s["sx2h"], 2 * (i + 1))
            e.bn_stats(P.st6[bi][:, 0:6], P.x2sb[bi][:, 0:512])
            vm(P, e.bn_stats(P.st6[bi][:, 6:12], P.x2sb[bi][:, 512:1024]))
            vw(P, e)
            vm(P, e.bn_aggr(P.st2[bi][:, :], P.st6[bi][:, :]))
            vw(P, e)
            e.tensor_scalar(P.vep[bi][:, :], P.st2[bi][:, 1:2], EPS, None, ALU.add).then_inc(s["svst"], 1)
            e.wait_ge(s["srst"], NT + i + 1)
            if i >= 2:
                e.wait_ge(s[f"sln2w{i % 2}"], 16 * (i // 2))
            e.tensor_scalar(P.ln2tm[bi][:, :], P.x2sb[bi][:, :], P.st2[bi][:, 0:1],
                            P.rstd[bi][:, 0:1], ALU.subtract, ALU.mult).then_inc(s["sln2a"], 1)


def B_scalar(P, e):
    s = P.sem
    for st in range(NST):
        for ic in range(8):
            g = 8 * st + ic
            e.wait_ge(s["smn"], NT + g + 1)
            if g >= 2:
                e.wait_ge(s["sqe"], g - 1)
            e.activation(P.wcl[ic % 2][:, :], P.ucl[ic % 2][:, :], AF.Exp).then_inc(s["suw"], 1)
        e.wait_ge(s["szs"], st + 1)
        sm(P, e.activation(P.zl[:, :], P.zsb[:, :], AF.Ln))
        sw(P, e)
        e.activation(P.zr[:, :], P.zl[:, :], AF.Exp, scale=-1.0).then_inc(s["szr"], 1)
        for t in range(4):
            i = st * 4 + t
            bi = i % 2
            e.wait_ge(s["svst"], NT + i + 1)
            sm(P, e.activation(P.lnv[bi][:, :], P.vep[bi][:, :], AF.Ln))
            e.wait_ge(s["sx2h"], 2 * (i + 1))
            e.dma_start(out=P.o_x2[i * 128:(i + 1) * 128, :], in_=P.x2sb[bi][:, :]).then_inc(s[f"sx2w{i % 2}"], 16)
            sw(P, e)
            e.activation(P.rstd[bi][:, :], P.lnv[bi][:, :], AF.Exp, scale=-0.5).then_inc(s["srst"], 1)
            e.wait_ge(s["sln2a"], i + 1)
            e.dma_start(out=P.o_ln2[i * 128:(i + 1) * 128, :], in_=P.ln2tm[bi][:, :]).then_inc(s[f"sln2w{i % 2}"], 16)


def C0_sync(P, e):
    s = P.sem
    for st in range(NST):
        e.wait_ge(s["sln2w0"], 16 * 2 * (st + 1))
        e.wait_ge(s["sln2w1"], 16 * 2 * (st + 1))
        if st >= 2:
            e.wait_ge(s["sct"], st - 1)
        for jc in range(8):
            e.dma_start_transpose(P.ln2f[st % 2][:, jc * 512:(jc + 1) * 512],
                                  P.o_ln2[st * 512:(st + 1) * 512, jc * 128:(jc + 1) * 128]
                                  ).then_inc(s[f"slnc{st % 2}"], 16)


def C0_tensor(P, e):
    s = P.sem
    for st in range(NST):
        b = st % 2
        e.wait_ge(s[f"slnc{st % 2}"], 128 * (st // 2 + 1))
        if st >= 1:
            e.wait_ge(s["slgc"], st)
        for t in range(4):
            for jc in range(8):
                ins = e.matmul(P.pk[0][:, t * 8:(t + 1) * 8],
                               P.ln2f[b][:, jc * 512 + t * 128: jc * 512 + t * 128 + 128],
                               P.wg_sb[:, jc * 8:(jc + 1) * 8],
                               start=(jc == 0), stop=(jc == 7))
        ins.then_inc(s["sct"], 1)
        for t in range(4):
            i = st * 4 + t
            e.wait_ge(s["soh"], 4 * (st + 1))
            e.wait_ge(s["srun"], i)
            if i >= 4:
                e.wait_ge(s["ssel"], 4 * st)
            sl = P.pv[0][:, (i % 4) * 8:(i % 4) * 8 + 8]
            e.matmul(sl, P.ones1f[:, :], P.runrow_f[0:1, :], start=True, stop=False)
            e.matmul(sl, P.SU[:, :], P.ohb[:, (st % 2) * 32 + t * 8:(st % 2) * 32 + (t + 1) * 8], start=False, stop=True)
            e.matmul(P.pkv2[32:33, 0:8], P.onesc[:, :], P.ohb[:, (st % 2) * 32 + t * 8:(st % 2) * 32 + (t + 1) * 8],
                     start=(i == 0), stop=(i == NT - 1)).then_inc(s["srk"], 1)


def C0_vector(P, e):
    s = P.sem
    for st in range(NST):
        e.wait_ge(s["sct"], st + 1)
        if st >= 1:
            e.wait_ge(s["sgx"], st)
        vm(P, e.tensor_copy(P.lg[:, :], P.pk[0][:, 0:32]))
        vw(P, e)
        for t in range(4):
            ins = e.reduce_max(P.mx8[:, t:t + 1], P.lg[:, t * 8:(t + 1) * 8], AX.X)
        vm(P, ins)
        vw(P, e)
        for t in range(4):
            ins = e.tensor_scalar(P.smx[:, t * 8:(t + 1) * 8], P.lg[:, t * 8:(t + 1) * 8],
                                  P.mx8[:, t:t + 1], None, ALU.subtract)
            if t == 3:
                ins.then_inc(s["sgd"], 1)
        e.wait_ge(s["sgd"], st + 1)
        for t in range(4):
            ins = e.tensor_scalar(P.eqm[:, t * 8:(t + 1) * 8], P.smx[:, t * 8:(t + 1) * 8],
                                  -1e-30, None, ALU.is_ge)
            if t == 0:
                ins.then_inc(s["slgc"], 1)
        vm(P, ins)
        vw(P, e)
        for t in range(4):
            ins = e.scalar_tensor_tensor(P.cnd[:, t * 8:(t + 1) * 8], P.eqm[:, t * 8:(t + 1) * 8],
                                         -16384.0, P.iota8B[:, :], ALU.mult, ALU.add)
        vm(P, ins)
        vw(P, e)
        for t in range(4):
            ins = e.tensor_reduce(P.eixT[:, st * 4 + t: st * 4 + t + 1], P.cnd[:, t * 8:(t + 1) * 8],
                                  AX.X, ALU.min)
        vm(P, ins)
        vw(P, e)
        if st >= 2:
            e.wait_ge(s["srk"], 4 * (st - 1))
        ohof = (st % 2) * 32
        for t in range(4):
            ins = e.tensor_scalar(P.ohb[:, ohof + t * 8:ohof + (t + 1) * 8], P.iota8f[:, :],
                                  P.eixT[:, st * 4 + t: st * 4 + t + 1], None, ALU.is_equal)
        ins.then_inc(s["soh"], 4)
        e.wait_ge(s["sgx"], st + 1)
        for t in range(4):
            ins = e.reduce_sum(P.ssum[:, t:t + 1], P.sme[:, t * 8:(t + 1) * 8], AX.X)
        vm(P, ins)
        vw(P, e)
        e.reciprocal(P.gvT[:, st * 4:(st + 1) * 4], P.ssum[:, 0:4])
        for t in range(4):
            i = st * 4 + t
            e.wait_ge(s["srk"], i + 1)
            ins = e.tensor_tensor(P.rl[:, t * 8:(t + 1) * 8], P.pv[0][:, (i % 4) * 8:(i % 4) * 8 + 8],
                                  P.ohb[:, (st % 2) * 32 + t * 8:(st % 2) * 32 + (t + 1) * 8], ALU.mult)
            if t == 3:
                vm(P, ins)
            e.tensor_copy(P.runrow_f[0:1, :], P.pkv2[32:33, 0:8]).then_inc(s["srun"], 1)
        vw(P, e)
        for t in range(4):
            ins = e.reduce_sum(P.rnkT[:, st * 4 + t: st * 4 + t + 1], P.rl[:, t * 8:(t + 1) * 8], AX.X)
        ins.then_inc(s["ssel"], 4)
    e.wait_ge(s["srun"], NT)
    e.tensor_copy(P.tot_sb[0:1, :], P.pkv2[32:33, 0:8]).then_inc(s["sfin"], 1)


def C0_scalar(P, e):
    s = P.sem
    for st in range(NST):
        e.wait_ge(s["sgd"], st + 1)
        e.activation(P.sme[:, :], P.smx[:, :], AF.Exp).then_inc(s["sgx"], 1)


def final_gpsimd(P, e):
    s = P.sem
    e.wait_ge(s["sfin"], 1)
    e.wait_ge(s["ssel"], NT)
    n = P.swt_c
    e.dma_start(out=P.o_cnt[:, :], in_=P.tot_sb[0:1, :]).then_inc(s["swt"], 16)
    e.dma_start(out=P.o_rank[:, :], in_=P.rnkT[:, :]).then_inc(s["swt"], 16)
    e.dma_start(out=P.o_eidx[:, :], in_=P.eixT[:, :]).then_inc(s["swt"], 16)
    e.dma_start(out=P.o_gv[:, :], in_=P.gvT[:, :]).then_inc(s["swt"], 16)
    e.wait_ge(s["swt"], n + 64)
    e.wait_ge(s["sx2w0"], 16 * NT // 2)
    e.wait_ge(s["sx2w1"], 16 * NT // 2)
    e.wait_ge(s["sln2w0"], 16 * NT // 2)
    e.wait_ge(s["sln2w1"], 16 * NT // 2)
    e.wait_ge(s["swc"], 16 * 16)


def emit_A(P, phases=("A", "B", "C0")):
    nc = P.nc
    pA = "A" in phases; pB = "B" in phases; pC = "C0" in phases
    with P.es:
        block = P.es.enter_context(nc.Block())

        @block.sync
        def _(e):
            if pA: A_sync(P, e)
            if pB: B_sync(P, e)
            if pC: C0_sync(P, e)

        @block.gpsimd
        def _(e):
            A_gpsimd_consts(P, e)
            if pB: B_gpsimd(P, e)
            if pC:
                final_gpsimd(P, e)
            elif pB:
                e.wait_ge(P.sem["sx2w0"], 16 * NT // 2)
                e.wait_ge(P.sem["sx2w1"], 16 * NT // 2)
                e.wait_ge(P.sem["sln2w0"], 16 * NT // 2)
                e.wait_ge(P.sem["sln2w1"], 16 * NT // 2)
            else:
                if ALVL >= 3:
                    e.wait_ge(P.sem["sstw0"], 16 * NT // 2)
                    e.wait_ge(P.sem["sstw1"], 16 * NT // 2)
                if ALVL >= 4:
                    e.wait_ge(P.sem["selu"], NT)
                if ALVL >= 50:
                    e.wait_ge(P.sem["skva"], NT)
                if ALVL < 3:
                    e.wait_ge(P.sem["srst"], NT)

        @block.vector
        def _(e):
            A_vec_consts(P, e)
            if pA: A_vector(P, e)
            if pA: A2_vector(P, e)
            if pB: B_vector(P, e)
            if pC: C0_vector(P, e)

        @block.scalar
        def _(e):
            if pA: A_scalar(P, e)
            if pB: B_scalar(P, e)
            if pC: C0_scalar(P, e)

        @block.tensor
        def _(e):
            if pA: A_tensor(P, e)
            if pA: A2_tensor(P, e)
            if pB: B_tensor(P, e)
            if pC: C0_tensor(P, e)
    return nc


def host_prep_A(inp):
    Wq = np.asarray(inp["L0_attn_Wq"]); Wk = np.asarray(inp["L0_attn_Wk"])
    Wv = np.asarray(inp["L0_attn_Wv"]); Wo = np.asarray(inp["L0_attn_Wo"])
    base = {
        "wqt": np.ascontiguousarray(Wq.T), "wkt": np.ascontiguousarray(Wk.T),
        "wvt": np.ascontiguousarray(Wv.T), "wot": np.ascontiguousarray(Wo.T),
        "bk": np.asarray(inp["L0_attn_bk"]).reshape(1, D),
        "bv": np.asarray(inp["L0_attn_bv"]).reshape(1, D),
        "bo": np.asarray(inp["L0_attn_bo"]).reshape(1, D),
        "bqc": np.ascontiguousarray(np.asarray(inp["L0_attn_bq"]).reshape(8, 128).T),
        "wg": np.ascontiguousarray(np.asarray(inp["L0_ffn_wg"])),
        "w1r": np.ascontiguousarray(np.asarray(inp["L0_ffn_w1"]).reshape(8, 8, 128, DH).transpose(0, 2, 1, 3)).reshape(1, -1),
        "w2r": np.ascontiguousarray(np.asarray(inp["L0_ffn_w2"]).reshape(8, 32, 128, 1024).transpose(0, 2, 1, 3)).reshape(1, -1),
    }
    x = np.asarray(inp["L0_attn_hidden_states"])
    return [dict(base, x=np.ascontiguousarray(x[c])) for c in range(8)]


GRP = 256
NGRP = 40
NSLOT = NGRP * GRP
W1SL = 128 * 8 * 512
W2SL = 128 * 4 * 1024
SENT = 1 << 30


def build_B():
    nc = bass.Bass("TRN2", target_bir_lowering=False)
    P = P_()
    P.nc = nc
    P.cnt8 = nc.dram_tensor("cnt8", [8, 8], F32, kind="ExternalInput").ap()
    P.rmsk = nc.dram_tensor("rmsk", [8, 1], F32, kind="ExternalInput").ap()
    P.mhot = nc.dram_tensor("mhot", [8, 1], F32, kind="ExternalInput").ap()
    P.x2d = nc.dram_tensor("x2d", [S, D], F32, kind="ExternalInput").ap()
    P.ln2d = nc.dram_tensor("ln2d", [S, D], BF16, kind="ExternalInput").ap()
    P.rnkd = nc.dram_tensor("rnkd", [128, NT], F32, kind="ExternalInput").ap()
    P.eixd = nc.dram_tensor("eixd", [128, NT], F32, kind="ExternalInput").ap()
    P.gvd = nc.dram_tensor("gvd", [128, NT], F32, kind="ExternalInput").ap()
    P.w1c = nc.dram_tensor("w1c", [1, 8 * 128 * 8 * DH], BF16, kind="ExternalInput").ap()
    P.w2c = nc.dram_tensor("w2c", [1, 8 * 128 * 32 * 1024], BF16, kind="ExternalInput").ap()
    P.b1r = nc.dram_tensor("b1r", [8 * 128, 32], F32, kind="ExternalInput").ap()
    P.b2d = nc.dram_tensor("b2d", [8, 1024], F32, kind="ExternalInput").ap()
    P.y = nc.dram_tensor("y", [S, D], F32, kind="ExternalOutput").ap()
    P.dispd = nc.dram_tensor("dispd", [NSLOT, D], BF16).ap()
    P.eod = nc.dram_tensor("eod", [NSLOT, D], BF16).ap()

    es = ExitStack()
    P.es = es
    sb = lambda name, shape, dt: es.enter_context(nc.sbuf_tensor(name, shape, dt))
    ps = lambda name, shape, dt: es.enter_context(nc.psum_tensor(name, shape, dt))
    P.w1sb = sb("w1sb", [128, 8 * DH], BF16)
    P.w2sb = sb("w2sb", [128, 32 * 1024], BF16)
    P.hfm = sb("hfm", [128, 32 * GRP], BF16)
    P.dspf = [sb(f"dspf{b}", [128, 8 * GRP], BF16) for b in range(2)]
    P.b1g = [sb(f"b1g{b}", [128, 32], F32) for b in range(2)]
    P.b2sb = sb("b2sb", [8, 1024], BF16)
    P.eosb = [sb(f"eosb{b}", [128, D], BF16) for b in range(2)]
    P.cnts = sb("cnts", [8, 8], F32)
    P.rmsks = sb("rmsks", [8, 1], F32)
    P.mhots = sb("mhots", [8, 1], F32)
    P.rnks = sb("rnks", [128, NT], F32)
    P.eixs = sb("eixs", [128, NT], F32)
    P.gvs = sb("gvs", [128, NT], F32)
    P.gvmT = sb("gvmT", [128, NT], F32)
    P.slotT = sb("slotT", [128, NT], I32)
    P.slotF = sb("slotF", [128, NT], F32)
    P.coffr = sb("coffr", [1, 8], F32)
    P.mycr = sb("mycr", [1, 8], F32)
    P.remr = sb("remr", [1, 8], F32)
    P.kptr = sb("kptr", [1, 8], F32)
    P.ntgi = sb("ntgi", [1, 8], I32)
    P.t8i = sb("t8i", [1, 8], I32)
    P.t8i2 = sb("t8i2", [1, 8], I32)
    P.ntgr = sb("ntgr", [1, 8], F32)
    P.ntgc = sb("ntgc", [8, 1], F32)
    P.gsr = sb("gsr", [1, 8], F32)
    P.gs256r = sb("gs256r", [1, 8], F32)
    P.t8a = sb("t8a", [1, 8], F32)
    P.t8b = sb("t8b", [1, 8], F32)
    P.Erow = sb("Erow", [1, NGRP], F32)
    P.swf = sb("swf", [1, NGRP], F32)
    P.swsent = sb("swsent", [1, NGRP], F32)

    P.wof_f = sb("wof_f", [1, NGRP], F32)
    P.wof = sb("wof", [1, NGRP], I32)
    P.t40a = sb("t40a", [1, NGRP], F32)
    P.t40b = sb("t40b", [1, NGRP], F32)
    P.iota40 = sb("iota40", [1, NGRP], I32)
    P.iota40f = sb("iota40f", [1, NGRP], F32)
    P.iexb = sb("iexb", [128, NGRP], I32)
    P.iexf = sb("iexf", [128, NGRP], F32)
    P.ohgall = sb("ohgall", [8, NGRP], BF16)
    P.ohgf = sb("ohgf", [8, NGRP], F32)
    P.iotaP = sb("iotaP", [128, 1], I32)
    P.iotaPf = sb("iotaPf", [128, 1], F32)
    P.iota8 = sb("iota8", [128, 8], I32)
    P.iota8f = sb("iota8f", [128, 8], F32)
    P.iotaP8 = sb("iotaP8", [8, NGRP], I32)
    P.iotaP8f = sb("iotaP8f", [8, NGRP], F32)
    P.su8 = sb("su8", [8, 8], F32)
    P.su_i32 = sb("su_i32", [128, 128], I32)
    P.ones1f = sb("ones1f", [1, 128], F32)
    P.idenf = sb("idenf", [128, 128], F32)
    P.bigc = sb("bigc", [128, 1], F32)
    P.ohD = sb("ohD", [128, 8], F32)
    P.cselc = sb("cselc", [128, 8], F32)
    P.gselc = sb("gselc", [128, 8], F32)
    P.csel = sb("csel", [128, 1], F32)
    P.gsel = sb("gsel", [128, 1], F32)
    P.globc = sb("globc", [128, 1], F32)
    P.keepc = sb("keepc", [128, 1], F32)
    P.keepi = sb("keepi", [128, 1], I32)
    P.sl0 = sb("sl0", [128, 1], F32)
    P.coffb = sb("coffb", [128, 8], F32)
    P.gs256b = sb("gs256b", [128, 8], F32)
    P.zt = sb("zt", [128, D], BF16)
    P.ln2t = [sb(f"ln2t{b}", [128, D], BF16) for b in range(2)]
    P.x2t = [sb(f"x2t{b}", [128, D], F32) for b in range(2)]
    P.gath = [sb(f"gath{b}", [128, D], BF16) for b in range(2)]
    P.osb = [sb(f"osb{b}", [128, D], F32) for b in range(2)]
    P.ph = [ps(f"ph{b}", [128, GRP], F32) for b in range(2)]
    P.peo = [ps(f"peo{b}", [128, 512], F32) for b in range(2)]
    P.pmb = ps("pmb", [128, 512], F32)
    P.sem = {}
    for n in ["sin", "sioc", "sc1", "sc1b", "sslot", "sdl0", "sdl1", "ssc0", "ssc1",
              "swl", "smm", "sdf0", "sdf1", "sb1_0", "sb1_1", "sh", "sho",
              "seo", "sev", "seow0", "seow1", "sga0", "sga1", "szg", "sx2l0", "sx2l1", "sout",
              "syw0", "syw1", "svo", "szt", "sztd"]:
        P.sem[n] = es.enter_context(nc.semaphore(n))
    P.vn = 0
    return P


def Bl_gpsimd(P, e):
    s = P.sem
    n = 0
    def dma(dst, src):
        nonlocal n
        e.dma_start(out=dst, in_=src).then_inc(s["sin"], 16)
        n += 16
    dma(P.cnts[:, :], P.cnt8[:, :])
    dma(P.rmsks[:, :], P.rmsk[:, :])
    dma(P.mhots[:, :], P.mhot[:, :])
    dma(P.rnks[:, :], P.rnkd[:, :])
    dma(P.eixs[:, :], P.eixd[:, :])
    dma(P.gvs[:, :], P.gvd[:, :])
    dma(P.b2sb[:, :], P.b2d[:, :])
    P.sin_c = n
    e.iota(P.iota40[:, :], [[1, NGRP]], base=0, channel_multiplier=0).then_inc(s["sioc"], 1)
    e.iota(P.iotaP[:, :], [[1, 1]], base=0, channel_multiplier=1).then_inc(s["sioc"], 1)
    e.iota(P.iota8[:, :], [[1, 8]], base=0, channel_multiplier=0).then_inc(s["sioc"], 1)
    e.iota(P.iotaP8[:, :], [[0, NGRP]], base=0, channel_multiplier=1).then_inc(s["sioc"], 1)
    e.iota(P.su_i32[:, :], [[1, 128]], base=0, channel_multiplier=-1).then_inc(s["sioc"], 1)
    e.memset(P.gath[0][:, :], 0.0)
    e.memset(P.gath[1][:, :], 0.0).then_inc(s["szg"], 1)
    e.wait_ge(s["szt"], 1)
    e.dma_start(out=P.dispd[:, :].rearrange("(r p) d -> p r d", p=128),
                in_=P.zt[:, :].unsqueeze(1).to_broadcast([128, NSLOT // 128, D])
                ).then_inc(s["sztd"], 16)
    e.wait_ge(s["sztd"], 16)
    import os as _os0
    import contextlib as _ctx0
    _sc0 = (lambda n: P.nc.named_scope(n)) if _os0.environ.get("MK_SCOPE") else (lambda n: _ctx0.nullcontext())
    with e.register() as breg:
        e.reg_mov(breg, NSLOT - 1)
        bval = e.snap(breg)
        _g1 = _sc0("dsc"); _g1.__enter__()
        for i in range(NT):
            e.wait_ge(s["sslot"], i + 1)
            e.wait_ge(s[f"sdl{i % 2}"], 16 * (i // 2 + 1))
            if i >= 8:
                e.wait_ge(s[f"ssc{i % 2}"], 16 * ((i - 8) // 2 + 1))
            e.indirect_dma_start(
                P.dispd[:, :], IndirectOffsetOnAxis(P.slotT[:, i:i + 1], 0),
                P.ln2t[i % 2][:, :], None,
                bounds_check=bval, oob_is_err=False).then_inc(s[f"ssc{i % 2}"], 16)
    _g1.__exit__(None, None, None)
    with e.register() as breg1:
        e.reg_mov(breg1, 8 * 128 - 1)
        bval1 = e.snap(breg1)
        for g in range(NGRP):
            if g >= 2:
                e.wait_ge(s["sho"], 32 * (g - 1))
            e.indirect_dma_start(
                P.b1g[g % 2][:, :], None,
                P.b1r[:, :], IndirectOffsetOnAxis(P.iexb[:, g:g + 1], 0),
                bounds_check=bval1, oob_is_err=False).then_inc(s[f"sb1_{g % 2}"], 16)
    _g2 = _sc0("fga")
    _g2.__enter__()
    with e.register() as breg2:
      e.reg_mov(breg2, NSLOT - 1)
      bval2 = e.snap(breg2)
      for i in range(NT):
          e.wait_ge(s["seow0"], 16 * NGRP)
          e.wait_ge(s["seow1"], 16 * NGRP)
          if i >= 2:
                e.wait_ge(s["sout"], i - 1)
          e.indirect_dma_start(
                P.gath[i % 2][:, :], None,
                P.eod[:, :], IndirectOffsetOnAxis(P.slotT[:, i:i + 1], 0),
                bounds_check=bval2, oob_is_err=False).then_inc(s[f"sga{i % 2}"], 16)
    _g2.__exit__(None, None, None)
    e.wait_ge(s["syw0"], 16 * NT // 2)
    e.wait_ge(s["syw1"], 16 * NT // 2)


def Bl_sync(P, e):
    s = P.sem
    import os as _os
    import contextlib as _ctx
    _sc = (lambda n: P.nc.named_scope(n)) if _os.environ.get("MK_SCOPE") else (lambda n: _ctx.nullcontext())
    _s1 = _sc("l2l"); _s1.__enter__()
    for i in range(NT):
        if i >= 2:
            e.wait_ge(s[f"ssc{i % 2}"], 16 * ((i - 2) // 2 + 1))
        e.dma_start(out=P.ln2t[i % 2][:, :], in_=P.ln2d[i * 128:(i + 1) * 128, :]
                    ).then_inc(s[f"sdl{i % 2}"], 16)
    _s1.__exit__(None, None, None)
    _s2 = _sc("tld"); _s2.__enter__()
    for g in range(NGRP):
        e.wait_ge(s["ssc0"], 16 * NT // 2)
        e.wait_ge(s["ssc1"], 16 * NT // 2)
        if g >= 2:
            e.wait_ge(s["sh"], 32 * (g - 1))
        for mc in range(8):
            e.dma_start_transpose(
                P.dspf[g % 2][:, mc * GRP:(mc + 1) * GRP],
                P.dispd[g * GRP:(g + 1) * GRP, mc * 128:(mc + 1) * 128]
                ).then_inc(s[f"sdf{g % 2}"], 16)
    _s2.__exit__(None, None, None)
    _s3 = _sc("fxy"); _s3.__enter__()
    for i in range(NT + 2):
        if i < NT:
            if i >= 2:
                e.wait_ge(s["sout"], i - 1)
            e.dma_start(out=P.x2t[i % 2][:, :], in_=P.x2d[i * 128:(i + 1) * 128, :]
                        ).then_inc(s[f"sx2l{i % 2}"], 16)
        j = i - 2
        if j >= 0:
            e.wait_ge(s["sout"], j + 1)
            e.dma_start(out=P.y[j * 128:(j + 1) * 128, :], in_=P.osb[j % 2][:, :]
                        ).then_inc(s[f"syw{j % 2}"], 16)
    _s3.__exit__(None, None, None)


def Bl_tensor(P, e):
    s = P.sem
    e.wait_ge(s["sin"], P.sin_c)
    e.wait_ge(s["sc1"], 1)
    e.matmul(P.pmb[0:1, 0:8], P.rmsks[:, :], P.cnts[:, :], start=True, stop=True)
    e.matmul(P.pmb[32:33, 0:8], P.mhots[:, :], P.cnts[:, :], start=True, stop=True
             ).then_inc(s["sc1b"], 1)
    e.wait_ge(s["sc1"], 2)
    e.transpose(P.pmb[0:8, 8:9], P.ntgr[:, :], P.idenf[0:1, 0:1]).then_inc(s["sc1b"], 1)
    e.wait_ge(s["sc1"], 3)
    e.matmul(P.pmb[0:1, 16:24], P.ntgc[:, :], P.su8[:, :], start=True, stop=True
             ).then_inc(s["sc1b"], 1)
    e.wait_ge(s["sc1"], 4)
    e.matmul(P.pmb[:, 64:104], P.ones1f[:, :], P.Erow[:, :], start=True, stop=True)
    e.matmul(P.pmb[:, 104:112], P.ones1f[:, :], P.coffr[:, :], start=True, stop=True)
    e.matmul(P.pmb[:, 112:120], P.ones1f[:, :], P.gs256r[:, :], start=True, stop=True
             ).then_inc(s["sc1b"], 1)
    import os as _os
    import contextlib as _ctx
    _sc = (lambda n: P.nc.named_scope(n)) if _os.environ.get("MK_SCOPE") else (lambda n: _ctx.nullcontext())
    _e1 = _sc("eph"); _e1.__enter__()
    for g in range(NGRP):
      with (_sc(f"peg{g:02d}") if _os.environ.get("MK_SCOPE_PEG") else _ctx.nullcontext()):
        e.wait_ge(s["swl"], 32 * (g + 1))
        e.wait_ge(s[f"sdf{g % 2}"], 128 * (g // 2 + 1))
        for hc in range(32):
            ghc = 32 * g + hc
            if ghc >= 2:
                e.wait_ge(s["sho"], ghc - 1)
            for mcb in range(8):
                ins = e.matmul(P.ph[hc % 2][:, :],
                               P.w1sb[:, mcb * DH + hc * 128: mcb * DH + hc * 128 + 128],
                               P.dspf[g % 2][:, mcb * GRP:(mcb + 1) * GRP],
                               start=(mcb == 0), stop=(mcb == 7))
            ins.then_inc(s["sh"], 1)
        e.wait_ge(s["sho"], 32 * (g + 1))
        for cs in range(2):
            for hf in range(2):
                u = 4 * g + 2 * cs + hf
                if u >= 2:
                    e.wait_ge(s["sev"], u - 1)
                for hc in range(32):
                    e.matmul(P.peo[hf][:, :],
                             P.hfm[:, hc * GRP + cs * 128: hc * GRP + cs * 128 + 128],
                             P.w2sb[:, hc * 1024 + hf * 512: hc * 1024 + hf * 512 + 512],
                             start=(hc == 0), stop=False)
                ins = e.matmul(P.peo[hf][:, :],
                               P.ohgall[:, g:g + 1].to_broadcast([8, 128]),
                               P.b2sb[0:8, hf * 512:(hf + 1) * 512],
                               start=False, stop=True)
                ins.then_inc(s["seo"], 1)
    _e1.__exit__(None, None, None)


def Bl_vector(P, e):
    s = P.sem
    e.wait_ge(s["sin"], P.sin_c)
    e.wait_ge(s["sioc"], 5)
    e.memset(P.zt[:, :], 0.0).then_inc(s["szt"], 1)
    e.memset(P.ones1f[:, :], 1.0)
    e.memset(P.bigc[:, :], 1048576.0)
    e.memset(P.Erow[:, :], -1.0)
    e.memset(P.swf[:, :], 1.0)
    e.tensor_scalar(P.iota40f[:, :], P.iota40[:, :], 0.0, None, ALU.add)
    e.tensor_scalar(P.iotaPf[:, :], P.iotaP[:, :], 0.0, None, ALU.add)
    e.tensor_scalar(P.iota8f[:, :], P.iota8[:, :], 0.0, None, ALU.add)
    e.tensor_scalar(P.iotaP8f[:, :], P.iotaP8[:, :], 0.0, None, ALU.add)
    e.tensor_scalar(P.idenf[:, :], P.su_i32[:, :], 0.0, None, ALU.is_equal)
    vm(P, e.tensor_scalar(P.su8[:, :], P.su_i32[0:8, 0:8], 0.0, None, ALU.is_gt))
    vw(P, e)
    e.engine_nop().then_inc(s["sc1"], 1)
    e.wait_ge(s["sc1b"], 1)
    e.tensor_scalar(P.coffr[:, :], P.pmb[0:1, 0:8], 0.0, None, ALU.add)
    vm(P, e.tensor_scalar(P.mycr[:, :], P.pmb[32:33, 0:8], 0.0, None, ALU.add))
    vw(P, e)
    vm(P, e.tensor_scalar(P.remr[:, :], P.coffr[:, :], -1.0, 8192.0, ALU.mult, ALU.add))
    vw(P, e)
    vm(P, e.tensor_scalar(P.kptr[:, :], P.remr[:, :], 0.0, None, ALU.max))
    vw(P, e)
    vm(P, e.tensor_tensor(P.t8a[:, :], P.kptr[:, :], P.mycr[:, :], ALU.min))
    vw(P, e)
    vm(P, e.tensor_scalar(P.t8i[:, :], P.t8a[:, :], 0, None, ALU.add))
    vw(P, e)
    vm(P, e.tensor_scalar(P.t8i2[:, :], P.t8i[:, :], 255, None, ALU.add))
    vw(P, e)
    vm(P, e.tensor_scalar(P.ntgi[:, :], P.t8i2[:, :], 8, None, ALU.arith_shift_right))
    vw(P, e)
    vm(P, e.tensor_scalar(P.ntgr[:, :], P.ntgi[:, :], 0.0, None, ALU.add))
    vw(P, e)
    e.engine_nop().then_inc(s["sc1"], 1)
    e.wait_ge(s["sc1b"], 2)
    vm(P, e.tensor_scalar(P.ntgc[:, :], P.pmb[0:8, 8:9], 0.0, None, ALU.add))
    vw(P, e)
    e.engine_nop().then_inc(s["sc1"], 1)
    e.wait_ge(s["sc1b"], 3)
    vm(P, e.tensor_scalar(P.gsr[:, :], P.pmb[0:1, 16:24], 0.0, None, ALU.add))
    vw(P, e)
    vm(P, e.tensor_scalar(P.gs256r[:, :], P.gsr[:, :], 256.0, None, ALU.mult))
    vw(P, e)
    for ee in range(8):
        vm(P, e.tensor_scalar(P.t40a[:, :], P.iota40f[:, :], P.gsr[0:1, ee:ee + 1], None, ALU.is_ge))
        vw(P, e)
        vm(P, e.tensor_tensor(P.t40b[:, :], P.Erow[:, :], P.t40a[:, :], ALU.add))
        vw(P, e)
        vm(P, e.tensor_scalar(P.Erow[:, :], P.t40b[:, :], 0.0, None, ALU.add))
        vw(P, e)
    vm(P, e.tensor_tensor(P.swf[0:1, 1:NGRP], P.Erow[0:1, 1:NGRP], P.Erow[0:1, 0:NGRP - 1],
                          ALU.not_equal))
    vw(P, e)
    vm(P, e.tensor_scalar(P.swsent[:, :], P.swf[:, :], float(-SENT), float(SENT),
                          ALU.mult, ALU.add))
    vw(P, e)
    vm(P, e.tensor_scalar(P.t40a[:, :], P.Erow[:, :], float(128 * 8 * DH), None, ALU.mult))
    vw(P, e)
    vm(P, e.tensor_tensor(P.wof_f[:, :], P.t40a[:, :], P.swsent[:, :], ALU.add))
    vw(P, e)
    vm(P, e.tensor_scalar(P.wof[:, :], P.wof_f[:, :], 0.0, None, ALU.add))
    vw(P, e)

    e.engine_nop().then_inc(s["sc1"], 1)
    e.wait_ge(s["sc1b"], 4)
    vm(P, e.tensor_scalar(P.iexf[:, :], P.pmb[:, 64:104], 128.0, P.iotaPf[:, 0:1],
                          ALU.mult, ALU.add))
    vw(P, e)
    e.tensor_scalar(P.iexb[:, :], P.iexf[:, :], 0.0, None, ALU.add)

    vm(P, e.tensor_scalar(P.ohgf[:, :], P.pmb[0:8, 64:104], 0.0, None, ALU.add))
    vw(P, e)
    e.tensor_tensor(P.ohgall[:, :], P.iotaP8f[:, :], P.ohgf[:, :], ALU.is_equal)
    e.tensor_scalar(P.coffb[:, :], P.pmb[:, 104:112], 0.0, None, ALU.add)
    vm(P, e.tensor_scalar(P.gs256b[:, :], P.pmb[:, 112:120], 0.0, None, ALU.add))
    vw(P, e)
    import os as _osv
    import contextlib as _ctxv
    _scv = (lambda n: P.nc.named_scope(n)) if _osv.environ.get("MK_SCOPE") else (lambda n: _ctxv.nullcontext())
    _v1 = _scv("slt"); _v1.__enter__()
    for i in range(NT):
        vm(P, e.tensor_scalar(P.ohD[:, :], P.iota8f[:, :], P.eixs[:, i:i + 1], None, ALU.is_equal))
        vw(P, e)
        e.tensor_tensor(P.cselc[:, :], P.ohD[:, :], P.coffb[:, :], ALU.mult)
        vm(P, e.tensor_tensor(P.gselc[:, :], P.ohD[:, :], P.gs256b[:, :], ALU.mult))
        vw(P, e)
        e.reduce_sum(P.csel[:, :], P.cselc[:, :], AX.X)
        vm(P, e.reduce_sum(P.gsel[:, :], P.gselc[:, :], AX.X))
        vw(P, e)
        e.tensor_tensor(P.globc[:, :], P.rnks[:, i:i + 1], P.csel[:, :], ALU.add)
        vm(P, e.tensor_tensor(P.sl0[:, :], P.rnks[:, i:i + 1], P.gsel[:, :], ALU.add))
        vw(P, e)
        vm(P, e.tensor_scalar(P.keepc[:, :], P.globc[:, :], 8192.0, None, ALU.is_lt))
        vw(P, e)
        e.tensor_scalar(P.keepi[:, :], P.globc[:, :], 8192.0, None, ALU.is_lt)
        vm(P, e.tensor_tensor(P.gvmT[:, i:i + 1], P.gvs[:, i:i + 1], P.keepc[:, :], ALU.mult))
        vw(P, e)
        vm(P, e.select(P.slotF[:, i:i + 1], P.keepi[:, :], P.sl0[:, :], P.bigc[:, :], add_drain=True))
        vw(P, e)
        e.tensor_scalar(P.slotT[:, i:i + 1], P.slotF[:, i:i + 1], 0.0, None,
                        ALU.add).then_inc(s["sslot"], 1)
    _v1.__exit__(None, None, None)
    for g in range(NGRP):
        e.wait_ge(s[f"sb1_{g % 2}"], 16 * (g // 2 + 1))
        for hc in range(32):
            e.wait_ge(s["sh"], 32 * g + hc + 1)
            e.tensor_scalar(P.hfm[:, hc * GRP:(hc + 1) * GRP], P.ph[hc % 2][:, :],
                            P.b1g[g % 2][:, hc:hc + 1], 0.0, ALU.add, ALU.max
                            ).then_inc(s["sho"], 1)
        for cs in range(2):
            if g >= 1:
                e.wait_ge(s[f"seow{cs}"], 16 * g)
            for hf in range(2):
                u = 4 * g + 2 * cs + hf
                e.wait_ge(s["seo"], u + 1)
                e.tensor_scalar(P.eosb[cs][:, hf * 512:(hf + 1) * 512], P.peo[hf][:, :],
                                0.0, None, ALU.add).then_inc(s["sev"], 1)
    _v2 = _scv("cmb"); _v2.__enter__()
    for i in range(NT):
        if i < 2:
            e.wait_ge(s["szg"], 1)
        e.wait_ge(s[f"sga{i % 2}"], 16 * (i // 2 + 1))
        e.wait_ge(s[f"sx2l{i % 2}"], 16 * (i // 2 + 1))
        if i >= 2:
            e.wait_ge(s[f"syw{i % 2}"], 16 * (i // 2))
        e.scalar_tensor_tensor(P.osb[i % 2][:, :], P.gath[i % 2][:, :], P.gvmT[:, i:i + 1],
                               P.x2t[i % 2][:, :], ALU.mult, ALU.add).then_inc(s["sout"], 1)
    _v2.__exit__(None, None, None)


def Bl_scalar(P, e):
    s = P.sem
    EXPSZ = 128 * 8 * DH
    e.wait_ge(s["sslot"], 1)
    for g in range(NGRP):
        if g >= 1:
            e.wait_ge(s["sev"], 4 * g)
        if g >= 2:
            e.wait_ge(s["swl"], 32 * (g - 1))
        with e.register() as rg:
            e.reg_load(rg, P.wof[0:1, g:g + 1])
            val = e.snap(rg, min_val=0, max_val=7 * EXPSZ, donate=True)
            e.dma_start(P.w1sb[:, :], P.w1c[0:1, bass.ds(val, EXPSZ)],
                        bounds_check="skip_entire_dma").then_inc(s["swl"], 16)
            e.dma_start(P.w2sb[:, :], P.w2c[0:1, bass.ds(val, EXPSZ)],
                        bounds_check="skip_entire_dma").then_inc(s["swl"], 16)
        if g >= 1:
            for cs in range(2):
                e.wait_ge(s["sev"], 4 * (g - 1) + 2 * (cs + 1))
                e.dma_start(out=P.eod[(g - 1) * GRP + cs * 128: (g - 1) * GRP + (cs + 1) * 128, :],
                            in_=P.eosb[cs][:, :]).then_inc(s[f"seow{cs}"], 16)
    g = NGRP - 1
    for cs in range(2):
        e.wait_ge(s["sev"], 4 * g + 2 * (cs + 1))
        e.dma_start(out=P.eod[g * GRP + cs * 128: g * GRP + (cs + 1) * 128, :],
                    in_=P.eosb[cs][:, :]).then_inc(s[f"seow{cs}"], 16)


def emit_B(P):
    nc = P.nc
    with P.es:
        block = P.es.enter_context(nc.Block())

        @block.sync
        def _(e):
            Bl_sync(P, e)

        @block.gpsimd
        def _(e):
            Bl_gpsimd(P, e)

        @block.vector
        def _(e):
            Bl_vector(P, e)

        @block.scalar
        def _(e):
            Bl_scalar(P, e)

        @block.tensor
        def _(e):
            Bl_tensor(P, e)
    return nc


def host_prep_B(inp, a_results):
    w1 = np.asarray(inp["L0_ffn_w1"])
    w2 = np.asarray(inp["L0_ffn_w2"])
    b1 = np.asarray(inp["L0_ffn_b1"])
    b2 = np.asarray(inp["L0_ffn_b2"])
    b1r = np.ascontiguousarray(b1.reshape(8, 32, 128).transpose(0, 2, 1)).reshape(8 * 128, 32)
    cnt8 = np.concatenate([np.asarray(r["o_cnt"]) for r in a_results], 0)
    base = {"w1c": np.asarray(a_results[0]["w1c"]), "w2c": np.asarray(a_results[0]["w2c"]),
            "b1r": b1r, "b2d": b2, "cnt8": cnt8}
    maps = []
    for c in range(8):
        m = dict(base)
        m["rmsk"] = (np.arange(8) < c).astype(np.float32).reshape(8, 1)
        m["mhot"] = (np.arange(8) == c).astype(np.float32).reshape(8, 1)
        m["x2d"] = np.asarray(a_results[c]["o_x2"])
        m["ln2d"] = np.asarray(a_results[c]["o_ln2"])
        m["rnkd"] = np.asarray(a_results[c]["o_rank"])
        m["eixd"] = np.asarray(a_results[c]["o_eidx"])
        m["gvd"] = np.asarray(a_results[c]["o_gv"])
        maps.append(m)
    return maps


def kernel(**inputs):
    PA = build_A()
    ncA = emit_A(PA)
    mapsA = host_prep_A(inputs)
    resA = run_bass_kernel_spmd(ncA, mapsA, core_ids=list(range(8)))
    a_results = [{k: np.asarray(v) for k, v in resA.results[c].items()} for c in range(8)]

    PB = build_B()
    ncB = emit_B(PB)
    mapsB = host_prep_B(inputs, a_results)
    resB = run_bass_kernel_spmd(ncB, mapsB, core_ids=list(range(8)))
    y = np.stack([np.asarray(resB.results[c]["y"], dtype=np.float32) for c in range(8)])
    return y


# version 03 (final)
# speedup vs baseline: 1.3736x; 1.0204x over previous
import numpy as np
import ml_dtypes
from contextlib import ExitStack
import concourse.bass as bass
import concourse.mybir as mybir
from concourse.bass import IndirectOffsetOnAxis
from concourse.bass_utils import run_bass_kernel_spmd

F32 = mybir.dt.float32
BF16 = mybir.dt.bfloat16
I32 = mybir.dt.int32
AF = mybir.ActivationFunctionType
ALU = mybir.AluOpType
AX = mybir.AxisListType
bfnp = ml_dtypes.bfloat16

S, D, NT, NST = 8192, 1024, 64, 16
ALVL = 99
H, HD, NE, DH = 16, 64, 8, 4096
EPS = 1e-5


class P_:
    pass


def build_A():
    nc = bass.Bass("TRN2", target_bir_lowering=False)
    P = P_()
    P.nc = nc
    P.x = nc.dram_tensor("x", [S, D], F32, kind="ExternalInput").ap()
    P.wqt = nc.dram_tensor("wqt", [D, D], F32, kind="ExternalInput").ap()
    P.wkt = nc.dram_tensor("wkt", [D, D], F32, kind="ExternalInput").ap()
    P.wvt = nc.dram_tensor("wvt", [D, D], F32, kind="ExternalInput").ap()
    P.wot = nc.dram_tensor("wot", [D, D], F32, kind="ExternalInput").ap()
    P.bk_d = nc.dram_tensor("bk", [1, D], F32, kind="ExternalInput").ap()
    P.bv_d = nc.dram_tensor("bv", [1, D], F32, kind="ExternalInput").ap()
    P.bo_d = nc.dram_tensor("bo", [1, D], F32, kind="ExternalInput").ap()
    P.bqc_d = nc.dram_tensor("bqc", [128, 8], F32, kind="ExternalInput").ap()
    P.wg_d = nc.dram_tensor("wg", [D, NE], F32, kind="ExternalInput").ap()
    P.o_x2 = nc.dram_tensor("o_x2", [S, D], F32, kind="ExternalOutput").ap()
    P.o_ln2 = nc.dram_tensor("o_ln2", [S, D], BF16, kind="ExternalOutput").ap()
    P.o_cnt = nc.dram_tensor("o_cnt", [1, NE], F32, kind="ExternalOutput").ap()
    P.o_rank = nc.dram_tensor("o_rank", [128, NT], F32, kind="ExternalOutput").ap()
    P.o_eidx = nc.dram_tensor("o_eidx", [128, NT], F32, kind="ExternalOutput").ap()
    P.o_gv = nc.dram_tensor("o_gv", [128, NT], F32, kind="ExternalOutput").ap()
    P.w1r = nc.dram_tensor("w1r", [1, 8 * 128 * 8 * DH], F32, kind="ExternalInput").ap()
    P.w2r = nc.dram_tensor("w2r", [1, 8 * 128 * 32 * 1024], F32, kind="ExternalInput").ap()
    P.w1c = nc.dram_tensor("w1c", [1, 8 * 128 * 8 * DH], BF16, kind="ExternalOutput").ap()
    P.w2c = nc.dram_tensor("w2c", [1, 8 * 128 * 32 * 1024], BF16, kind="ExternalOutput").ap()
    P.ln1st = nc.dram_tensor("ln1stash", [S, D], BF16).ap()
    P.z_dram = nc.dram_tensor("z_dram", [H, 512], F32).ap()

    es = ExitStack()
    P.es = es
    sb = lambda name, shape, dt: es.enter_context(nc.sbuf_tensor(name, shape, dt))
    ps = lambda name, shape, dt: es.enter_context(nc.psum_tensor(name, shape, dt))
    P.wk_sb = sb("wk_sb", [128, 8 * D], BF16)
    P.wv_sb = sb("wv_sb", [128, 8 * D], BF16)
    P.wq_sb = sb("wq_sb", [128, 8 * D], BF16)
    P.wo_sb = sb("wo_sb", [128, 8 * D], BF16)
    P.xa = [sb(f"xa{b}", [128, D], F32) for b in range(2)]
    P.ln1tm = [sb(f"ln1tm{b}", [128, D], BF16) for b in range(2)]
    P.ln1f = [sb(f"ln1f{b}", [128, D], BF16) for b in range(2)]
    P.kt = [sb(f"kt{b}", [128, D], BF16) for b in range(2)]
    P.vt = [sb(f"vt{b}", [128, D], BF16) for b in range(2)]
    P.kvt = sb("kvt", [128, 8 * HD], BF16)
    P.kvacc = [sb(f"kvacc{b}", [128, 512], F32) for b in range(2)]
    P.ksacc = [sb(f"ksacc{b}", [33, 512], F32) for b in range(2)]
    P.ksc = sb("ksc", [128, 8], F32)
    P.KS = sb("KS", [128, 8 * H], BF16)
    P.st6 = [sb(f"st6_{b}", [128, 12], F32) for b in range(2)]
    P.st2 = [sb(f"st2_{b}", [128, 2], F32) for b in range(2)]
    P.vep = [sb(f"vep{b}", [128, 1], F32) for b in range(2)]
    P.lnv = [sb(f"lnv{b}", [128, 1], F32) for b in range(2)]
    P.rstd = [sb(f"rstd{b}", [128, 1], F32) for b in range(2)]
    P.ucl = [sb(f"ucl{b}", [128, 512], F32) for b in range(2)]
    P.wcl = [sb(f"wcl{b}", [128, 512], BF16) for b in range(2)]
    P.tcl = [sb(f"tcl{b}", [128, 512], BF16) for b in range(2)]
    P.bkr = sb("bkr", [1, D], BF16)
    P.bvr = sb("bvr", [1, D], BF16)
    P.bor = sb("bor", [1, D], BF16)
    P.bqcs = sb("bqcs", [128, 8], F32)
    P.nbqc = sb("nbqc", [128, 8], F32)
    P.wg_sb = sb("wg_sb", [128, 8 * NE], BF16)
    P.ones1 = sb("ones1", [1, 128], BF16)
    P.ones1f = sb("ones1f", [1, 128], F32)
    P.onesc = sb("onesc", [128, 1], BF16)
    P.iden = sb("iden", [128, 128], BF16)
    P.idenf = sb("idenf", [128, 128], F32)
    P.SU = sb("SU", [128, 128], BF16)
    P.su_i32 = sb("su_i32", [128, 128], I32)
    P.iotaP = sb("iotaP", [128, 1], I32)
    P.iota8 = sb("iota8", [128, 8], I32)
    P.iota16 = sb("iota16", [128, 16], I32)
    P.iota8f = sb("iota8f", [128, 8], F32)
    P.iota8B = sb("iota8B", [128, 8], F32)
    P.iota16f = sb("iota16f", [128, 16], F32)
    P.hpc = sb("hpc", [128, 1], F32)
    P.hb8 = sb("hb8", [128, 8], F32)
    P.msk16 = sb("msk16", [128, 128], F32)
    P.spacer = sb("spacer", [128, 8], F32)
    P.ln1fB = [sb(f"ln1fB{b}", [128, 8 * 512], BF16) for b in range(2)]
    P.q0 = sb("q0", [128, 8 * 512], BF16)
    P.atf = sb("atf", [128, 8 * 512], BF16)
    P.zex = sb("zex", [128, 8 * 512], BF16)
    P.zsb = sb("zsb", [H, 512], F32)
    P.zl = sb("zl", [H, 512], F32)
    P.zr = sb("zr", [H, 512], F32)
    P.xb = [sb(f"xb{b}", [128, D], F32) for b in range(2)]
    P.x2sb = [sb(f"x2sb{b}", [128, D], F32) for b in range(2)]
    P.ln2tm = [sb(f"ln2tm{b}", [128, D], BF16) for b in range(2)]
    P.ln2f = [sb(f"ln2f{b}", [128, 8 * 512], BF16) for b in range(2)]
    P.lg = sb("lg", [128, 32], F32)
    P.mx8 = sb("mx8", [128, 4], F32)
    P.sme = sb("sme", [128, 32], F32)
    P.smx = sb("smx", [128, 32], F32)
    P.ssum = sb("ssum", [128, 4], F32)
    P.eqm = sb("eqm", [128, 32], F32)
    P.cnd = sb("cnd", [128, 32], F32)
    P.ohb = sb("ohb", [128, 64], BF16)
    P.rl = sb("rl", [128, 32], F32)
    P.runrow_f = sb("runrow_f", [1, NE], F32)
    P.rnkT = sb("rnkT", [128, NT], F32)
    P.eixT = sb("eixT", [128, NT], F32)
    P.gvT = sb("gvT", [128, NT], F32)
    P.tot_sb = sb("tot_sb", [1, NE], F32)
    P.pk = [ps(f"pk{b}", [128, 512], F32) for b in range(2)]
    P.pv = [ps(f"pv{b}", [128, 512], F32) for b in range(2)]
    P.pkv = ps("pkv", [128, 512], F32)
    P.pmisc = ps("pmisc", [128, 512], F32)
    P.pkv2 = ps("pkv2", [128, 512], F32)
    P.ptr = ps("ptr", [128, 1024], BF16)
    P.sem = {}
    for n in ["sxa0", "sxa1", "svst", "srst", "sapp", "sstw0", "sstw1", "strn", "stev", "skp", "sta0", "sta1",
              "suw", "selu", "skv", "swt", "sa2",
              "slnf0", "slnf1", "slnc0", "slnc1", "sqp", "sqe", "szp", "szs", "szr", "szw", "szx",
              "sat", "sae", "sop", "sxb0", "sxb1", "sx2h", "sx2w0", "sx2w1", "sln2a",
              "sln2w0", "sln2w1",
              "sct", "sgd", "sgx", "soh", "srk", "srun", "ssel", "slgc", "sfin", "swtc", "svo", "sso", "skva", "smn", "swc"]:
        P.sem[n] = es.enter_context(nc.semaphore(n))
    P.vn = 0
    P.sn = 0
    return P


def vm(P, ins):
    P.vn += 1
    ins.then_inc(P.sem["svo"], 1)


def vw(P, e):
    e.wait_ge(P.sem["svo"], P.vn)


def sm(P, ins):
    P.sn += 1
    ins.then_inc(P.sem["sso"], 1)


def sw(P, e):
    e.wait_ge(P.sem["sso"], P.sn)


def A_sync(P, e):
    s = P.sem
    for i in range(NT):
        if i >= 2:
            e.wait_ge(s["sapp" if ALVL >= 3 else "svst"], i - 1)
        e.dma_start(out=P.xa[i % 2][:, :], in_=P.x[i * 128:(i + 1) * 128, :]).then_inc(s[f"sxa{i % 2}"], 16)


def A_gpsimd_consts(P, e):
    s = P.sem["swt"]
    n = 0
    def dma(dst, src):
        nonlocal n
        e.dma_start(out=dst, in_=src).then_inc(s, 16)
        n += 16
    dma(P.wk_sb[:, :], P.wkt.rearrange("(jc p) i -> p jc i", p=128))
    dma(P.wv_sb[:, :], P.wvt.rearrange("(jc p) i -> p jc i", p=128))
    dma(P.bkr[:, :], P.bk_d[:, :])
    dma(P.bvr[:, :], P.bv_d[:, :])
    dma(P.bor[:, :], P.bo_d[:, :])
    dma(P.bqcs[:, :], P.bqc_d[:, :])
    dma(P.wg_sb[:, :], P.wg_d.rearrange("(jc p) e -> p jc e", p=128))
    sc = P.sem["swtc"]
    e.iota(P.iotaP[:, :], [[1, 1]], base=0, channel_multiplier=1).then_inc(sc, 1)
    e.iota(P.iota8[:, :], [[1, 8]], base=0, channel_multiplier=0).then_inc(sc, 1)
    e.iota(P.iota16[:, :], [[1, 16]], base=0, channel_multiplier=0).then_inc(sc, 1)
    e.iota(P.su_i32[:, :], [[1, 128]], base=0, channel_multiplier=-1).then_inc(sc, 1)
    CH = 1 << 22
    for c8 in range(8):
        for t, (dst, srcw) in enumerate(((P.w1c, P.w1r), (P.w2c, P.w2r))):
            e.dma_start(
                out=dst[0:1, c8 * CH:(c8 + 1) * CH].rearrange("o (r f) -> (o r) f", f=8192),
                in_=srcw[0:1, c8 * CH:(c8 + 1) * CH].rearrange("o (r f) -> (o r) f", f=8192),
            ).then_inc(P.sem["swc"], 16)
    P.swt_pre = n
    dma(P.wq_sb[:, :], P.wqt.rearrange("(jc p) i -> p jc i", p=128))
    dma(P.wo_sb[:, :], P.wot.rearrange("(jc p) i -> p jc i", p=128))
    P.swt_c = n


def A_vec_consts(P, e):
    s = P.sem
    e.wait_ge(s["swt"], P.swt_c)
    e.wait_ge(s["swtc"], 4)
    e.memset(P.ones1[:, :], 1.0)
    e.memset(P.ones1f[:, :], 1.0)
    e.memset(P.onesc[:, :], 1.0)
    e.memset(P.runrow_f[:, :], 0.0)
    e.tensor_scalar(P.SU[:, :], P.su_i32[:, :], 0.0, None, ALU.is_gt)
    e.tensor_scalar(P.iden[:, :], P.su_i32[:, :], 0.0, None, ALU.is_equal).then_inc(s["swtc"], 1)
    e.tensor_scalar(P.iota8f[:, :], P.iota8[:, :], 0.0, None, ALU.add)
    e.tensor_scalar(P.iota16f[:, :], P.iota16[:, :], 0.0, None, ALU.add)
    e.tensor_scalar(P.iota8B[:, :], P.iota8[:, :], 16384.0, None, ALU.add)
    e.tensor_scalar(P.hpc[:, :], P.iotaP[:, :], 63.5, None, ALU.is_gt)
    e.wait_ge(s["swtc"], 5)
    e.tensor_scalar(P.idenf[:, :], P.iden[:, :], 0.0, None, ALU.add)
    e.memset(P.pmisc[0:33, :], 0.0)
    e.tensor_scalar(P.nbqc[:, :], P.bqcs[:, :], -1.0, None, ALU.mult).then_inc(s["swtc"], 1)


def A_vector(P, e):
    s = P.sem
    e.wait_ge(s["swtc"], 6)
    for i in range(NT + 4):
        j = i - 2
        a = i - 1
        if i < NT:
            b = i % 2
            e.wait_ge(s[f"sxa{i % 2}"], 16 * (i // 2 + 1))
            e.bn_stats(P.st6[b][:, 0:6], P.xa[b][:, 0:512])
            vm(P, e.bn_stats(P.st6[b][:, 6:12], P.xa[b][:, 512:1024]))
        if 0 <= j < NT and ALVL >= 4:
            bj = j % 2
            e.wait_ge(s["skp"], j + 1)
            if j >= 1:
                e.wait_ge(s["skv"] if ALVL >= 50 else s["skp"], j if ALVL >= 50 else j + 1)
                e.wait_ge(s["selu"], j)
            e.tensor_scalar(P.ucl[0][:, :], P.pk[0][:, :], 0.0, None, ALU.min)
            e.tensor_scalar(P.ucl[1][:, :], P.pk[1][:, :], 0.0, None, ALU.min).then_inc(s["smn"], 1)
            e.tensor_scalar(P.tcl[0][:, :], P.pk[0][:, :], 0.0, None, ALU.max)
            vm(P, e.tensor_scalar(P.tcl[1][:, :], P.pk[1][:, :], 0.0, None, ALU.max))
            e.tensor_copy(P.vt[bj][:, 0:512], P.pv[0][:, :])
            e.tensor_copy(P.vt[bj][:, 512:1024], P.pv[1][:, :])
        if i < NT:
            b = i % 2
            vw(P, e)
            vm(P, e.bn_aggr(P.st2[b][:, :], P.st6[b][:, :]))
        if 0 <= j < NT and ALVL >= 4:
            bj = j % 2
            if ALVL >= 40:
                e.wait_ge(s["suw"], j + 1)
            vw(P, e)
            if ALVL >= 40:
                e.tensor_tensor(P.kt[bj][:, 0:512], P.tcl[0][:, :], P.wcl[0][:, :], ALU.add)
                e.tensor_tensor(P.kt[bj][:, 512:1024], P.tcl[1][:, :], P.wcl[1][:, :],
                                ALU.add).then_inc(s["selu"], 1)
            elif ALVL >= 38:
                e.tensor_scalar(P.kt[bj][:, 0:512], P.tcl[0][:, :], 0.0, None, ALU.add)
                e.tensor_scalar(P.kt[bj][:, 512:1024], P.tcl[1][:, :], 0.0, None, ALU.add).then_inc(s["selu"], 1)
            elif ALVL >= 37:
                e.tensor_tensor(P.kt[bj][:, 0:512], P.tcl[0][:, :], P.tcl[1][:, :], ALU.add)
                e.tensor_tensor(P.kt[bj][:, 512:1024], P.tcl[1][:, :], P.tcl[0][:, :],
                                ALU.add).then_inc(s["selu"], 1)
            elif ALVL >= 36:
                e.tensor_scalar(P.kt[bj][:, 0:512], P.tcl[0][:, :], 0.0, None, ALU.add)
                e.tensor_scalar(P.kt[bj][:, 512:1024], P.tcl[1][:, :], 0.0, None, ALU.add).then_inc(s["selu"], 1)
            elif ALVL >= 35:
                e.tensor_copy(P.kt[bj][:, 0:512], P.tcl[0][:, :])
                e.tensor_copy(P.kt[bj][:, 512:1024], P.tcl[1][:, :]).then_inc(s["selu"], 1)
            else:
                e.engine_nop().then_inc(s["selu"], 1)
        jk = i - 3
        if 0 <= jk < NT and ALVL >= 50:
            e.wait_ge(s["skv"], jk + 1)
            if jk >= 1:
                e.wait_ge(s["skva"], jk)
            if jk == 0:
                e.tensor_copy(P.kvacc[0][0:64, :], P.pkv[0:64, :])
                e.tensor_copy(P.kvacc[0][64:128, :], P.pkv2[0:64, :])
                ins = e.tensor_copy(P.ksacc[0][:, :], P.pmisc[0:33, :])
            else:
                e.tensor_tensor(P.kvacc[jk % 2][0:64, :], P.pkv[0:64, :],
                                P.kvacc[(jk - 1) % 2][0:64, :], ALU.add)
                e.tensor_tensor(P.kvacc[jk % 2][64:128, :], P.pkv2[0:64, :],
                                P.kvacc[(jk - 1) % 2][64:128, :], ALU.add)
                ins = e.tensor_tensor(P.ksacc[jk % 2][:, :], P.pmisc[0:33, :],
                                      P.ksacc[(jk - 1) % 2][:, :], ALU.add)
            ins.then_inc(s["skva"], 1)
        if i < NT:
            b = i % 2
            vw(P, e)
            e.tensor_scalar(P.vep[b][:, :], P.st2[b][:, 1:2], EPS, None, ALU.add).then_inc(s["svst"], 1)
        if 0 <= a < NT and ALVL >= 3:
            ba = a % 2
            e.wait_ge(s["srst"], a + 1)
            if a >= 2:
                e.wait_ge(s["strn"], a - 1)
                e.wait_ge(s[f"sstw{a % 2}"], 16 * (a // 2))
            e.tensor_scalar(P.ln1tm[ba][:, :], P.xa[ba][:, :], P.st2[ba][:, 0:1],
                            P.rstd[ba][:, 0:1], ALU.subtract, ALU.mult).then_inc(s["sapp"], 1)
        if 0 <= a < NT and ALVL >= 3:
            e.wait_ge(s["strn"], a + 1)
            if a >= 2 and ALVL >= 4:
                e.wait_ge(s["skp"], a - 1)
            e.tensor_copy(P.ln1f[a % 2][:, :], P.ptr[:, :]).then_inc(s["stev"], 1)



def A_scalar(P, e):
    s = P.sem
    for i in range(NT + 1):
        j = i - 1
        if i < NT:
            b = i % 2
            e.wait_ge(s["svst"], i + 1)
            sm(P, e.activation(P.lnv[b][:, :], P.vep[b][:, :], AF.Ln))
        if 0 <= j < NT and ALVL >= 3:
            e.wait_ge(s["sapp"], j + 1)
            e.dma_start(out=P.ln1st[j * 128:(j + 1) * 128, :],
                        in_=P.ln1tm[j % 2][:, :]).then_inc(s[f"sstw{j % 2}"], 16)
        if 0 <= j < NT and ALVL >= 38:
            e.wait_ge(s["smn"], j + 1)
            e.activation(P.wcl[0][:, :], P.ucl[0][:, :], AF.Exp)
            e.activation(P.wcl[1][:, :], P.ucl[1][:, :], AF.Exp).then_inc(s["suw"], 1)
        if i < NT:
            b = i % 2
            sw(P, e)
            e.activation(P.rstd[b][:, :], P.lnv[b][:, :], AF.Exp, scale=-0.5).then_inc(s["srst"], 1)


def A_tensor(P, e):
    s = P.sem
    e.wait_ge(s["swt"], P.swt_c)
    e.wait_ge(s["swtc"], 6)
    for i in range(NT + 3):
        a = i - 1
        j = i - 2
        j2 = i - 3
        if 0 <= j < NT and ALVL >= 4:
            bj = j % 2
            e.wait_ge(s["stev"], j + 1)
            if j >= 1:
                e.wait_ge(s["selu"], j)
            for hlf in range(2):
                for jc in range(8):
                    e.matmul(P.pk[hlf][:, :], P.ln1f[bj][:, jc * 128:(jc + 1) * 128],
                             P.wk_sb[:, jc * D + hlf * 512: jc * D + hlf * 512 + 512],
                             start=(jc == 0), stop=False)
                e.matmul(P.pk[hlf][:, :], P.ones1[:, :],
                         P.bkr[0:1, hlf * 512:(hlf + 1) * 512], start=False, stop=True)
            for hlf in range(2):
                for jc in range(8):
                    e.matmul(P.pv[hlf][:, :], P.ln1f[bj][:, jc * 128:(jc + 1) * 128],
                             P.wv_sb[:, jc * D + hlf * 512: jc * D + hlf * 512 + 512],
                             start=(jc == 0), stop=False)
                ins = e.matmul(P.pv[hlf][:, :], P.ones1[:, :],
                               P.bvr[0:1, hlf * 512:(hlf + 1) * 512], start=False, stop=True)
            ins.then_inc(s["skp"], 1)
        if 0 <= j2 < NT and ALVL >= 50:
            bj = j2 % 2
            e.wait_ge(s["selu"], j2 + 1)
            if j2 >= 1:
                e.wait_ge(s["skva"], j2)
            for h in range(H):
                pr = h // 2
                bank = P.pkv if h % 2 == 0 else P.pkv2
                e.matmul(bank[0:64, pr * 64:(pr + 1) * 64],
                         P.kt[bj][:, h * 64:(h + 1) * 64], P.vt[bj][:, h * 64:(h + 1) * 64],
                         start=(h < 2), stop=(h >= H - 2))
            for hlf in range(2):
                ins = e.matmul(P.pmisc[hlf * 32:hlf * 32 + 1, :], P.onesc[:, :],
                               P.kt[bj][:, hlf * 512:(hlf + 1) * 512],
                               start=True, stop=True)
            ins.then_inc(s["skv"], 1)
        if 0 <= a < NT and ALVL >= 3:
            e.wait_ge(s["sapp"], a + 1)
            if a >= 1:
                e.wait_ge(s["stev"], a)
            for jc in range(8):
                ins = e.transpose(P.ptr[:, jc * 128:(jc + 1) * 128],
                                  P.ln1tm[a % 2][:, jc * 128:(jc + 1) * 128], P.iden[:, :])
            ins.then_inc(s["strn"], 1)



def A2_vector(P, e):
    if ALVL < 50:
        return
    s = P.sem
    e.wait_ge(s["skva"], NT)
    vw(P, e)
    e.tensor_scalar(P.kvt[:, :], P.kvacc[(NT - 1) % 2][:, :], 0.0, None, ALU.add).then_inc(s["sa2"], 1)
    e.wait_ge(s["sa2"], 3)
    e.tensor_copy(P.ksc[:, :], P.pmisc[:, 64:72])
    vm(P, e.tensor_scalar(P.hb8[:, :], P.iota8f[:, :], 2.0, P.hpc[:, 0:1], ALU.mult, ALU.add))
    vw(P, e)
    for c in range(8):
        ins = e.tensor_scalar(P.msk16[:, c * 16:(c + 1) * 16], P.iota16f[:, :],
                              P.hb8[:, c:c + 1], None, ALU.is_equal)
    vm(P, ins)
    vw(P, e)
    for c in range(8):
        ins = e.tensor_scalar(P.KS[:, c * 16:(c + 1) * 16], P.msk16[:, c * 16:(c + 1) * 16],
                              P.ksc[:, c:c + 1], None, ALU.mult)
    ins.then_inc(s["sa2"], 1)


def A2_tensor(P, e):
    if ALVL < 50:
        return
    s = P.sem
    e.wait_ge(s["sa2"], 1)
    e.wait_ge(s["skva"], NT)
    ka = P.ksacc[(NT - 1) % 2]
    for jc in range(8):
        if jc < 4:
            srcrow, idn = ka[0:1, jc * 128:(jc + 1) * 128], P.idenf[0:1, 0:1]
        else:
            srcrow, idn = ka[32:33, (jc - 4) * 128:(jc - 3) * 128], P.idenf[32:33, 32:33]
        ins = e.transpose(P.pmisc[:, 64 + jc:65 + jc], srcrow, idn)
    ins.then_inc(s["sa2"], 2)


def B_sync(P, e):
    s = P.sem
    for st in range(NST):
        e.wait_ge(s["sstw0"], 16 * (2 * st + 2))
        e.wait_ge(s["sstw1"], 16 * (2 * st + 2))
        if st >= 2:
            e.wait_ge(s["sqp"], 8 * (st - 1))
        for jc in range(8):
            e.dma_start_transpose(P.ln1fB[st % 2][:, jc * 512:(jc + 1) * 512],
                                  P.ln1st[st * 512:(st + 1) * 512, jc * 128:(jc + 1) * 128]
                                  ).then_inc(s[f"slnf{st % 2}"], 16)
        e.wait_ge(s["szr"], st + 1)
        if st >= 1:
            e.wait_ge(s["szx"], 32 * st)
        e.dma_start(out=P.z_dram[:, :], in_=P.zr[:, :]).then_inc(s["szw"], 16)
        for t in range(4):
            i = st * 4 + t
            if i >= 2:
                e.wait_ge(s["sx2h"], 2 * (i - 1))
            e.dma_start(out=P.xb[i % 2][:, :], in_=P.x[i * 128:(i + 1) * 128, :]).then_inc(s[f"sxb{i % 2}"], 16)


def B_gpsimd(P, e):
    s = P.sem
    for st in range(NST):
        e.wait_ge(s["szw"], 16 * (st + 1))
        if st >= 1:
            e.wait_ge(s["sae"], 8 * st)
        dstv = P.zex[:, :].rearrange("p (c t) -> p c t", t=512)
        src_ev = P.z_dram[0::2, :].unsqueeze(0).to_broadcast([64, 8, 512])
        src_od = P.z_dram[1::2, :].unsqueeze(0).to_broadcast([64, 8, 512])
        e.dma_start(out=dstv[0:64, :, :], in_=src_ev).then_inc(s["szx"], 16)
        e.dma_start(out=dstv[64:128, :, :], in_=src_od).then_inc(s["szx"], 16)


def B_tensor(P, e):
    s = P.sem
    e.wait_ge(s["swt"], P.swt_c)
    e.wait_ge(s["swtc"], 6)
    e.wait_ge(s["sa2"], 4)
    for st in range(NST):
        b = st % 2
        e.wait_ge(s[f"slnf{st % 2}"], 128 * (st // 2 + 1))
        for ic in range(8):
            g = 8 * st + ic
            pq = P.pk[ic % 2]
            if g >= 2:
                e.wait_ge(s["sqe"], g - 1)
            for jc in range(8):
                ins = e.matmul(pq[:, :], P.wq_sb[:, jc * D + ic * 128: jc * D + ic * 128 + 128],
                               P.ln1fB[b][:, jc * 512:(jc + 1) * 512],
                               start=(jc == 0), stop=(jc == 7))
            ins.then_inc(s["sqp"], 1)
        e.wait_ge(s["sqe"], 8 * (st + 1))
        if st >= 1:
            e.wait_ge(s["szs"], st)
        for jc in range(8):
            ins = e.matmul(P.pmisc[0:16, :], P.KS[:, jc * 16:(jc + 1) * 16],
                           P.q0[:, jc * 512:(jc + 1) * 512], start=(jc == 0), stop=(jc == 7))
        ins.then_inc(s["szp"], 1)
        for c in range(8):
            g = 8 * st + c
            pa = P.pv[c % 2]
            if g >= 2:
                e.wait_ge(s["sae"], g - 1)
            for hh in range(2):
                h = 2 * c + hh
                ins = e.matmul(pa[hh * 64:hh * 64 + 64, :],
                               P.kvt[hh * 64:hh * 64 + 64, c * 64:(c + 1) * 64],
                               P.q0[hh * 64:hh * 64 + 64, c * 512:(c + 1) * 512],
                               start=True, stop=True)
            ins.then_inc(s["sat"], 1)
        e.wait_ge(s["sae"], 8 * (st + 1))
        for t in range(4):
            i = st * 4 + t
            for hlf in range(2):
                gg = 2 * i + hlf
                if gg >= 1:
                    e.wait_ge(s["sx2h"], gg)
                for mc in range(8):
                    e.matmul(P.pkv[:, :], P.atf[:, mc * 512 + t * 128: mc * 512 + t * 128 + 128],
                             P.wo_sb[:, mc * D + hlf * 512: mc * D + hlf * 512 + 512],
                             start=(mc == 0), stop=False)
                e.matmul(P.pkv[:, :], P.ones1[:, :], P.bor[0:1, hlf * 512:(hlf + 1) * 512],
                         start=False, stop=True).then_inc(s["sop"], 1)


def B_vector(P, e):
    s = P.sem
    for st in range(NST):
        for ic in range(8):
            g = 8 * st + ic
            pq = P.pk[ic % 2]
            e.wait_ge(s["sqp"], g + 1)
            if g >= 1:
                e.wait_ge(s["sqe"], g)
            e.tensor_scalar(P.ucl[ic % 2][:, :], pq[:, :], P.bqcs[:, ic:ic + 1], 0.0,
                            ALU.add, ALU.min).then_inc(s["smn"], 1)
            vm(P, e.tensor_scalar(P.tcl[0][:, :], pq[:, :], P.bqcs[:, ic:ic + 1], 0.0,
                                  ALU.add, ALU.max))
            e.wait_ge(s["suw"], NT + g + 1)
            vw(P, e)
            e.tensor_tensor(P.q0[:, ic * 512:(ic + 1) * 512], P.tcl[0][:, :],
                            P.wcl[ic % 2][:, :], ALU.add).then_inc(s["sqe"], 1)
        e.wait_ge(s["szp"], st + 1)
        e.tensor_scalar(P.zsb[:, :], P.pmisc[0:16, :], 1e-6, None, ALU.add).then_inc(s["szs"], 1)
        for c in range(8):
            g = 8 * st + c
            pa = P.pv[c % 2]
            e.wait_ge(s["sat"], g + 1)
            e.wait_ge(s["szx"], 32 * (st + 1))
            e.tensor_tensor(P.atf[:, c * 512:(c + 1) * 512], pa[:, :],
                            P.zex[:, c * 512:(c + 1) * 512], ALU.mult).then_inc(s["sae"], 1)
        for t in range(4):
            i = st * 4 + t
            bi = i % 2
            e.wait_ge(s[f"sxb{i % 2}"], 16 * (i // 2 + 1))
            if i >= 2:
                e.wait_ge(s[f"sx2w{i % 2}"], 16 * (i // 2))
                e.wait_ge(s["sln2a"], i - 1)
            for hlf in range(2):
                e.wait_ge(s["sop"], 2 * i + hlf + 1)
                e.tensor_tensor(P.x2sb[bi][:, hlf * 512:(hlf + 1) * 512], P.pkv[:, :],
                                P.xb[bi][:, hlf * 512:(hlf + 1) * 512], ALU.add
                                ).then_inc(s["sx2h"], 1)
            e.wait_ge(s["sx2h"], 2 * (i + 1))
            e.bn_stats(P.st6[bi][:, 0:6], P.x2sb[bi][:, 0:512])
            vm(P, e.bn_stats(P.st6[bi][:, 6:12], P.x2sb[bi][:, 512:1024]))
            vw(P, e)
            vm(P, e.bn_aggr(P.st2[bi][:, :], P.st6[bi][:, :]))
            vw(P, e)
            e.tensor_scalar(P.vep[bi][:, :], P.st2[bi][:, 1:2], EPS, None, ALU.add).then_inc(s["svst"], 1)
            e.wait_ge(s["srst"], NT + i + 1)
            if i >= 2:
                e.wait_ge(s[f"sln2w{i % 2}"], 16 * (i // 2))
            e.tensor_scalar(P.ln2tm[bi][:, :], P.x2sb[bi][:, :], P.st2[bi][:, 0:1],
                            P.rstd[bi][:, 0:1], ALU.subtract, ALU.mult).then_inc(s["sln2a"], 1)


def B_scalar(P, e):
    s = P.sem
    for st in range(NST):
        for ic in range(8):
            g = 8 * st + ic
            e.wait_ge(s["smn"], NT + g + 1)
            if g >= 2:
                e.wait_ge(s["sqe"], g - 1)
            e.activation(P.wcl[ic % 2][:, :], P.ucl[ic % 2][:, :], AF.Exp).then_inc(s["suw"], 1)
        e.wait_ge(s["szs"], st + 1)
        sm(P, e.activation(P.zl[:, :], P.zsb[:, :], AF.Ln))
        sw(P, e)
        e.activation(P.zr[:, :], P.zl[:, :], AF.Exp, scale=-1.0).then_inc(s["szr"], 1)
        for t in range(4):
            i = st * 4 + t
            bi = i % 2
            e.wait_ge(s["svst"], NT + i + 1)
            sm(P, e.activation(P.lnv[bi][:, :], P.vep[bi][:, :], AF.Ln))
            e.wait_ge(s["sx2h"], 2 * (i + 1))
            e.dma_start(out=P.o_x2[i * 128:(i + 1) * 128, :], in_=P.x2sb[bi][:, :]).then_inc(s[f"sx2w{i % 2}"], 16)
            sw(P, e)
            e.activation(P.rstd[bi][:, :], P.lnv[bi][:, :], AF.Exp, scale=-0.5).then_inc(s["srst"], 1)
            e.wait_ge(s["sln2a"], i + 1)
            e.dma_start(out=P.o_ln2[i * 128:(i + 1) * 128, :], in_=P.ln2tm[bi][:, :]).then_inc(s[f"sln2w{i % 2}"], 16)


def C0_sync(P, e):
    s = P.sem
    for st in range(NST):
        e.wait_ge(s["sln2w0"], 16 * 2 * (st + 1))
        e.wait_ge(s["sln2w1"], 16 * 2 * (st + 1))
        if st >= 2:
            e.wait_ge(s["sct"], st - 1)
        for jc in range(8):
            e.dma_start_transpose(P.ln2f[st % 2][:, jc * 512:(jc + 1) * 512],
                                  P.o_ln2[st * 512:(st + 1) * 512, jc * 128:(jc + 1) * 128]
                                  ).then_inc(s[f"slnc{st % 2}"], 16)


def C0_tensor(P, e):
    s = P.sem
    for st in range(NST):
        b = st % 2
        e.wait_ge(s[f"slnc{st % 2}"], 128 * (st // 2 + 1))
        if st >= 1:
            e.wait_ge(s["slgc"], st)
        for t in range(4):
            for jc in range(8):
                ins = e.matmul(P.pk[0][:, t * 8:(t + 1) * 8],
                               P.ln2f[b][:, jc * 512 + t * 128: jc * 512 + t * 128 + 128],
                               P.wg_sb[:, jc * 8:(jc + 1) * 8],
                               start=(jc == 0), stop=(jc == 7))
        ins.then_inc(s["sct"], 1)
        for t in range(4):
            i = st * 4 + t
            e.wait_ge(s["soh"], 4 * (st + 1))
            e.wait_ge(s["srun"], i)
            if i >= 4:
                e.wait_ge(s["ssel"], 4 * st)
            sl = P.pv[0][:, (i % 4) * 8:(i % 4) * 8 + 8]
            e.matmul(sl, P.ones1f[:, :], P.runrow_f[0:1, :], start=True, stop=False)
            e.matmul(sl, P.SU[:, :], P.ohb[:, (st % 2) * 32 + t * 8:(st % 2) * 32 + (t + 1) * 8], start=False, stop=True)
            e.matmul(P.pkv2[32:33, 0:8], P.onesc[:, :], P.ohb[:, (st % 2) * 32 + t * 8:(st % 2) * 32 + (t + 1) * 8],
                     start=(i == 0), stop=(i == NT - 1)).then_inc(s["srk"], 1)


def C0_vector(P, e):
    s = P.sem
    for st in range(NST):
        e.wait_ge(s["sct"], st + 1)
        if st >= 1:
            e.wait_ge(s["sgx"], st)
        vm(P, e.tensor_copy(P.lg[:, :], P.pk[0][:, 0:32]))
        vw(P, e)
        for t in range(4):
            ins = e.reduce_max(P.mx8[:, t:t + 1], P.lg[:, t * 8:(t + 1) * 8], AX.X)
        vm(P, ins)
        vw(P, e)
        for t in range(4):
            ins = e.tensor_scalar(P.smx[:, t * 8:(t + 1) * 8], P.lg[:, t * 8:(t + 1) * 8],
                                  P.mx8[:, t:t + 1], None, ALU.subtract)
            if t == 3:
                ins.then_inc(s["sgd"], 1)
        e.wait_ge(s["sgd"], st + 1)
        for t in range(4):
            ins = e.tensor_scalar(P.eqm[:, t * 8:(t + 1) * 8], P.smx[:, t * 8:(t + 1) * 8],
                                  -1e-30, None, ALU.is_ge)
            if t == 0:
                ins.then_inc(s["slgc"], 1)
        vm(P, ins)
        vw(P, e)
        for t in range(4):
            ins = e.scalar_tensor_tensor(P.cnd[:, t * 8:(t + 1) * 8], P.eqm[:, t * 8:(t + 1) * 8],
                                         -16384.0, P.iota8B[:, :], ALU.mult, ALU.add)
        vm(P, ins)
        vw(P, e)
        for t in range(4):
            ins = e.tensor_reduce(P.eixT[:, st * 4 + t: st * 4 + t + 1], P.cnd[:, t * 8:(t + 1) * 8],
                                  AX.X, ALU.min)
        vm(P, ins)
        vw(P, e)
        if st >= 2:
            e.wait_ge(s["srk"], 4 * (st - 1))
        ohof = (st % 2) * 32
        for t in range(4):
            ins = e.tensor_scalar(P.ohb[:, ohof + t * 8:ohof + (t + 1) * 8], P.iota8f[:, :],
                                  P.eixT[:, st * 4 + t: st * 4 + t + 1], None, ALU.is_equal)
        ins.then_inc(s["soh"], 4)
        e.wait_ge(s["sgx"], st + 1)
        for t in range(4):
            ins = e.reduce_sum(P.ssum[:, t:t + 1], P.sme[:, t * 8:(t + 1) * 8], AX.X)
        vm(P, ins)
        vw(P, e)
        e.reciprocal(P.gvT[:, st * 4:(st + 1) * 4], P.ssum[:, 0:4])
        for t in range(4):
            i = st * 4 + t
            e.wait_ge(s["srk"], i + 1)
            ins = e.tensor_tensor(P.rl[:, t * 8:(t + 1) * 8], P.pv[0][:, (i % 4) * 8:(i % 4) * 8 + 8],
                                  P.ohb[:, (st % 2) * 32 + t * 8:(st % 2) * 32 + (t + 1) * 8], ALU.mult)
            if t == 3:
                vm(P, ins)
            e.tensor_copy(P.runrow_f[0:1, :], P.pkv2[32:33, 0:8]).then_inc(s["srun"], 1)
        vw(P, e)
        for t in range(4):
            ins = e.reduce_sum(P.rnkT[:, st * 4 + t: st * 4 + t + 1], P.rl[:, t * 8:(t + 1) * 8], AX.X)
        ins.then_inc(s["ssel"], 4)
    e.wait_ge(s["srun"], NT)
    e.tensor_copy(P.tot_sb[0:1, :], P.pkv2[32:33, 0:8]).then_inc(s["sfin"], 1)


def C0_scalar(P, e):
    s = P.sem
    for st in range(NST):
        e.wait_ge(s["sgd"], st + 1)
        e.activation(P.sme[:, :], P.smx[:, :], AF.Exp).then_inc(s["sgx"], 1)


def final_gpsimd(P, e):
    s = P.sem
    e.wait_ge(s["sfin"], 1)
    e.wait_ge(s["ssel"], NT)
    n = P.swt_c
    e.dma_start(out=P.o_cnt[:, :], in_=P.tot_sb[0:1, :]).then_inc(s["swt"], 16)
    e.dma_start(out=P.o_rank[:, :], in_=P.rnkT[:, :]).then_inc(s["swt"], 16)
    e.dma_start(out=P.o_eidx[:, :], in_=P.eixT[:, :]).then_inc(s["swt"], 16)
    e.dma_start(out=P.o_gv[:, :], in_=P.gvT[:, :]).then_inc(s["swt"], 16)
    e.wait_ge(s["swt"], n + 64)
    e.wait_ge(s["sx2w0"], 16 * NT // 2)
    e.wait_ge(s["sx2w1"], 16 * NT // 2)
    e.wait_ge(s["sln2w0"], 16 * NT // 2)
    e.wait_ge(s["sln2w1"], 16 * NT // 2)
    e.wait_ge(s["swc"], 16 * 16)


def emit_A(P, phases=("A", "B", "C0")):
    nc = P.nc
    pA = "A" in phases; pB = "B" in phases; pC = "C0" in phases
    with P.es:
        block = P.es.enter_context(nc.Block())

        @block.sync
        def _(e):
            if pA: A_sync(P, e)
            if pB: B_sync(P, e)
            if pC: C0_sync(P, e)

        @block.gpsimd
        def _(e):
            A_gpsimd_consts(P, e)
            if pB: B_gpsimd(P, e)
            if pC:
                final_gpsimd(P, e)
            elif pB:
                e.wait_ge(P.sem["sx2w0"], 16 * NT // 2)
                e.wait_ge(P.sem["sx2w1"], 16 * NT // 2)
                e.wait_ge(P.sem["sln2w0"], 16 * NT // 2)
                e.wait_ge(P.sem["sln2w1"], 16 * NT // 2)
            else:
                if ALVL >= 3:
                    e.wait_ge(P.sem["sstw0"], 16 * NT // 2)
                    e.wait_ge(P.sem["sstw1"], 16 * NT // 2)
                if ALVL >= 4:
                    e.wait_ge(P.sem["selu"], NT)
                if ALVL >= 50:
                    e.wait_ge(P.sem["skva"], NT)
                if ALVL < 3:
                    e.wait_ge(P.sem["srst"], NT)

        @block.vector
        def _(e):
            A_vec_consts(P, e)
            if pA: A_vector(P, e)
            if pA: A2_vector(P, e)
            if pB: B_vector(P, e)
            if pC: C0_vector(P, e)

        @block.scalar
        def _(e):
            if pA: A_scalar(P, e)
            if pB: B_scalar(P, e)
            if pC: C0_scalar(P, e)

        @block.tensor
        def _(e):
            import os as _os
            import contextlib as _ctx
            _sc = (lambda n: P.nc.named_scope(n)) if _os.environ.get("MK_SCOPE") else (lambda n: _ctx.nullcontext())
            with _sc("Apet"):
                if pA: A_tensor(P, e)
            with _sc("A2pe"):
                if pA: A2_tensor(P, e)
            with _sc("Bpet"):
                if pB: B_tensor(P, e)
            with _sc("C0pe"):
                if pC: C0_tensor(P, e)
    return nc


def host_prep_A(inp):
    Wq = np.asarray(inp["L0_attn_Wq"]); Wk = np.asarray(inp["L0_attn_Wk"])
    Wv = np.asarray(inp["L0_attn_Wv"]); Wo = np.asarray(inp["L0_attn_Wo"])
    base = {
        "wqt": np.ascontiguousarray(Wq.T), "wkt": np.ascontiguousarray(Wk.T),
        "wvt": np.ascontiguousarray(Wv.T), "wot": np.ascontiguousarray(Wo.T),
        "bk": np.asarray(inp["L0_attn_bk"]).reshape(1, D),
        "bv": np.asarray(inp["L0_attn_bv"]).reshape(1, D),
        "bo": np.asarray(inp["L0_attn_bo"]).reshape(1, D),
        "bqc": np.ascontiguousarray(np.asarray(inp["L0_attn_bq"]).reshape(8, 128).T),
        "wg": np.ascontiguousarray(np.asarray(inp["L0_ffn_wg"])),
        "w1r": np.ascontiguousarray(np.asarray(inp["L0_ffn_w1"]).reshape(8, 8, 128, DH).transpose(0, 2, 1, 3)).reshape(1, -1),
        "w2r": np.ascontiguousarray(np.asarray(inp["L0_ffn_w2"]).reshape(8, 32, 128, 1024).transpose(0, 2, 1, 3)).reshape(1, -1),
    }
    x = np.asarray(inp["L0_attn_hidden_states"])
    return [dict(base, x=np.ascontiguousarray(x[c])) for c in range(8)]


GRP = 256
NGRP = 40
NSLOT = NGRP * GRP
W1SL = 128 * 8 * 512
W2SL = 128 * 4 * 1024
SENT = 1 << 30


def build_B():
    nc = bass.Bass("TRN2", target_bir_lowering=False)
    P = P_()
    P.nc = nc
    P.cnt8 = nc.dram_tensor("cnt8", [8, 8], F32, kind="ExternalInput").ap()
    P.rmsk = nc.dram_tensor("rmsk", [8, 1], F32, kind="ExternalInput").ap()
    P.mhot = nc.dram_tensor("mhot", [8, 1], F32, kind="ExternalInput").ap()
    P.x2d = nc.dram_tensor("x2d", [S, D], F32, kind="ExternalInput").ap()
    P.ln2d = nc.dram_tensor("ln2d", [S, D], BF16, kind="ExternalInput").ap()
    P.rnkd = nc.dram_tensor("rnkd", [128, NT], F32, kind="ExternalInput").ap()
    P.eixd = nc.dram_tensor("eixd", [128, NT], F32, kind="ExternalInput").ap()
    P.gvd = nc.dram_tensor("gvd", [128, NT], F32, kind="ExternalInput").ap()
    P.w1c = nc.dram_tensor("w1c", [1, 8 * 128 * 8 * DH], BF16, kind="ExternalInput").ap()
    P.w2c = nc.dram_tensor("w2c", [1, 8 * 128 * 32 * 1024], BF16, kind="ExternalInput").ap()
    P.b1r = nc.dram_tensor("b1r", [8 * 128, 32], F32, kind="ExternalInput").ap()
    P.b2d = nc.dram_tensor("b2d", [8, 1024], F32, kind="ExternalInput").ap()
    P.y = nc.dram_tensor("y", [S, D], F32, kind="ExternalOutput").ap()
    P.dispd = nc.dram_tensor("dispd", [NSLOT, D], BF16).ap()
    P.eod = nc.dram_tensor("eod", [NSLOT, D], BF16).ap()

    es = ExitStack()
    P.es = es
    sb = lambda name, shape, dt: es.enter_context(nc.sbuf_tensor(name, shape, dt))
    ps = lambda name, shape, dt: es.enter_context(nc.psum_tensor(name, shape, dt))
    P.w1sb = sb("w1sb", [128, 8 * DH], BF16)
    P.w2sb = sb("w2sb", [128, 32 * 1024], BF16)
    P.hfm = sb("hfm", [128, 32 * GRP], BF16)
    P.dspf = [sb(f"dspf{b}", [128, 8 * GRP], BF16) for b in range(2)]
    P.b1g = [sb(f"b1g{b}", [128, 32], F32) for b in range(2)]
    P.b2sb = sb("b2sb", [8, 1024], BF16)
    P.eosb = [sb(f"eosb{b}", [128, D], BF16) for b in range(2)]
    P.cnts = sb("cnts", [8, 8], F32)
    P.rmsks = sb("rmsks", [8, 1], F32)
    P.mhots = sb("mhots", [8, 1], F32)
    P.rnks = sb("rnks", [128, NT], F32)
    P.eixs = sb("eixs", [128, NT], F32)
    P.gvs = sb("gvs", [128, NT], F32)
    P.gvmT = sb("gvmT", [128, NT], F32)
    P.slotT = sb("slotT", [128, NT], I32)
    P.slotF = sb("slotF", [128, NT], F32)
    P.coffr = sb("coffr", [1, 8], F32)
    P.mycr = sb("mycr", [1, 8], F32)
    P.remr = sb("remr", [1, 8], F32)
    P.kptr = sb("kptr", [1, 8], F32)
    P.ntgi = sb("ntgi", [1, 8], I32)
    P.t8i = sb("t8i", [1, 8], I32)
    P.t8i2 = sb("t8i2", [1, 8], I32)
    P.ntgr = sb("ntgr", [1, 8], F32)
    P.ntgc = sb("ntgc", [8, 1], F32)
    P.gsr = sb("gsr", [1, 8], F32)
    P.gs256r = sb("gs256r", [1, 8], F32)
    P.t8a = sb("t8a", [1, 8], F32)
    P.t8b = sb("t8b", [1, 8], F32)
    P.Erow = sb("Erow", [1, NGRP], F32)
    P.swf = sb("swf", [1, NGRP], F32)
    P.swsent = sb("swsent", [1, NGRP], F32)

    P.wof_f = sb("wof_f", [1, NGRP], F32)
    P.wof = sb("wof", [1, NGRP], I32)
    P.t40a = sb("t40a", [1, NGRP], F32)
    P.t40b = sb("t40b", [1, NGRP], F32)
    P.iota40 = sb("iota40", [1, NGRP], I32)
    P.iota40f = sb("iota40f", [1, NGRP], F32)
    P.iexb = sb("iexb", [128, NGRP], I32)
    P.iexf = sb("iexf", [128, NGRP], F32)
    P.ohgall = sb("ohgall", [8, NGRP], BF16)
    P.ohgf = sb("ohgf", [8, NGRP], F32)
    P.iotaP = sb("iotaP", [128, 1], I32)
    P.iotaPf = sb("iotaPf", [128, 1], F32)
    P.iota8 = sb("iota8", [128, 8], I32)
    P.iota8f = sb("iota8f", [128, 8], F32)
    P.iotaP8 = sb("iotaP8", [8, NGRP], I32)
    P.iotaP8f = sb("iotaP8f", [8, NGRP], F32)
    P.su8 = sb("su8", [8, 8], F32)
    P.su_i32 = sb("su_i32", [128, 128], I32)
    P.ones1f = sb("ones1f", [1, 128], F32)
    P.idenf = sb("idenf", [128, 128], F32)
    P.bigc = sb("bigc", [128, 1], F32)
    P.ohD = sb("ohD", [128, 8], F32)
    P.cselc = sb("cselc", [128, 8], F32)
    P.gselc = sb("gselc", [128, 8], F32)
    P.csel = sb("csel", [128, 1], F32)
    P.gsel = sb("gsel", [128, 1], F32)
    P.globc = sb("globc", [128, 1], F32)
    P.keepc = sb("keepc", [128, 1], F32)
    P.keepi = sb("keepi", [128, 1], I32)
    P.sl0 = sb("sl0", [128, 1], F32)
    P.coffb = sb("coffb", [128, 8], F32)
    P.gs256b = sb("gs256b", [128, 8], F32)
    P.zt = sb("zt", [128, D], BF16)
    P.ln2t = [sb(f"ln2t{b}", [128, D], BF16) for b in range(2)]
    P.x2t = [sb(f"x2t{b}", [128, D], F32) for b in range(2)]
    P.gath = [sb(f"gath{b}", [128, D], BF16) for b in range(2)]
    P.osb = [sb(f"osb{b}", [128, D], F32) for b in range(2)]
    P.ph = [ps(f"ph{b}", [128, GRP], F32) for b in range(2)]
    P.peo = [ps(f"peo{b}", [128, 512], F32) for b in range(2)]
    P.pmb = ps("pmb", [128, 512], F32)
    P.sem = {}
    for n in ["sin", "sioc", "sc1", "sc1b", "sslot", "sdl0", "sdl1", "ssc0", "ssc1",
              "swl", "smm", "sdf0", "sdf1", "sb1_0", "sb1_1", "sh", "sho",
              "seo", "sev", "seow0", "seow1", "sga0", "sga1", "szg", "sx2l0", "sx2l1", "sout",
              "syw0", "syw1", "svo", "szt", "sztd"]:
        P.sem[n] = es.enter_context(nc.semaphore(n))
    P.vn = 0
    return P


def Bl_gpsimd(P, e):
    s = P.sem
    n = 0
    def dma(dst, src):
        nonlocal n
        e.dma_start(out=dst, in_=src).then_inc(s["sin"], 16)
        n += 16
    dma(P.cnts[:, :], P.cnt8[:, :])
    dma(P.rmsks[:, :], P.rmsk[:, :])
    dma(P.mhots[:, :], P.mhot[:, :])
    dma(P.rnks[:, :], P.rnkd[:, :])
    dma(P.eixs[:, :], P.eixd[:, :])
    dma(P.gvs[:, :], P.gvd[:, :])
    dma(P.b2sb[:, :], P.b2d[:, :])
    P.sin_c = n
    e.iota(P.iota40[:, :], [[1, NGRP]], base=0, channel_multiplier=0).then_inc(s["sioc"], 1)
    e.iota(P.iotaP[:, :], [[1, 1]], base=0, channel_multiplier=1).then_inc(s["sioc"], 1)
    e.iota(P.iota8[:, :], [[1, 8]], base=0, channel_multiplier=0).then_inc(s["sioc"], 1)
    e.iota(P.iotaP8[:, :], [[0, NGRP]], base=0, channel_multiplier=1).then_inc(s["sioc"], 1)
    e.iota(P.su_i32[:, :], [[1, 128]], base=0, channel_multiplier=-1).then_inc(s["sioc"], 1)
    e.memset(P.gath[0][:, :], 0.0)
    e.memset(P.gath[1][:, :], 0.0).then_inc(s["szg"], 1)
    e.wait_ge(s["szt"], 1)
    e.dma_start(out=P.dispd[:, :].rearrange("(r p) d -> p r d", p=128),
                in_=P.zt[:, :].unsqueeze(1).to_broadcast([128, NSLOT // 128, D])
                ).then_inc(s["sztd"], 16)
    e.wait_ge(s["sztd"], 16)
    import os as _os0
    import contextlib as _ctx0
    _sc0 = (lambda n: P.nc.named_scope(n)) if _os0.environ.get("MK_SCOPE") else (lambda n: _ctx0.nullcontext())
    with e.register() as breg:
        e.reg_mov(breg, NSLOT - 1)
        bval = e.snap(breg)
        _g1 = _sc0("dsc"); _g1.__enter__()
        for i in range(NT):
            e.wait_ge(s["sslot"], i + 1)
            e.wait_ge(s[f"sdl{i % 2}"], 16 * (i // 2 + 1))
            if i >= 8:
                e.wait_ge(s[f"ssc{i % 2}"], 16 * ((i - 8) // 2 + 1))
            e.indirect_dma_start(
                P.dispd[:, :], IndirectOffsetOnAxis(P.slotT[:, i:i + 1], 0),
                P.ln2t[i % 2][:, :], None,
                bounds_check=bval, oob_is_err=False).then_inc(s[f"ssc{i % 2}"], 16)
    _g1.__exit__(None, None, None)
    with e.register() as breg1:
        e.reg_mov(breg1, 8 * 128 - 1)
        bval1 = e.snap(breg1)
        for g in range(NGRP):
            if g >= 2:
                e.wait_ge(s["sho"], 32 * (g - 1))
            e.indirect_dma_start(
                P.b1g[g % 2][:, :], None,
                P.b1r[:, :], IndirectOffsetOnAxis(P.iexb[:, g:g + 1], 0),
                bounds_check=bval1, oob_is_err=False).then_inc(s[f"sb1_{g % 2}"], 16)
    _g2 = _sc0("fga")
    _g2.__enter__()
    with e.register() as breg2:
      e.reg_mov(breg2, NSLOT - 1)
      bval2 = e.snap(breg2)
      for i in range(NT):
          e.wait_ge(s["seow0"], 16 * NGRP)
          e.wait_ge(s["seow1"], 16 * NGRP)
          if i >= 2:
                e.wait_ge(s["sout"], i - 1)
          e.indirect_dma_start(
                P.gath[i % 2][:, :], None,
                P.eod[:, :], IndirectOffsetOnAxis(P.slotT[:, i:i + 1], 0),
                bounds_check=bval2, oob_is_err=False).then_inc(s[f"sga{i % 2}"], 16)
    _g2.__exit__(None, None, None)
    e.wait_ge(s["syw0"], 16 * NT // 2)
    e.wait_ge(s["syw1"], 16 * NT // 2)


def Bl_sync(P, e):
    s = P.sem
    import os as _os
    import contextlib as _ctx
    _sc = (lambda n: P.nc.named_scope(n)) if _os.environ.get("MK_SCOPE") else (lambda n: _ctx.nullcontext())
    _s1 = _sc("l2l"); _s1.__enter__()
    for i in range(NT):
        if i >= 2:
            e.wait_ge(s[f"ssc{i % 2}"], 16 * ((i - 2) // 2 + 1))
        e.dma_start(out=P.ln2t[i % 2][:, :], in_=P.ln2d[i * 128:(i + 1) * 128, :]
                    ).then_inc(s[f"sdl{i % 2}"], 16)
    _s1.__exit__(None, None, None)
    _s2 = _sc("tld"); _s2.__enter__()
    for g in range(NGRP):
        e.wait_ge(s["ssc0"], 16 * NT // 2)
        e.wait_ge(s["ssc1"], 16 * NT // 2)
        if g >= 2:
            e.wait_ge(s["sh"], 32 * (g - 1))
        for mc in range(8):
            e.dma_start_transpose(
                P.dspf[g % 2][:, mc * GRP:(mc + 1) * GRP],
                P.dispd[g * GRP:(g + 1) * GRP, mc * 128:(mc + 1) * 128]
                ).then_inc(s[f"sdf{g % 2}"], 16)
    _s2.__exit__(None, None, None)
    _s3 = _sc("fxy"); _s3.__enter__()
    for i in range(NT + 2):
        if i < NT:
            if i >= 2:
                e.wait_ge(s["sout"], i - 1)
            e.dma_start(out=P.x2t[i % 2][:, :], in_=P.x2d[i * 128:(i + 1) * 128, :]
                        ).then_inc(s[f"sx2l{i % 2}"], 16)
        j = i - 2
        if j >= 0:
            e.wait_ge(s["sout"], j + 1)
            e.dma_start(out=P.y[j * 128:(j + 1) * 128, :], in_=P.osb[j % 2][:, :]
                        ).then_inc(s[f"syw{j % 2}"], 16)
    _s3.__exit__(None, None, None)


def Bl_tensor(P, e):
    s = P.sem
    e.wait_ge(s["sin"], P.sin_c)
    e.wait_ge(s["sc1"], 1)
    e.matmul(P.pmb[0:1, 0:8], P.rmsks[:, :], P.cnts[:, :], start=True, stop=True)
    e.matmul(P.pmb[32:33, 0:8], P.mhots[:, :], P.cnts[:, :], start=True, stop=True
             ).then_inc(s["sc1b"], 1)
    e.wait_ge(s["sc1"], 2)
    e.transpose(P.pmb[0:8, 8:9], P.ntgr[:, :], P.idenf[0:1, 0:1]).then_inc(s["sc1b"], 1)
    e.wait_ge(s["sc1"], 3)
    e.matmul(P.pmb[0:1, 16:24], P.ntgc[:, :], P.su8[:, :], start=True, stop=True
             ).then_inc(s["sc1b"], 1)
    e.wait_ge(s["sc1"], 4)
    e.matmul(P.pmb[:, 64:104], P.ones1f[:, :], P.Erow[:, :], start=True, stop=True)
    e.matmul(P.pmb[:, 104:112], P.ones1f[:, :], P.coffr[:, :], start=True, stop=True)
    e.matmul(P.pmb[:, 112:120], P.ones1f[:, :], P.gs256r[:, :], start=True, stop=True
             ).then_inc(s["sc1b"], 1)
    import os as _os
    import contextlib as _ctx
    _sc = (lambda n: P.nc.named_scope(n)) if _os.environ.get("MK_SCOPE") else (lambda n: _ctx.nullcontext())
    _e1 = _sc("eph"); _e1.__enter__()
    for g in range(NGRP):
      with (_sc(f"peg{g:02d}") if _os.environ.get("MK_SCOPE_PEG") else _ctx.nullcontext()):
        e.wait_ge(s["swl"], 32 * (g + 1))
        e.wait_ge(s[f"sdf{g % 2}"], 128 * (g // 2 + 1))
        for hc in range(32):
            ghc = 32 * g + hc
            if ghc >= 2:
                e.wait_ge(s["sho"], ghc - 1)
            for mcb in range(8):
                ins = e.matmul(P.ph[hc % 2][:, :],
                               P.w1sb[:, mcb * DH + hc * 128: mcb * DH + hc * 128 + 128],
                               P.dspf[g % 2][:, mcb * GRP:(mcb + 1) * GRP],
                               start=(mcb == 0), stop=(mcb == 7))
            ins.then_inc(s["sh"], 1)
        e.wait_ge(s["sho"], 32 * (g + 1))
        for cs in range(2):
            for hf in range(2):
                u = 4 * g + 2 * cs + hf
                if u >= 2:
                    e.wait_ge(s["sev"], u - 1)
                for hc in range(32):
                    e.matmul(P.peo[hf][:, :],
                             P.hfm[:, hc * GRP + cs * 128: hc * GRP + cs * 128 + 128],
                             P.w2sb[:, hc * 1024 + hf * 512: hc * 1024 + hf * 512 + 512],
                             start=(hc == 0), stop=False)
                ins = e.matmul(P.peo[hf][:, :],
                               P.ohgall[:, g:g + 1].to_broadcast([8, 128]),
                               P.b2sb[0:8, hf * 512:(hf + 1) * 512],
                               start=False, stop=True)
                ins.then_inc(s["seo"], 1)
    _e1.__exit__(None, None, None)


def Bl_vector(P, e):
    s = P.sem
    e.wait_ge(s["sin"], P.sin_c)
    e.wait_ge(s["sioc"], 5)
    e.memset(P.zt[:, :], 0.0).then_inc(s["szt"], 1)
    e.memset(P.ones1f[:, :], 1.0)
    e.memset(P.bigc[:, :], 1048576.0)
    e.memset(P.Erow[:, :], -1.0)
    e.memset(P.swf[:, :], 1.0)
    e.tensor_scalar(P.iota40f[:, :], P.iota40[:, :], 0.0, None, ALU.add)
    e.tensor_scalar(P.iotaPf[:, :], P.iotaP[:, :], 0.0, None, ALU.add)
    e.tensor_scalar(P.iota8f[:, :], P.iota8[:, :], 0.0, None, ALU.add)
    e.tensor_scalar(P.iotaP8f[:, :], P.iotaP8[:, :], 0.0, None, ALU.add)
    e.tensor_scalar(P.idenf[:, :], P.su_i32[:, :], 0.0, None, ALU.is_equal)
    vm(P, e.tensor_scalar(P.su8[:, :], P.su_i32[0:8, 0:8], 0.0, None, ALU.is_gt))
    vw(P, e)
    e.engine_nop().then_inc(s["sc1"], 1)
    e.wait_ge(s["sc1b"], 1)
    e.tensor_scalar(P.coffr[:, :], P.pmb[0:1, 0:8], 0.0, None, ALU.add)
    vm(P, e.tensor_scalar(P.mycr[:, :], P.pmb[32:33, 0:8], 0.0, None, ALU.add))
    vw(P, e)
    vm(P, e.tensor_scalar(P.remr[:, :], P.coffr[:, :], -1.0, 8192.0, ALU.mult, ALU.add))
    vw(P, e)
    vm(P, e.tensor_scalar(P.kptr[:, :], P.remr[:, :], 0.0, None, ALU.max))
    vw(P, e)
    vm(P, e.tensor_tensor(P.t8a[:, :], P.kptr[:, :], P.mycr[:, :], ALU.min))
    vw(P, e)
    vm(P, e.tensor_scalar(P.t8i[:, :], P.t8a[:, :], 0, None, ALU.add))
    vw(P, e)
    vm(P, e.tensor_scalar(P.t8i2[:, :], P.t8i[:, :], 255, None, ALU.add))
    vw(P, e)
    vm(P, e.tensor_scalar(P.ntgi[:, :], P.t8i2[:, :], 8, None, ALU.arith_shift_right))
    vw(P, e)
    vm(P, e.tensor_scalar(P.ntgr[:, :], P.ntgi[:, :], 0.0, None, ALU.add))
    vw(P, e)
    e.engine_nop().then_inc(s["sc1"], 1)
    e.wait_ge(s["sc1b"], 2)
    vm(P, e.tensor_scalar(P.ntgc[:, :], P.pmb[0:8, 8:9], 0.0, None, ALU.add))
    vw(P, e)
    e.engine_nop().then_inc(s["sc1"], 1)
    e.wait_ge(s["sc1b"], 3)
    vm(P, e.tensor_scalar(P.gsr[:, :], P.pmb[0:1, 16:24], 0.0, None, ALU.add))
    vw(P, e)
    vm(P, e.tensor_scalar(P.gs256r[:, :], P.gsr[:, :], 256.0, None, ALU.mult))
    vw(P, e)
    for ee in range(8):
        vm(P, e.tensor_scalar(P.t40a[:, :], P.iota40f[:, :], P.gsr[0:1, ee:ee + 1], None, ALU.is_ge))
        vw(P, e)
        vm(P, e.tensor_tensor(P.t40b[:, :], P.Erow[:, :], P.t40a[:, :], ALU.add))
        vw(P, e)
        vm(P, e.tensor_scalar(P.Erow[:, :], P.t40b[:, :], 0.0, None, ALU.add))
        vw(P, e)
    vm(P, e.tensor_tensor(P.swf[0:1, 1:NGRP], P.Erow[0:1, 1:NGRP], P.Erow[0:1, 0:NGRP - 1],
                          ALU.not_equal))
    vw(P, e)
    vm(P, e.tensor_scalar(P.swsent[:, :], P.swf[:, :], float(-SENT), float(SENT),
                          ALU.mult, ALU.add))
    vw(P, e)
    vm(P, e.tensor_scalar(P.t40a[:, :], P.Erow[:, :], float(128 * 8 * DH), None, ALU.mult))
    vw(P, e)
    vm(P, e.tensor_tensor(P.wof_f[:, :], P.t40a[:, :], P.swsent[:, :], ALU.add))
    vw(P, e)
    vm(P, e.tensor_scalar(P.wof[:, :], P.wof_f[:, :], 0.0, None, ALU.add))
    vw(P, e)

    e.engine_nop().then_inc(s["sc1"], 1)
    e.wait_ge(s["sc1b"], 4)
    vm(P, e.tensor_scalar(P.iexf[:, :], P.pmb[:, 64:104], 128.0, P.iotaPf[:, 0:1],
                          ALU.mult, ALU.add))
    vw(P, e)
    e.tensor_scalar(P.iexb[:, :], P.iexf[:, :], 0.0, None, ALU.add)

    vm(P, e.tensor_scalar(P.ohgf[:, :], P.pmb[0:8, 64:104], 0.0, None, ALU.add))
    vw(P, e)
    e.tensor_tensor(P.ohgall[:, :], P.iotaP8f[:, :], P.ohgf[:, :], ALU.is_equal)
    e.tensor_scalar(P.coffb[:, :], P.pmb[:, 104:112], 0.0, None, ALU.add)
    vm(P, e.tensor_scalar(P.gs256b[:, :], P.pmb[:, 112:120], 0.0, None, ALU.add))
    vw(P, e)
    import os as _osv
    import contextlib as _ctxv
    _scv = (lambda n: P.nc.named_scope(n)) if _osv.environ.get("MK_SCOPE") else (lambda n: _ctxv.nullcontext())
    _v1 = _scv("slt"); _v1.__enter__()
    for i in range(NT):
        vm(P, e.tensor_scalar(P.ohD[:, :], P.iota8f[:, :], P.eixs[:, i:i + 1], None, ALU.is_equal))
        vw(P, e)
        e.tensor_tensor(P.cselc[:, :], P.ohD[:, :], P.coffb[:, :], ALU.mult)
        vm(P, e.tensor_tensor(P.gselc[:, :], P.ohD[:, :], P.gs256b[:, :], ALU.mult))
        vw(P, e)
        e.reduce_sum(P.csel[:, :], P.cselc[:, :], AX.X)
        vm(P, e.reduce_sum(P.gsel[:, :], P.gselc[:, :], AX.X))
        vw(P, e)
        e.tensor_tensor(P.globc[:, :], P.rnks[:, i:i + 1], P.csel[:, :], ALU.add)
        vm(P, e.tensor_tensor(P.sl0[:, :], P.rnks[:, i:i + 1], P.gsel[:, :], ALU.add))
        vw(P, e)
        vm(P, e.tensor_scalar(P.keepc[:, :], P.globc[:, :], 8192.0, None, ALU.is_lt))
        vw(P, e)
        e.tensor_scalar(P.keepi[:, :], P.globc[:, :], 8192.0, None, ALU.is_lt)
        vm(P, e.tensor_tensor(P.gvmT[:, i:i + 1], P.gvs[:, i:i + 1], P.keepc[:, :], ALU.mult))
        vw(P, e)
        vm(P, e.select(P.slotF[:, i:i + 1], P.keepi[:, :], P.sl0[:, :], P.bigc[:, :], add_drain=True))
        vw(P, e)
        e.tensor_scalar(P.slotT[:, i:i + 1], P.slotF[:, i:i + 1], 0.0, None,
                        ALU.add).then_inc(s["sslot"], 1)
    _v1.__exit__(None, None, None)
    for g in range(NGRP):
        e.wait_ge(s[f"sb1_{g % 2}"], 16 * (g // 2 + 1))
        for hc in range(32):
            e.wait_ge(s["sh"], 32 * g + hc + 1)
            e.tensor_scalar(P.hfm[:, hc * GRP:(hc + 1) * GRP], P.ph[hc % 2][:, :],
                            P.b1g[g % 2][:, hc:hc + 1], 0.0, ALU.add, ALU.max
                            ).then_inc(s["sho"], 1)
        for cs in range(2):
            if g >= 1:
                e.wait_ge(s[f"seow{cs}"], 16 * g)
            for hf in range(2):
                u = 4 * g + 2 * cs + hf
                e.wait_ge(s["seo"], u + 1)
                e.tensor_scalar(P.eosb[cs][:, hf * 512:(hf + 1) * 512], P.peo[hf][:, :],
                                0.0, None, ALU.add).then_inc(s["sev"], 1)
    _v2 = _scv("cmb"); _v2.__enter__()
    for i in range(NT):
        if i < 2:
            e.wait_ge(s["szg"], 1)
        e.wait_ge(s[f"sga{i % 2}"], 16 * (i // 2 + 1))
        e.wait_ge(s[f"sx2l{i % 2}"], 16 * (i // 2 + 1))
        if i >= 2:
            e.wait_ge(s[f"syw{i % 2}"], 16 * (i // 2))
        e.scalar_tensor_tensor(P.osb[i % 2][:, :], P.gath[i % 2][:, :], P.gvmT[:, i:i + 1],
                               P.x2t[i % 2][:, :], ALU.mult, ALU.add).then_inc(s["sout"], 1)
    _v2.__exit__(None, None, None)


def Bl_scalar(P, e):
    s = P.sem
    EXPSZ = 128 * 8 * DH
    e.wait_ge(s["sslot"], 1)
    for g in range(NGRP):
        if g >= 1:
            e.wait_ge(s["sev"], 4 * g)
        if g >= 2:
            e.wait_ge(s["swl"], 32 * (g - 1))
        with e.register() as rg:
            e.reg_load(rg, P.wof[0:1, g:g + 1])
            val = e.snap(rg, min_val=0, max_val=7 * EXPSZ, donate=True)
            e.dma_start(P.w1sb[:, :], P.w1c[0:1, bass.ds(val, EXPSZ)],
                        bounds_check="skip_entire_dma").then_inc(s["swl"], 16)
            e.dma_start(P.w2sb[:, :], P.w2c[0:1, bass.ds(val, EXPSZ)],
                        bounds_check="skip_entire_dma").then_inc(s["swl"], 16)
        if g >= 1:
            for cs in range(2):
                e.wait_ge(s["sev"], 4 * (g - 1) + 2 * (cs + 1))
                e.dma_start(out=P.eod[(g - 1) * GRP + cs * 128: (g - 1) * GRP + (cs + 1) * 128, :],
                            in_=P.eosb[cs][:, :]).then_inc(s[f"seow{cs}"], 16)
    g = NGRP - 1
    for cs in range(2):
        e.wait_ge(s["sev"], 4 * g + 2 * (cs + 1))
        e.dma_start(out=P.eod[g * GRP + cs * 128: g * GRP + (cs + 1) * 128, :],
                    in_=P.eosb[cs][:, :]).then_inc(s[f"seow{cs}"], 16)


def emit_B(P):
    nc = P.nc
    with P.es:
        block = P.es.enter_context(nc.Block())

        @block.sync
        def _(e):
            Bl_sync(P, e)

        @block.gpsimd
        def _(e):
            Bl_gpsimd(P, e)

        @block.vector
        def _(e):
            Bl_vector(P, e)

        @block.scalar
        def _(e):
            Bl_scalar(P, e)

        @block.tensor
        def _(e):
            Bl_tensor(P, e)
    return nc


def host_prep_B(inp, a_results):
    w1 = np.asarray(inp["L0_ffn_w1"])
    w2 = np.asarray(inp["L0_ffn_w2"])
    b1 = np.asarray(inp["L0_ffn_b1"])
    b2 = np.asarray(inp["L0_ffn_b2"])
    b1r = np.ascontiguousarray(b1.reshape(8, 32, 128).transpose(0, 2, 1)).reshape(8 * 128, 32)
    cnt8 = np.concatenate([np.asarray(r["o_cnt"]) for r in a_results], 0)
    base = {"w1c": np.asarray(a_results[0]["w1c"]), "w2c": np.asarray(a_results[0]["w2c"]),
            "b1r": b1r, "b2d": b2, "cnt8": cnt8}
    maps = []
    for c in range(8):
        m = dict(base)
        m["rmsk"] = (np.arange(8) < c).astype(np.float32).reshape(8, 1)
        m["mhot"] = (np.arange(8) == c).astype(np.float32).reshape(8, 1)
        m["x2d"] = np.asarray(a_results[c]["o_x2"])
        m["ln2d"] = np.asarray(a_results[c]["o_ln2"])
        m["rnkd"] = np.asarray(a_results[c]["o_rank"])
        m["eixd"] = np.asarray(a_results[c]["o_eidx"])
        m["gvd"] = np.asarray(a_results[c]["o_gv"])
        maps.append(m)
    return maps

def kernel(**inputs):
    PA = build_A()
    ncA = emit_A(PA)
    mapsA = host_prep_A(inputs)
    resA = run_bass_kernel_spmd(ncA, mapsA, core_ids=list(range(8)))
    a_results = [{k: np.asarray(v) for k, v in resA.results[c].items()} for c in range(8)]

    PB = build_B()
    ncB = emit_B(PB)
    mapsB = host_prep_B(inputs, a_results)
    resB = run_bass_kernel_spmd(ncB, mapsB, core_ids=list(range(8)))
    y = np.stack([np.asarray(resB.results[c]["y"], dtype=np.float32) for c in range(8)])
    return y
